# version 07
# baseline (speedup 1.0000x reference)
import contextlib
import numpy as np
import ml_dtypes
import concourse.bass as bass
import concourse.mybir as mybir
from concourse.bass_utils import run_bass_kernel_spmd

F32 = mybir.dt.float32
BF16 = mybir.dt.bfloat16
I32 = mybir.dt.int32
U8 = mybir.dt.uint8
AF = mybir.ActivationFunctionType
ALU = mybir.AluOpType
AX = mybir.AxisListType
BF = ml_dtypes.bfloat16

NCORES = 8
T_OWN = 1024
HALO = 3
T_EXT = T_OWN + HALO
DM = 4096
DH = 1024
NH = 128
HD = 64
NST = 128
G = 8
CS = 256
NCH = T_OWN // CS
DIN = NH * HD
CONV_DIM = DIN + 2 * G * NST
PROJ = DIN + CONV_DIM + NH
EPS = 1e-5
GSZ = DIN // G

ENGINES = ("tensor", "vector", "scalar", "gpsimd", "sync")
DMA_QUEUES = ("sync", "scalar", "gpsimd", "vector")


class Op:
    __slots__ = ("eng", "fn", "reads", "writes", "dma", "idx", "deps",
                 "need_inc", "inc_val", "tag", "lane")

    def __init__(self, eng, fn, reads, writes, dma, idx, tag):
        self.eng, self.fn = eng, fn
        self.reads, self.writes = reads, writes
        self.dma, self.idx, self.tag = dma, idx, tag
        self.deps = []
        self.need_inc = False
        self.inc_val = None
        self.lane = None


class Prog:

    def __init__(self):
        self.ops = []
        self.last_write = {}
        self.readers = {}

    def barrier(self):
        if not hasattr(self, "last_eng"):
            self.last_eng = {e: None for e in ENGINES}
        if not hasattr(self, "last_dma"):
            self.last_dma = {}
        last = dict(self.last_eng)
        for k, d in self.last_dma.items():
            if d is not None:
                last[("dma",) + k] = d
        nops = []
        for e in ENGINES:
            op = Op(e, lambda eng: eng.nop(), (), (), False, len(self.ops), f"bar_{e}")
            for d in last.values():
                if d is not None:
                    op.deps.append(d)
            self.ops.append(op)
            self.last_eng[e] = op
            nops.append(op)
        self.barrier_ops = {op.eng: op for op in nops}

    def add(self, eng, fn, reads=(), writes=(), dma=False, tag=""):
        op = Op(eng, fn, tuple(reads), tuple(writes), dma, len(self.ops), tag)
        if not hasattr(self, "last_eng"):
            self.last_eng = {e: None for e in ENGINES}
        if not hasattr(self, "last_dma"):
            self.last_dma = {}
        if not hasattr(self, "barrier_ops"):
            self.barrier_ops = {}
        if not hasattr(self, "qcnt"):
            self.qcnt = {e: 0 for e in ENGINES}
            self.lane_cnt = {}
        if dma:
            i = self.qcnt[eng]
            self.qcnt[eng] += 1
            op.lane = i % self.NLANES
            seq = self.lane_cnt.get((eng, op.lane), 0) + 1
            self.lane_cnt[(eng, op.lane)] = seq
            op.inc_val = 16 * seq
            op.need_inc = True
        deps = {}
        b = self.barrier_ops.get(eng)
        if b is not None:
            deps[b.idx] = b
        for k in op.reads:
            w = self.last_write.get(k)
            if w is not None:
                deps[w.idx] = w
        for k in op.writes:
            w = self.last_write.get(k)
            if w is not None:
                deps[w.idx] = w
            for r in self.readers.get(k, ()):
                deps[r.idx] = r
        for d in deps.values():
            if d.eng == "tensor" and op.eng == "tensor" and not d.dma and not op.dma:
                continue
            op.deps.append(d)
        for k in op.reads:
            self.readers.setdefault(k, []).append(op)
        for k in op.writes:
            self.last_write[k] = op
            self.readers[k] = []
        self.last_eng[eng] = op
        if dma:
            self.last_dma[(eng, op.lane)] = op
        self.ops.append(op)
        return op

    NLANES = 8

    def emit(self, nc):
        for op in self.ops:
            for d in op.deps:
                d.need_inc = True
        csem_count = {e: 0 for e in ENGINES}
        lane_cnt = getattr(self, "lane_cnt", {})
        for op in self.ops:
            if op.dma:
                pass
            elif op.need_inc:
                csem_count[op.eng] += 1
                op.inc_val = csem_count[op.eng]
        with contextlib.ExitStack() as stack:
            csem = {e: stack.enter_context(nc.semaphore(f"c_{e}")) for e in ENGINES}
            qsem = {(e, l): stack.enter_context(nc.semaphore(f"q_{e}{l}"))
                    for e in DMA_QUEUES for l in range(self.NLANES)}
            block = stack.enter_context(nc.Block())
            per_eng = {e: [] for e in ENGINES}
            for op in self.ops:
                per_eng[op.eng].append(op)

            def emit_engine(eng_name, eng):
                watermark = {}
                for op in per_eng[eng_name]:
                    needed = {}
                    for d in op.deps:
                        wk = (d.eng, d.lane if d.dma else None)
                        needed[wk] = max(needed.get(wk, 0), d.inc_val)
                    if op.dma and op.inc_val > 16:
                        wk = (op.eng, op.lane)
                        needed[wk] = max(needed.get(wk, 0), op.inc_val - 16)
                    for (deng, dlane), val in needed.items():
                        if watermark.get((deng, dlane), -1) >= val:
                            continue
                        sem = csem[deng] if dlane is None else qsem[(deng, dlane)]
                        eng.wait_ge(sem, val)
                        watermark[(deng, dlane)] = val
                    inst = op.fn(eng)
                    if op.need_inc:
                        if inst is None:
                            raise RuntimeError(f"op {op.tag}: fn returned None")
                        sem = qsem[(op.eng, op.lane)] if op.dma else csem[op.eng]
                        inst.then_inc(sem, 16 if op.dma else 1)
                if eng_name in DMA_QUEUES:
                    for l in range(self.NLANES):
                        tot = lane_cnt.get((eng_name, l), 0)
                        if tot and watermark.get((eng_name, l), -1) < 16 * tot:
                            eng.wait_ge(qsem[(eng_name, l)], 16 * tot)

            @block.tensor
            def _(te):
                emit_engine("tensor", te)

            @block.vector
            def _(v):
                emit_engine("vector", v)

            @block.scalar
            def _(sc):
                emit_engine("scalar", sc)

            @block.gpsimd
            def _(g):
                emit_engine("gpsimd", g)

            @block.sync
            def _(sy):
                emit_engine("sync", sy)


class Ctx:
    def __init__(self, nc, prog, stack):
        self.nc, self.p, self.stack = nc, prog, stack
        self._stacks = [stack]

    def sbuf(self, name, shape, dtype):
        return self._stacks[-1].enter_context(self.nc.sbuf_tensor(name, shape, dtype))

    def psum(self, name, shape, dtype):
        return self._stacks[-1].enter_context(self.nc.psum_tensor(name, shape, dtype))

    @contextlib.contextmanager
    def scope(self):
        st = contextlib.ExitStack()
        self._stacks.append(st)
        try:
            yield
        finally:
            self._stacks.pop()
            st.close()
            self.p.barrier()




def emit_ln_stream(ctx, src_rows_ap, dst_dram, dst_key, n_rows, tag,
                   also_keep=None, src_keys=None):
    p = ctx.p
    nt = (n_rows + 127) // 128
    xt = [ctx.sbuf(f"{tag}xt{j}", [128, DM], F32) for j in range(2)]
    sq = ctx.sbuf(f"{tag}sq", [128, DM], F32)
    cc = ctx.sbuf(f"{tag}cc", [128, 8], F32)
    xnb = [ctx.sbuf(f"{tag}xnb{j}", [128, DM], BF16) for j in range(2)]
    for i in range(nt):
        j = i % 2
        r = min(128, n_rows - i * 128)
        xk = (f"{tag}xt", j)
        rdeps = [] if src_keys is None else src_keys
        p.add("sync", lambda sy, i=i, j=j, r=r: sy.dma_start(
            out=xt[j][0:r, :], in_=src_rows_ap[i * 128:i * 128 + r, :]),
            reads=rdeps, writes=[xk], dma=True, tag=f"{tag}ld{i}")
        emit_ln_tile(ctx, xt[j], xk, r, xnb[j], (f"{tag}xnb", j), sq,
                     (f"{tag}sq",), cc, (f"{tag}cc",), f"{tag}{i}")
        p.add("gpsimd", lambda g, i=i, j=j, r=r: g.dma_start(
            out=dst_dram[i * 128:i * 128 + r, :], in_=xnb[j][0:r, :]),
            reads=[(f"{tag}xnb", j)], writes=[(dst_key, i)], dma=True,
            tag=f"{tag}st{i}")
        if also_keep is not None:
            also_keep(i, r, xt[j], xk)


def emit_ln_tile(ctx, xap, xkey, r, out_ap, out_key, sq, sqk, cc, cck, tag):
    p = ctx.p
    p.add("vector", lambda v: v.tensor_reduce(
        cc[0:r, 0:1], xap[0:r, :], AX.X, ALU.add),
        reads=[xkey], writes=[cck], tag=f"{tag}_mean")
    p.add("scalar", lambda sc: sc.activation(
        sq[0:r, :], xap[0:r, :], AF.Square, accum_out=cc[0:r, 1:2]),
        reads=[xkey], writes=[sqk, cck], tag=f"{tag}_sq")
    p.add("vector", lambda v: v.tensor_scalar(
        cc[0:r, 2:3], cc[0:r, 0:1], 1.0 / DM, 0.0, ALU.mult, ALU.add),
        reads=[cck], writes=[cck], tag=f"{tag}_m")
    p.add("vector", lambda v: v.tensor_tensor(
        cc[0:r, 3:4], cc[0:r, 2:3], cc[0:r, 2:3], ALU.mult),
        reads=[cck], writes=[cck], tag=f"{tag}_m2")
    p.add("vector", lambda v: v.tensor_scalar(
        cc[0:r, 4:5], cc[0:r, 1:2], 1.0 / DM, EPS, ALU.mult, ALU.add),
        reads=[cck], writes=[cck], tag=f"{tag}_ex2")
    p.add("vector", lambda v: v.tensor_tensor(
        cc[0:r, 5:6], cc[0:r, 4:5], cc[0:r, 3:4], ALU.subtract),
        reads=[cck], writes=[cck], tag=f"{tag}_var")
    p.add("scalar", lambda sc: sc.activation(
        cc[0:r, 6:7], cc[0:r, 5:6], AF.Sqrt),
        reads=[cck], writes=[cck], tag=f"{tag}_sqrt")
    p.add("vector", lambda v: v.reciprocal(
        cc[0:r, 7:8], cc[0:r, 6:7]),
        reads=[cck], writes=[cck], tag=f"{tag}_rstd")
    p.add("vector", lambda v: v.tensor_tensor(
        cc[0:r, 2:3], cc[0:r, 2:3], cc[0:r, 7:8], ALU.mult),
        reads=[cck], writes=[cck], tag=f"{tag}_mr")
    p.add("vector", lambda v: v.tensor_scalar(
        cc[0:r, 2:3], cc[0:r, 2:3], -1.0, 0.0, ALU.mult, ALU.add),
        reads=[cck], writes=[cck], tag=f"{tag}_neg")
    p.add("vector", lambda v: v.tensor_scalar(
        out_ap[0:r, :], xap[0:r, :], cc[0:r, 7:8], cc[0:r, 2:3],
        ALU.mult, ALU.add),
        reads=[xkey, cck], writes=[out_key], tag=f"{tag}_xn")




def build_L1():
    nc = bass.Bass("TRN2", target_bir_lowering=False, debug=False)
    p = Prog()

    x0 = nc.dram_tensor("x0", [T_OWN, DM], F32, kind="ExternalInput").ap()
    wqkv = nc.dram_tensor("wqkv", [DM, 3 * DH], F32, kind="ExternalInput").ap()
    bqkv = nc.dram_tensor("bqkv", [1, 3 * DH], F32, kind="ExternalInput").ap()
    q3 = nc.dram_tensor("q3", [512 * 128, 12], U8, kind="ExternalInput").ap()
    wnorm = nc.dram_tensor("wnorm", [512 * 128, 1], F32, kind="ExternalInput").ap()
    inproj_f32 = nc.dram_tensor("inproj_f32", [PROJ // 8, DM], F32, kind="ExternalInput").ap()
    outproj_f32 = nc.dram_tensor("outproj_f32", [DM // 8, DIN], F32, kind="ExternalInput").ap()

    q_out = nc.dram_tensor("q_out", [DH, T_OWN], BF16, kind="ExternalOutput").ap()
    k_out = nc.dram_tensor("k_out", [DH, T_OWN], BF16, kind="ExternalOutput").ap()
    v_out = nc.dram_tensor("v_out", [T_OWN, DH], BF16, kind="ExternalOutput").ap()
    wffnt_out = nc.dram_tensor("wffnt_out", [DM, 512], BF16, kind="ExternalOutput").ap()
    ipt_out = nc.dram_tensor("ipt_out", [DM, PROJ // 8], BF16, kind="ExternalOutput").ap()
    opt_out = nc.dram_tensor("opt_out", [DIN, DM // 8], BF16, kind="ExternalOutput").ap()
    id_bf = nc.dram_tensor("id_bf", [128, 128], BF16, kind="ExternalInput").ap()

    xn1 = nc.dram_tensor("xn1", [T_OWN, DM], BF16).ap()
    wqkv_bf = nc.dram_tensor("wqkv_bf", [DM, 3 * DH], BF16).ap()

    with contextlib.ExitStack() as stack:
        ctx = Ctx(nc, p, stack)

        def cast_chunks(dst, src, rows, tag, nchunks=4):
            step = (rows + nchunks - 1) // nchunks
            for i in range(0, rows, step):
                r = min(step, rows - i)
                p.add("gpsimd", lambda g, i=i, r=r: g.dma_start(
                    out=dst[i:i + r, :], in_=src[i:i + r, :]),
                    reads=[], writes=[(tag, i)], dma=True, tag=f"{tag}{i}")

        cast_chunks(wqkv_bf, wqkv, DM, "cast_qkv", 4)
        wqkv_cast_keys = [("cast_qkv", i) for i in range(0, DM, DM // 4)]

        with ctx.scope():
            emit_ln_stream(ctx, x0, xn1, "xn1", T_OWN, "ln1")
        xn1_keys = [("xn1", i) for i in range(8)]

        xn1fm = ctx.sbuf("xn1fm", [128, 32 * T_OWN], BF16)
        for cc_i in range(32):
            p.add("sync", lambda sc, cc_i=cc_i: sc.dma_start(
                out=xn1fm[:, cc_i * T_OWN:(cc_i + 1) * T_OWN],
                in_=xn1[:, cc_i * 128:(cc_i + 1) * 128], transpose=True),
                reads=xn1_keys, writes=[("xn1fm", cc_i)], dma=True,
                tag=f"trl_xn1_{cc_i}")
        xn1fm_keys = [("xn1fm", i) for i in range(32)]

        bqb = ctx.sbuf("bqb", [1, 3 * DH], BF16)
        ones = ctx.sbuf("ones", [1, 512], BF16)
        p.add("gpsimd", lambda g: g.dma_start(out=bqb[:, :], in_=bqkv[:, :]),
              reads=[], writes=["bqb"], dma=True, tag="ld_bqb")
        p.add("vector", lambda v: v.memset(ones[:, :], 1.0),
              reads=[], writes=["ones"], tag="ones")

        wtile = [ctx.sbuf(f"wt{j}", [128, 768], BF16) for j in range(4)]
        psb = [ctx.psum(f"ps{j}", [128, 512], F32) for j in range(7)]
        bkt = ctx.psum("bankT", [128, 1024], BF16)
        evb = [ctx.sbuf(f"ev{j}", [128, 512], BF16) for j in range(2)]
        identb = ctx.sbuf("identb", [128, 128], BF16)
        p.add("sync", lambda sy: sy.dma_start(out=identb[:, :], in_=id_bf[:, :]),
              writes=["identb"], dma=True, tag="ld_idb")
        trev = [ctx.sbuf(f"trev{j}", [128, 128], BF16) for j in range(4)]
        f_groups = [list(range(0, 6)), list(range(6, 12)), list(range(12, 16))]
        for th in range(2):
            tsl = slice(th * 512, (th + 1) * 512)
            for gi, fg in enumerate(f_groups):
                f0 = fg[0] * 128
                fw = len(fg) * 128
                for cc_i in range(32):
                    j = cc_i % 4
                    wk = (f"wt", j)
                    p.add("sync", lambda sy, cc_i=cc_i, j=j, f0=f0, fw=fw: sy.dma_start(
                        out=wtile[j][:, 0:fw],
                        in_=wqkv_bf[cc_i * 128:(cc_i + 1) * 128, f0:f0 + fw]),
                        reads=wqkv_cast_keys, writes=[wk], dma=True,
                        tag=f"qk_w{th}_{gi}_{cc_i}")
                    for fi, f in enumerate(fg):
                        bank = fi
                        p.add("tensor", lambda te, cc_i=cc_i, j=j, fi=fi, bank=bank, tsl=tsl: te.matmul(
                            psb[bank][:, :],
                            wtile[j][:, fi * 128:(fi + 1) * 128],
                            xn1fm[:, cc_i * T_OWN:cc_i * T_OWN + T_OWN][:, tsl],
                            start=(cc_i == 0), stop=False),
                            reads=[wk, ("xn1fm", cc_i)], writes=[("bank", bank)],
                            tag=f"qk_mm{th}_{gi}_{cc_i}_{fi}")
                for fi, f in enumerate(fg):
                    bank = fi
                    p.add("tensor", lambda te, f=f, bank=bank: te.matmul(
                        psb[bank][:, :], bqb[:, f * 128:(f + 1) * 128],
                        ones[:, :], start=False, stop=True),
                        reads=["bqb", "ones"], writes=[("bank", bank)],
                        tag=f"qk_bias{th}_{gi}_{fi}")
                for fi, f in enumerate(fg):
                    bank = fi
                    j = fi % 2
                    p.add("scalar", lambda sc, bank=bank, j=j: sc.activation(
                        evb[j][:, :], psb[bank][:, :], AF.Copy),
                        reads=[("bank", bank)], writes=[("ev", j)],
                        tag=f"qk_ev{th}_{gi}_{fi}")
                    dst = q_out if f < 8 else k_out
                    fr = f * 128 if f < 8 else (f - 8) * 128
                    p.add("gpsimd", lambda g, dst=dst, fr=fr, j=j, th=th: g.dma_start(
                        out=dst[fr:fr + 128, th * 512:(th + 1) * 512],
                        in_=evb[j][:, :]),
                        reads=[("ev", j)], writes=[("qkout", f, th)], dma=True,
                        tag=f"qk_st{th}_{gi}_{fi}")

        wvt = [ctx.sbuf(f"wvt{j}", [128, 1024], BF16) for j in range(4)]
        for tp in range(4):
            for cc_i in range(32):
                j = cc_i % 4
                wk = (f"wvt", j)
                p.add("sync", lambda sy, cc_i=cc_i, j=j: sy.dma_start(
                    out=wvt[j][:, :],
                    in_=wqkv_bf[cc_i * 128:(cc_i + 1) * 128, 2048:3072]),
                    reads=wqkv_cast_keys, writes=[wk], dma=True,
                    tag=f"v_w{tp}_{cc_i}")
                for ti in range(2):
                    t = tp * 2 + ti
                    for fh in range(2):
                        bank = ti * 2 + fh
                        p.add("tensor", lambda te, cc_i=cc_i, j=j, t=t, fh=fh, bank=bank: te.matmul(
                            psb[bank][:, :],
                            xn1fm[:, cc_i * T_OWN + t * 128: cc_i * T_OWN + (t + 1) * 128],
                            wvt[j][:, fh * 512:(fh + 1) * 512],
                            start=(cc_i == 0), stop=False),
                            reads=[wk, ("xn1fm", cc_i)], writes=[("bank", bank)],
                            tag=f"v_mm{t}_{fh}_{cc_i}")
            for ti in range(2):
                t = tp * 2 + ti
                for fh in range(2):
                    bank = ti * 2 + fh
                    p.add("tensor", lambda te, fh=fh, bank=bank, t=t: te.matmul(
                        psb[bank][:, :], ones[:, 0:128],
                        bqb[:, 2048 + fh * 512:2048 + (fh + 1) * 512],
                        start=False, stop=True),
                        reads=["bqb", "ones"], writes=[("bank", bank)],
                        tag=f"v_bias{t}_{fh}")
                    j = fh
                    p.add("scalar", lambda sc, bank=bank, j=j: sc.activation(
                        evb[j][:, :], psb[bank][:, :], AF.Copy),
                        reads=[("bank", bank)], writes=[("ev", j)],
                        tag=f"v_ev{t}_{fh}")
                    p.add("gpsimd", lambda g, t=t, fh=fh, j=j: g.dma_start(
                        out=v_out[t * 128:(t + 1) * 128, fh * 512:(fh + 1) * 512],
                        in_=evb[j][:, :]),
                        reads=[("ev", j)], writes=[("vout", t, fh)], dma=True,
                        tag=f"v_st{t}_{fh}")

        with ctx.scope():
            q3v = q3.rearrange("(o g) b -> o (g b)", g=128)
            wnv = wnorm.rearrange("(o g) c -> o (g c)", g=128)
            dqb = ctx.sbuf("dqb", [128, 1536], U8)
            di = [ctx.sbuf(f"di{j}", [128, 512], I32) for j in range(3)]
            dn = ctx.sbuf("dn", [128, 512], I32)
            dq = ctx.sbuf("dq", [128, DM], I32)
            df = ctx.sbuf("df", [128, DM], F32)
            dnrm = ctx.sbuf("dnrm", [128, 128], F32)
            dout = ctx.sbuf("dout", [128, DM], BF16)
            for blk in range(4):
                p.add("sync", lambda sy, blk=blk: sy.dma_start(
                    out=dqb[:, :], in_=q3v[blk * 128:(blk + 1) * 128, :]),
                    reads=[("dq_done", blk - 1)] if blk else [],
                    writes=["dqb"], dma=True, tag=f"dq_ld{blk}")
                p.add("sync", lambda sy, blk=blk: sy.dma_start(
                    out=dnrm[:, :], in_=wnv[blk * 128:(blk + 1) * 128, :]),
                    reads=[], writes=["dnrm"], dma=True, tag=f"dq_ldn{blk}")
                bv = dqb[:, :].rearrange("p (t k) -> p t k", k=3)
                for j in range(3):
                    p.add("vector", lambda v, j=j, bv=bv: v.tensor_copy(
                        di[j][:, :], bv[:, :, j]),
                        reads=["dqb"], writes=[("di", j)], tag=f"dq_b{blk}_{j}")
                p.add("vector", lambda v: v.tensor_scalar(
                    dn[:, :], di[1][:, :], 8, 0, ALU.logical_shift_left, ALU.bitwise_or),
                    reads=[("di", 1)], writes=["dn"], tag=f"dq_s1_{blk}")
                p.add("vector", lambda v: v.tensor_tensor(
                    dn[:, :], dn[:, :], di[0][:, :], ALU.bitwise_or),
                    reads=["dn", ("di", 0)], writes=["dn"], tag=f"dq_o1_{blk}")
                p.add("vector", lambda v: v.tensor_scalar(
                    di[1][:, :], di[2][:, :], 16, 0, ALU.logical_shift_left, ALU.bitwise_or),
                    reads=[("di", 2)], writes=[("di", 1)], tag=f"dq_s2_{blk}")
                p.add("vector", lambda v: v.tensor_tensor(
                    dn[:, :], dn[:, :], di[1][:, :], ALU.bitwise_or),
                    reads=["dn", ("di", 1)], writes=["dn"], tag=f"dq_o2_{blk}")
                dqv = dq[:, :].rearrange("p (t k) -> p t k", k=8)
                for pos in range(8):
                    p.add("vector", lambda v, pos=pos, dqv=dqv: v.tensor_scalar(
                        dqv[:, :, pos], dn[:, :], 3 * pos, 7,
                        ALU.logical_shift_right, ALU.bitwise_and),
                        reads=["dn"], writes=["dq"], tag=f"dq_d{blk}_{pos}")
                p.add("vector", lambda v: v.tensor_scalar(
                    dq[:, :], dq[:, :], 2, -7, ALU.mult, ALU.add),
                    reads=["dq"], writes=["dq"], tag=f"dq_aff{blk}")
                p.add("vector", lambda v: v.tensor_copy(df[:, :], dq[:, :]),
                    reads=["dq"], writes=["df"], tag=f"dq_f{blk}")
                nview = dnrm[:, :].unsqueeze(-1).broadcast_to([128, 128, 32])
                dfv = df[:, :].rearrange("p (a b) -> p a b", a=128)
                p.add("vector", lambda v, nview=nview, dfv=dfv: v.scalar_tensor_tensor(
                    dout[:, :].rearrange("p (a b) -> p a b", a=128),
                    dfv, 1.0 / 7.0, nview, ALU.mult, ALU.mult),
                    reads=["df", "dnrm"], writes=["dout"], tag=f"dq_w{blk}")
                for cb in range(32):
                    tj = cb % 4
                    p.add("tensor", lambda te, cb=cb: te.transpose(
                        bkt[:, 0:128], dout[:, cb * 128:(cb + 1) * 128], identb[:, :]),
                        reads=["dout", "identb"], writes=["bkt"], tag=f"dq_t{blk}_{cb}")
                    p.add("vector", lambda v, tj=tj: v.tensor_copy(
                        trev[tj][:, :], bkt[:, 0:128]),
                        reads=["bkt"], writes=[("trev", tj)], tag=f"dq_tc{blk}_{cb}")
                    p.add("gpsimd", lambda g, blk=blk, cb=cb, tj=tj: g.dma_start(
                        out=wffnt_out[cb * 128:(cb + 1) * 128, blk * 128:(blk + 1) * 128],
                        in_=trev[tj][:, :]),
                        reads=[("trev", tj)], writes=[("dq_done", blk)], dma=True,
                        tag=f"dq_st{blk}_{cb}")

            wtc = [ctx.sbuf(f"wtc{j}", [128, DIN], BF16) for j in range(2)]
            NIP = PROJ // 8
            for ft in range((NIP + 127) // 128):
                r = min(128, NIP - ft * 128)
                j = ft % 2
                p.add("gpsimd", lambda g, ft=ft, r=r, j=j: g.dma_start(
                    out=wtc[j][0:r, 0:DM], in_=inproj_f32[ft * 128:ft * 128 + r, :]),
                    writes=[("wtc", j)], dma=True, tag=f"ipl{ft}")
                for cb in range(32):
                    tj = cb % 4
                    p.add("tensor", lambda te, r=r, j=j, cb=cb: te.transpose(
                        bkt[:, 0:r], wtc[j][0:r, cb * 128:(cb + 1) * 128],
                        identb[0:r, 0:r]),
                        reads=[("wtc", j), "identb"], writes=["bkt"], tag=f"ipt{ft}_{cb}")
                    p.add("vector", lambda v, tj=tj, r=r: v.tensor_copy(
                        trev[tj][:, 0:r], bkt[:, 0:r]),
                        reads=["bkt"], writes=[("trev", tj)], tag=f"iptc{ft}_{cb}")
                    p.add("scalar", lambda v, ft=ft, cb=cb, tj=tj, r=r: v.dma_start(
                        out=ipt_out[cb * 128:(cb + 1) * 128, ft * 128:ft * 128 + r],
                        in_=trev[tj][:, 0:r]),
                        reads=[("trev", tj)], writes=[("ipt_o", ft, cb)], dma=True,
                        tag=f"ips{ft}_{cb}")
            for ft in range(4):
                j = ft % 2
                p.add("gpsimd", lambda g, ft=ft, j=j: g.dma_start(
                    out=wtc[j][:, :], in_=outproj_f32[ft * 128:(ft + 1) * 128, :]),
                    writes=[("wtc", j)], dma=True, tag=f"opl{ft}")
                for cb in range(64):
                    tj = cb % 4
                    p.add("tensor", lambda te, j=j, cb=cb: te.transpose(
                        bkt[:, 0:128], wtc[j][:, cb * 128:(cb + 1) * 128], identb[:, :]),
                        reads=[("wtc", j), "identb"], writes=["bkt"], tag=f"opt{ft}_{cb}")
                    p.add("vector", lambda v, tj=tj: v.tensor_copy(
                        trev[tj][:, :], bkt[:, 0:128]),
                        reads=["bkt"], writes=[("trev", tj)], tag=f"optc{ft}_{cb}")
                    p.add("scalar", lambda v, ft=ft, cb=cb, tj=tj: v.dma_start(
                        out=opt_out[cb * 128:(cb + 1) * 128, ft * 128:(ft + 1) * 128],
                        in_=trev[tj][:, :]),
                        reads=[("trev", tj)], writes=[("opt_o", ft, cb)], dma=True,
                        tag=f"ops{ft}_{cb}")

        p.emit(nc)
    return nc



T_PAD = 1152
NQT = 9
QT_ROWS = [128] * 8 + [3]


def build_L2():
    nc = bass.Bass("TRN2", target_bir_lowering=False, debug=False)
    p = Prog()

    x0e = nc.dram_tensor("x0e", [T_EXT, DM], F32, kind="ExternalInput").ap()
    q_ext = nc.dram_tensor("q_ext", [DH, T_EXT], BF16, kind="ExternalInput").ap()
    k_fm = nc.dram_tensor("k_fm", [DH, 8192], BF16, kind="ExternalInput").ap()
    v_tm = nc.dram_tensor("v_tm", [8192, DH], BF16, kind="ExternalInput").ap()
    wo = nc.dram_tensor("wo", [DH, DM], F32, kind="ExternalInput").ap()
    bo = nc.dram_tensor("bo", [1, DM], F32, kind="ExternalInput").ap()
    wffn = nc.dram_tensor("wffn", [DM, DM], BF16, kind="ExternalInput").ap()
    fbias = nc.dram_tensor("fbias", [1, DM], F32, kind="ExternalInput").ap()
    ipbf = nc.dram_tensor("ipbf", [DM, PROJ], BF16, kind="ExternalInput").ap()
    convw_r = nc.dram_tensor("convw_r", [128, 320], F32, kind="ExternalInput").ap()
    convb_r = nc.dram_tensor("convb_r", [128, 80], F32, kind="ExternalInput").ap()
    dtb_col = nc.dram_tensor("dtb_col", [128, 1], F32, kind="ExternalInput").ap()
    alog_col = nc.dram_tensor("alog_col", [128, 1], F32, kind="ExternalInput").ap()
    d_exp = nc.dram_tensor("d_exp", [128, DIN], F32, kind="ExternalInput").ap()
    halo_m = nc.dram_tensor("halo_m", [128, 4], F32, kind="ExternalInput").ap()
    tri0 = nc.dram_tensor("tri0", [128, 256], F32, kind="ExternalInput").ap()
    tri1 = nc.dram_tensor("tri1", [128, 256], F32, kind="ExternalInput").ap()
    id_bf = nc.dram_tensor("id_bf", [128, 128], BF16, kind="ExternalInput").ap()
    id_f = nc.dram_tensor("id_f", [128, 128], F32, kind="ExternalInput").ap()

    x2_out = nc.dram_tensor("x2_out", [T_EXT, DM], F32, kind="ExternalOutput").ap()
    y_part = nc.dram_tensor("y_part", [T_OWN, DIN], F32, kind="ExternalOutput").ap()
    gate_out = nc.dram_tensor("gate_out", [T_OWN, DIN], F32, kind="ExternalOutput").ap()
    c_out = nc.dram_tensor("c_out", [G * NST, T_OWN], BF16, kind="ExternalOutput").ap()
    acs_out = nc.dram_tensor("acs_out", [T_OWN, NH], F32, kind="ExternalOutput").ap()
    t_out = nc.dram_tensor("t_out", [NH, NCH], F32, kind="ExternalOutput").ap()
    s_out = nc.dram_tensor("s_out", [NST, NH * HD], F32, kind="ExternalOutput").ap()
    sloc_out = nc.dram_tensor("sloc_out", [(NCH - 1) * NST, NH * HD], F32, kind="ExternalOutput").ap()

    x1d = nc.dram_tensor("x1d", [T_EXT, DM], F32, kind="ExternalOutput").ap()
    xn2d = nc.dram_tensor("xn2d", [T_PAD, DM], BF16).ap()
    xn3d = nc.dram_tensor("xn3d", [T_PAD, DM], BF16).ap()
    xsil_d = nc.dram_tensor("xsil_d", [DIN, T_OWN], BF16).ap()
    acsht_d = nc.dram_tensor("acsht_d", [NH, T_OWN], F32).ap()

    with contextlib.ExitStack() as stack:
        ctx = Ctx(nc, p, stack)
        bank = [ctx.psum(f"bank{i}", [128, 512], F32) for i in range(7)]
        bkt = ctx.psum("bankT", [128, 1024], BF16)
        BK = [("bank", i) for i in range(7)]
        BKT = ("bankT",)

        ones_b = ctx.sbuf("ones_b", [1, 512], BF16)
        ones_f = ctx.sbuf("ones_f", [1, 128], F32)
        identb = ctx.sbuf("identb", [128, 128], BF16)
        identf = ctx.sbuf("identf", [128, 128], F32)
        ev_f = [ctx.sbuf(f"ev_f{j}", [128, 512], F32) for j in range(4)]
        p.add("vector", lambda v: v.memset(ones_b[:, :], 1.0), writes=["ones_b"], tag="onesb")
        p.add("vector", lambda v: v.memset(ones_f[:, :], 1.0), writes=["ones_f"], tag="onesf")
        p.add("sync", lambda sy: sy.dma_start(out=identb[:, :], in_=id_bf[:, :]),
              writes=["identb"], dma=True, tag="ld_idb")
        p.add("sync", lambda sy: sy.dma_start(out=identf[:, :], in_=id_f[:, :]),
              writes=["identf"], dma=True, tag="ld_idf")

        with ctx.scope():
            zb = ctx.sbuf("zb", [128, DM], BF16)
            p.add("vector", lambda v: v.memset(zb[:, :], 0.0), writes=["zb"], tag="zb")
            for nm, d in (("xn2d", xn2d), ("xn3d", xn3d)):
                p.add("gpsimd", lambda g, d=d: g.dma_start(
                    out=d[1024:1152, :], in_=zb[:, :]),
                    reads=["zb"], writes=[(nm, 8)], dma=True, tag=f"zpad_{nm}")

        ao_keys = [("ao_fm", dc, qt) for dc in range(8) for qt in range(NQT)]
        with ctx.scope():
            ao_fm = ctx.sbuf("ao_fm", [128, 8 * T_EXT], BF16)
            with ctx.scope():
                q_sb = ctx.sbuf("q_sb", [128, 8 * T_EXT], BF16)
                for dc in range(8):
                    p.add("sync", lambda sy, dc=dc: sy.dma_start(
                        out=q_sb[:, dc * T_EXT:(dc + 1) * T_EXT],
                        in_=q_ext[dc * 128:(dc + 1) * 128, :]),
                        writes=[("q_sb", dc)], dma=True, tag=f"ldq{dc}")
                KQ = 1024
                k_sb = [ctx.sbuf(f"k_sb{j}", [128, 8 * KQ], BF16) for j in range(2)]
                v_sb = [ctx.sbuf(f"v_sb{j}", [128, (KQ // 128) * DH], BF16) for j in range(2)]
                S_f = [ctx.sbuf(f"S_f{q}", [128, KQ], F32) for q in range(2)]
                P_b = [ctx.sbuf(f"P_b{q}", [128, KQ], BF16) for q in range(2)]
                PT_b = [ctx.sbuf(f"PT_b{q}", [128, 16 * 128], BF16) for q in range(2)]
                oacc = [ctx.sbuf(f"oacc{t}", [128, DH], F32) for t in range(NQT)]
                mrun = ctx.sbuf("mrun", [128, NQT], F32)
                srun = ctx.sbuf("srun", [128, NQT], F32)
                tcolr = [ctx.sbuf(f"tcol{q}", [128, 6], F32) for q in range(2)]
                ao_tm = [ctx.sbuf(f"ao_tm{q}", [128, DH], BF16) for q in range(2)]
                SCALE = 1.0 / 32.0
                NKQ = 8192 // KQ
                NKC = KQ // 128
                for kq in range(NKQ):
                    kp = kq % 2
                    for dc in range(8):
                        p.add("sync", lambda sy, dc=dc, kq=kq, kp=kp: sy.dma_start(
                            out=k_sb[kp][:, dc * KQ:(dc + 1) * KQ],
                            in_=k_fm[dc * 128:(dc + 1) * 128, kq * KQ:(kq + 1) * KQ]),
                            writes=[("k_sb", kp, dc)], dma=True, tag=f"ldk{kq}_{dc}")
                    for kc in range(NKC):
                        p.add("scalar", lambda sc, kc=kc, kq=kq, kp=kp: sc.dma_start(
                            out=v_sb[kp][:, kc * DH:(kc + 1) * DH],
                            in_=v_tm[kq * KQ + kc * 128: kq * KQ + (kc + 1) * 128, :]),
                            writes=[("v_sb", kp, kc)], dma=True, tag=f"ldv{kq}_{kc}")
                    for qt in range(NQT):
                        r = QT_ROWS[qt]
                        q0 = qt * 128
                        qp = qt % 2
                        tcol = tcolr[qp]
                        for sb4 in range(KQ // 512):
                            bk = sb4 % 4
                            for dc in range(8):
                                p.add("tensor", lambda te, dc=dc, sb4=sb4, bk=bk, q0=q0, r=r, kp=kp: te.matmul(
                                    bank[bk][0:r, :],
                                    q_sb[:, dc * T_EXT + q0: dc * T_EXT + q0 + r],
                                    k_sb[kp][:, dc * KQ + sb4 * 512: dc * KQ + (sb4 + 1) * 512],
                                    start=(dc == 0), stop=(dc == 7)),
                                    reads=[("q_sb", dc), ("k_sb", kp, dc)], writes=[BK[bk]],
                                    tag=f"s_mm{kq}_{qt}_{sb4}_{dc}")
                            p.add("scalar", lambda sc, sb4=sb4, bk=bk, r=r, qp=qp: sc.activation(
                                S_f[qp][0:r, sb4 * 512:(sb4 + 1) * 512], bank[bk][0:r, :], AF.Copy),
                                reads=[BK[bk]], writes=[("S_f", qp, sb4)],
                                tag=f"s_ev{kq}_{qt}_{sb4}")
                        SFK = [("S_f", qp, i) for i in range(KQ // 512)]
                        p.add("vector", lambda v, r=r, tcol=tcol, qp=qp: v.tensor_reduce(
                            tcol[0:r, 0:1], S_f[qp][0:r, :], AX.X, ALU.max, negate=True),
                            reads=SFK, writes=[("tcol", qp, 0)], tag=f"qmax{kq}_{qt}")
                        if kq == 0:
                            p.add("vector", lambda v, r=r, qt=qt, tcol=tcol: v.tensor_scalar(
                                mrun[0:r, qt:qt + 1], tcol[0:r, 0:1], -1.0, 0.0,
                                ALU.mult, ALU.add),
                                reads=[("tcol", qp, 0)], writes=[("mrun", qt)],
                                tag=f"mr0_{kq}_{qt}")
                        else:
                            p.add("vector", lambda v, r=r, tcol=tcol: v.tensor_scalar(
                                tcol[0:r, 1:2], tcol[0:r, 0:1], -1.0, 0.0, ALU.mult, ALU.add),
                                reads=[("tcol", qp, 0)], writes=[("tcol", qp, 1)], tag=f"qm_{kq}_{qt}")
                            p.add("vector", lambda v, r=r, qt=qt, tcol=tcol: v.tensor_tensor(
                                tcol[0:r, 2:3], mrun[0:r, qt:qt + 1], tcol[0:r, 1:2], ALU.max),
                                reads=[("tcol", qp, 1), ("mrun", qt)], writes=[("tcol", qp, 2)],
                                tag=f"mn_{kq}_{qt}")
                            p.add("vector", lambda v, r=r, qt=qt, tcol=tcol: v.tensor_tensor(
                                tcol[0:r, 3:4], mrun[0:r, qt:qt + 1], tcol[0:r, 2:3], ALU.subtract),
                                reads=[("tcol", qp, 2), ("mrun", qt)], writes=[("tcol", qp, 3)],
                                tag=f"md_{kq}_{qt}")
                            p.add("scalar", lambda sc, r=r, tcol=tcol: sc.activation(
                                tcol[0:r, 4:5], tcol[0:r, 3:4], AF.Exp, scale=SCALE),
                                reads=[("tcol", qp, 3)], writes=[("tcol", qp, 4)], tag=f"al_{kq}_{qt}")
                            p.add("vector", lambda v, r=r, qt=qt, tcol=tcol: v.tensor_copy(
                                mrun[0:r, qt:qt + 1], tcol[0:r, 2:3]),
                                reads=[("tcol", qp, 2)], writes=[("mrun", qt)], tag=f"mu_{kq}_{qt}")
                        p.add("vector", lambda v, r=r, qt=qt, tcol=tcol: v.tensor_scalar(
                            tcol[0:r, 5:6], mrun[0:r, qt:qt + 1], -SCALE, 0.0, ALU.mult, ALU.add),
                            reads=[("mrun", qt)], writes=[("tcol", qp, 5)], tag=f"pb_{kq}_{qt}")
                        p.add("scalar", lambda sc, r=r, tcol=tcol, qp=qp: sc.activation(
                            P_b[qp][0:r, :], S_f[qp][0:r, :], AF.Exp, scale=SCALE,
                            bias=tcol[0:r, 5:6], accum_out=tcol[0:r, 0:1]),
                            reads=SFK + [("tcol", qp, 5)], writes=[("P_b", qp), ("tcol", qp, 0)],
                            tag=f"pe_{kq}_{qt}")
                        if kq == 0:
                            p.add("vector", lambda v, r=r, qt=qt, tcol=tcol: v.tensor_copy(
                                srun[0:r, qt:qt + 1], tcol[0:r, 0:1]),
                                reads=[("tcol", qp, 0)], writes=[("srun", qt)], tag=f"s0_{kq}_{qt}")
                        else:
                            p.add("vector", lambda v, r=r, qt=qt, tcol=tcol: v.scalar_tensor_tensor(
                                srun[0:r, qt:qt + 1], srun[0:r, qt:qt + 1], tcol[0:r, 4:5],
                                tcol[0:r, 0:1], ALU.mult, ALU.add),
                                reads=[("tcol", qp, 0), ("tcol", qp, 4), ("srun", qt)],
                                writes=[("srun", qt)], tag=f"sr_{kq}_{qt}")
                            p.add("vector", lambda v, r=r, qt=qt, tcol=tcol: v.tensor_scalar(
                                oacc[qt][0:r, :], oacc[qt][0:r, :], tcol[0:r, 4:5], 0.0,
                                ALU.mult, ALU.add),
                                reads=[("oacc", qt), ("tcol", qp, 4)], writes=[("oacc", qt)],
                                tag=f"oa_{kq}_{qt}")
                        for kc in range(NKC):
                            p.add("tensor", lambda te, kc=kc, r=r, qp=qp: te.transpose(
                                bkt[:, 0:r], P_b[qp][0:r, kc * 128:(kc + 1) * 128], identb[0:r, 0:r]),
                                reads=[("P_b", qp), "identb"], writes=[BKT], tag=f"pt_{kq}_{qt}_{kc}")
                            p.add("vector", lambda v, kc=kc, r=r, qp=qp: v.tensor_copy(
                                PT_b[qp][:, kc * 128:kc * 128 + r], bkt[:, 0:r]),
                                reads=[BKT], writes=[("PT_b", qp, kc)], tag=f"pc_{kq}_{qt}_{kc}")
                        for fh in range(2):
                            bk = 4 + fh
                            for kc in range(NKC):
                                p.add("tensor", lambda te, kc=kc, fh=fh, bk=bk, r=r, qp=qp, kp=kp: te.matmul(
                                    bank[bk][0:r, :],
                                    PT_b[qp][:, kc * 128:kc * 128 + r],
                                    v_sb[kp][:, kc * DH + fh * 512: kc * DH + (fh + 1) * 512],
                                    start=(kc == 0), stop=(kc == NKC - 1)),
                                    reads=[("PT_b", qp, kc), ("v_sb", kp, kc)], writes=[BK[bk]],
                                    tag=f"pv_{kq}_{qt}_{fh}_{kc}")
                            if kq == 0:
                                p.add("scalar", lambda sc, fh=fh, bk=bk, r=r, qt=qt: sc.activation(
                                    oacc[qt][0:r, fh * 512:(fh + 1) * 512],
                                    bank[bk][0:r, :], AF.Copy),
                                    reads=[BK[bk]], writes=[("oacc", qt)],
                                    tag=f"pz_{kq}_{qt}_{fh}")
                            else:
                                p.add("vector", lambda v, fh=fh, bk=bk, r=r, qt=qt, qp=qp: v.tensor_tensor(
                                    oacc[qt][0:r, fh * 512:(fh + 1) * 512],
                                    oacc[qt][0:r, fh * 512:(fh + 1) * 512],
                                    bank[bk][0:r, :], ALU.add),
                                    reads=[BK[bk], ("oacc", qt)], writes=[("oacc", qt)],
                                    tag=f"pa_{kq}_{qt}_{fh}")
                for qt in range(NQT):
                    r = QT_ROWS[qt]
                    qp = qt % 2
                    tcol = tcolr[qp]
                    p.add("vector", lambda v, r=r, qt=qt, tcol=tcol: v.reciprocal(
                        tcol[0:r, 0:1], srun[0:r, qt:qt + 1]),
                        reads=[("srun", qt)], writes=[("tcol", qp, 0)], tag=f"fr{qt}")
                    p.add("vector", lambda v, r=r, qt=qt, tcol=tcol, qp=qp: v.tensor_scalar(
                        ao_tm[qp][0:r, :], oacc[qt][0:r, :], tcol[0:r, 0:1], 0.0,
                        ALU.mult, ALU.add),
                        reads=[("oacc", qt), ("tcol", qp, 0)], writes=[("ao_tm", qp)], tag=f"fs{qt}")
                    for dc in range(8):
                        p.add("tensor", lambda te, dc=dc, r=r, qp=qp: te.transpose(
                            bkt[:, 0:r], ao_tm[qp][0:r, dc * 128:(dc + 1) * 128], identb[0:r, 0:r]),
                            reads=[("ao_tm", qp), "identb"], writes=[BKT], tag=f"at{qt}_{dc}")
                        p.add("vector", lambda v, dc=dc, r=r, qt=qt: v.tensor_copy(
                            ao_fm[:, dc * T_EXT + qt * 128: dc * T_EXT + qt * 128 + r],
                            bkt[:, 0:r]),
                            reads=[BKT], writes=[("ao_fm", dc, qt)], tag=f"af{qt}_{dc}")
            with ctx.scope():
                wo_t = [ctx.sbuf(f"wo_t{j}", [128, 512], BF16) for j in range(4)]
                bo_b = ctx.sbuf("bo_b", [1, DM], BF16)
                x0_t = [ctx.sbuf(f"x0_t{j}", [128, 512], F32) for j in range(2)]
                p.add("gpsimd", lambda g: g.dma_start(out=bo_b[:, :], in_=bo[:, :]),
                      writes=["bo_b"], dma=True, tag="ld_bo")
                for qtg in (range(0, 6), range(6, 9)):
                    for f5 in range(8):
                        for dc in range(8):
                            j = dc % 4
                            p.add("gpsimd", lambda g, f5=f5, dc=dc, j=j: g.dma_start(
                                out=wo_t[j][:, :],
                                in_=wo[dc * 128:(dc + 1) * 128, f5 * 512:(f5 + 1) * 512]),
                                writes=[("wo_t", j)], dma=True, tag=f"lw{f5}_{dc}")
                            for bi, qt in enumerate(qtg):
                                r = QT_ROWS[qt]
                                p.add("tensor", lambda te, dc=dc, j=j, bi=bi, qt=qt, r=r: te.matmul(
                                    bank[bi][0:r, :],
                                    ao_fm[:, dc * T_EXT + qt * 128: dc * T_EXT + qt * 128 + r],
                                    wo_t[j][:, :], start=(dc == 0), stop=False),
                                    reads=[("wo_t", j), ("ao_fm", dc, qt)], writes=[BK[bi]],
                                    tag=f"wm{f5}_{qt}_{dc}")
                        for bi, qt in enumerate(qtg):
                            r = QT_ROWS[qt]
                            p.add("tensor", lambda te, bi=bi, r=r, f5=f5: te.matmul(
                                bank[bi][0:r, :], ones_b[:, 0:r],
                                bo_b[:, f5 * 512:(f5 + 1) * 512], start=False, stop=True),
                                reads=["ones_b", "bo_b"], writes=[BK[bi]], tag=f"wb{f5}_{qt}")
                            j = bi % 2
                            p.add("sync", lambda sy, qt=qt, f5=f5, j=j, r=r: sy.dma_start(
                                out=x0_t[j][0:r, :],
                                in_=x0e[qt * 128:qt * 128 + r, f5 * 512:(f5 + 1) * 512]),
                                writes=[("x0_t", j)], dma=True, tag=f"lx{f5}_{qt}")
                            p.add("vector", lambda v, bi=bi, j=j, r=r: v.tensor_tensor(
                                ev_f[j][0:r, :], bank[bi][0:r, :], x0_t[j][0:r, :], ALU.add),
                                reads=[BK[bi], ("x0_t", j)], writes=[("ev_f", j)],
                                tag=f"wr{f5}_{qt}")
                            p.add("gpsimd", lambda g, qt=qt, f5=f5, j=j, r=r: g.dma_start(
                                out=x1d[qt * 128:qt * 128 + r, f5 * 512:(f5 + 1) * 512],
                                in_=ev_f[j][0:r, :]),
                                reads=[("ev_f", j)], writes=[("x1d", qt, f5)], dma=True,
                                tag=f"sx{f5}_{qt}")
        X1K = [("x1d", qt, f5) for qt in range(NQT) for f5 in range(8)]

        with ctx.scope():
            emit_ln_stream(ctx, x1d, xn2d, "xn2d", T_EXT, "ln2", src_keys=X1K)
        XN2K = [("xn2d", i) for i in range(9)]
        with ctx.scope():
            xn2fm = ctx.sbuf("xn2fm", [128, 32 * T_PAD], BF16)
            for cc_i in range(32):
                p.add("sync", lambda sc, cc_i=cc_i: sc.dma_start(
                    out=xn2fm[:, cc_i * T_PAD:(cc_i + 1) * T_PAD],
                    in_=xn2d[:, cc_i * 128:(cc_i + 1) * 128], transpose=True),
                    reads=XN2K, writes=[("xn2fm", cc_i)], dma=True, tag=f"t2_{cc_i}")
            fb_b = ctx.sbuf("fb_b", [1, DM], BF16)
            p.add("gpsimd", lambda g: g.dma_start(out=fb_b[:, :], in_=fbias[:, :]),
                  writes=["fb_b"], dma=True, tag="ld_fb")
            wf_t = [ctx.sbuf(f"wf_t{j}", [128, 512], BF16) for j in range(8)]
            xr_t = [ctx.sbuf(f"xr_t{j}", [128, 512], F32) for j in range(4)]

            def ffn_pass(tgs, fm_sb, fm_key, src_dram, src_keys, dst_dram,
                         dst_key, rows, tagp):
                for tg in tgs:
                    for og in range(8):
                        for cc_i in range(32):
                            j = cc_i % 8
                            p.add("sync", lambda sy, og=og, cc_i=cc_i, j=j: sy.dma_start(
                                out=wf_t[j][:, :],
                                in_=wffn[cc_i * 128:(cc_i + 1) * 128, og * 512:(og + 1) * 512]),
                                writes=[(f"{tagp}wf", j)], dma=True,
                                tag=f"{tagp}w{tg[0]}_{og}_{cc_i}")
                            for bi, qt in enumerate(tg):
                                r = rows[qt]
                                p.add("tensor", lambda te, cc_i=cc_i, j=j, bi=bi, qt=qt, r=r: te.matmul(
                                    bank[bi][0:r, :],
                                    fm_sb[:, cc_i * T_PAD + qt * 128: cc_i * T_PAD + qt * 128 + r],
                                    wf_t[j][:, :], start=(cc_i == 0), stop=False),
                                    reads=[(f"{tagp}wf", j), (fm_key, cc_i)],
                                    writes=[BK[bi]], tag=f"{tagp}m{og}_{qt}_{cc_i}")
                        for bi, qt in enumerate(tg):
                            r = rows[qt]
                            p.add("tensor", lambda te, bi=bi, r=r, og=og: te.matmul(
                                bank[bi][0:r, :], ones_b[:, 0:r],
                                fb_b[:, og * 512:(og + 1) * 512], start=False, stop=True),
                                reads=["ones_b", "fb_b"], writes=[BK[bi]],
                                tag=f"{tagp}b{og}_{qt}")
                            j = bi % 2
                            p.add("scalar", lambda sc, qt=qt, og=og, j=j, r=r: sc.dma_start(
                                out=xr_t[j][0:r, :],
                                in_=src_dram[qt * 128:qt * 128 + r, og * 512:(og + 1) * 512]),
                                reads=src_keys, writes=[(f"{tagp}x", j)], dma=True,
                                tag=f"{tagp}l{og}_{qt}")
                            p.add("vector", lambda v, bi=bi, j=j, r=r: v.tensor_tensor(
                                ev_f[j][0:r, :], bank[bi][0:r, :], xr_t[j][0:r, :], ALU.add),
                                reads=[BK[bi], (f"{tagp}x", j)], writes=[("ev_f", j)],
                                tag=f"{tagp}r{og}_{qt}")
                            p.add("gpsimd", lambda g, qt=qt, og=og, j=j, r=r: g.dma_start(
                                out=dst_dram[qt * 128:qt * 128 + r, og * 512:(og + 1) * 512],
                                in_=ev_f[j][0:r, :]),
                                reads=[("ev_f", j)], writes=[(dst_key, qt, og)], dma=True,
                                tag=f"{tagp}s{og}_{qt}")

            ffn_pass([range(0, 5), range(5, 9)], xn2fm, "xn2fm",
                     x1d, X1K, x2_out, "x2o", QT_ROWS, "f1")
        X2K = [("x2o", qt, f5) for qt in range(NQT) for f5 in range(8)]

        with ctx.scope():
            emit_ln_stream(ctx, x2_out, xn3d, "xn3d", T_EXT, "ln3", src_keys=X2K)
        XN3K = [("xn3d", i) for i in range(9)]

        B_fm = ctx.sbuf("B_fm", [128, 8 * T_OWN], BF16)
        C_fm = ctx.sbuf("C_fm", [128, 8 * T_OWN], BF16)
        dt_raw = ctx.sbuf("dt_raw", [128, T_PAD], F32)
        with ctx.scope():
            xn3fm = ctx.sbuf("xn3fm", [128, 32 * T_PAD], BF16)
            for cc_i in range(32):
                p.add("sync", lambda sc, cc_i=cc_i: sc.dma_start(
                    out=xn3fm[:, cc_i * T_PAD:(cc_i + 1) * T_PAD],
                    in_=xn3d[:, cc_i * 128:(cc_i + 1) * 128], transpose=True),
                    reads=XN3K, writes=[("xn3fm", cc_i)], dma=True, tag=f"t3_{cc_i}")
            ip_t = [ctx.sbuf(f"ip_t{j}", [128, 512], BF16) for j in range(8)]
            gate_ev = [ctx.sbuf(f"gate_ev{j}", [128, 512], F32) for j in range(2)]
            for tg in (range(0, 4), range(4, 8)):
                for f5 in range(16):
                    for cc_i in range(32):
                        j = cc_i % 8
                        p.add("sync", lambda sy, f5=f5, cc_i=cc_i, j=j: sy.dma_start(
                            out=ip_t[j][:, :],
                            in_=ipbf[cc_i * 128:(cc_i + 1) * 128, f5 * 512:(f5 + 1) * 512]),
                            writes=[("ip_t", j)], dma=True, tag=f"gw{tg[0]}_{f5}_{cc_i}")
                        for bi, t in enumerate(tg):
                            p.add("tensor", lambda te, cc_i=cc_i, j=j, bi=bi, t=t: te.matmul(
                                bank[bi][:, :],
                                xn3fm[:, cc_i * T_PAD + 3 + t * 128: cc_i * T_PAD + 3 + (t + 1) * 128],
                                ip_t[j][:, :], start=(cc_i == 0), stop=(cc_i == 31)),
                                reads=[("ip_t", j), ("xn3fm", cc_i)], writes=[BK[bi]],
                                tag=f"gm{f5}_{t}_{cc_i}")
                    for bi, t in enumerate(tg):
                        j = bi % 2
                        p.add("scalar", lambda sc, bi=bi, j=j, f5=f5, t=t: sc.activation(
                            gate_ev[j][:, :], bank[bi][:, :], AF.Silu),
                            reads=[BK[bi]], writes=[("gate_ev", j)], tag=f"ge{f5}_{t}")
                        p.add("gpsimd", lambda g, t=t, f5=f5, j=j: g.dma_start(
                            out=gate_out[t * 128:(t + 1) * 128, f5 * 512:(f5 + 1) * 512],
                            in_=gate_ev[j][:, :]),
                            reads=[("gate_ev", j)], writes=[("gate_o", t, f5)], dma=True,
                            tag=f"gs{f5}_{t}")
            stash = [ctx.sbuf(f"stash{c}", [128, 512], BF16) for c in range(32)]
            xbcr = ctx.sbuf("xbcr", [128, T_PAD], F32)
            conv_a = ctx.sbuf("conv_a", [128, T_OWN], F32)
            conv_b2 = ctx.sbuf("conv_b2", [128, T_OWN], F32)
            cwr = ctx.sbuf("cwr", [128, 320], F32)
            cbr = ctx.sbuf("cbr", [128, 80], F32)
            hm = ctx.sbuf("hm", [128, 4], F32)
            xsil_b = ctx.sbuf("xsil_b", [128, T_OWN], BF16)
            p.add("sync", lambda sy: sy.dma_start(out=cwr[:, :], in_=convw_r[:, :]),
                  writes=["cwr"], dma=True, tag="ld_cwr")
            p.add("sync", lambda sy: sy.dma_start(out=cbr[:, :], in_=convb_r[:, :]),
                  writes=["cbr"], dma=True, tag="ld_cbr")
            p.add("sync", lambda sy: sy.dma_start(out=hm[:, :], in_=halo_m[:, :]),
                  writes=["hm"], dma=True, tag="ld_hm")

            def emit_conv_silu(ft, tagc):
                p.add("vector", lambda v: v.tensor_tensor(
                    xbcr[:, 0:3], xbcr[:, 0:3], hm[:, 0:3], ALU.mult),
                    reads=["xbcr", "hm"], writes=["xbcr"], tag=f"{tagc}hm")
                p.add("vector", lambda v, ft=ft: v.tensor_scalar(
                    conv_a[:, :], xbcr[:, 0:T_OWN], cwr[:, ft * 4:ft * 4 + 1],
                    cbr[:, ft:ft + 1], ALU.mult, ALU.add),
                    reads=["xbcr", "cwr", "cbr"], writes=["conv_a"], tag=f"{tagc}c0")
                p.add("vector", lambda v, ft=ft: v.scalar_tensor_tensor(
                    conv_b2[:, :], xbcr[:, 1:1 + T_OWN], cwr[:, ft * 4 + 1:ft * 4 + 2],
                    conv_a[:, :], ALU.mult, ALU.add),
                    reads=["xbcr", "cwr", "conv_a"], writes=["conv_b2"], tag=f"{tagc}c1")
                p.add("vector", lambda v, ft=ft: v.scalar_tensor_tensor(
                    conv_a[:, :], xbcr[:, 2:2 + T_OWN], cwr[:, ft * 4 + 2:ft * 4 + 3],
                    conv_b2[:, :], ALU.mult, ALU.add),
                    reads=["xbcr", "cwr", "conv_b2"], writes=["conv_a"], tag=f"{tagc}c2")
                p.add("vector", lambda v, ft=ft: v.scalar_tensor_tensor(
                    conv_b2[:, :], xbcr[:, 3:3 + T_OWN], cwr[:, ft * 4 + 3:ft * 4 + 4],
                    conv_a[:, :], ALU.mult, ALU.add),
                    reads=["xbcr", "cwr", "conv_a"], writes=["conv_b2"], tag=f"{tagc}c3")
                if ft < 64:
                    p.add("scalar", lambda sc: sc.activation(
                        xsil_b[:, :], conv_b2[:, :], AF.Silu),
                        reads=["conv_b2"], writes=["xsil_b"], tag=f"{tagc}si")
                    p.add("gpsimd", lambda g, ft=ft: g.dma_start(
                        out=xsil_d[ft * 128:(ft + 1) * 128, :], in_=xsil_b[:, :]),
                        reads=["xsil_b"], writes=[("xsil", ft)], dma=True, tag=f"{tagc}sx")
                elif ft < 72:
                    gi = ft - 64
                    p.add("scalar", lambda sc, gi=gi: sc.activation(
                        B_fm[:, gi * T_OWN:(gi + 1) * T_OWN], conv_b2[:, :], AF.Silu),
                        reads=["conv_b2"], writes=[("B_fm", gi)], tag=f"{tagc}si")
                else:
                    gi = ft - 72
                    p.add("scalar", lambda sc, gi=gi: sc.activation(
                        C_fm[:, gi * T_OWN:(gi + 1) * T_OWN], conv_b2[:, :], AF.Silu),
                        reads=["conv_b2"], writes=[("C_fm", gi)], tag=f"{tagc}si")
                    p.add("gpsimd", lambda g, gi=gi: g.dma_start(
                        out=c_out[gi * 128:(gi + 1) * 128, :],
                        in_=C_fm[:, gi * T_OWN:(gi + 1) * T_OWN]),
                        reads=[("C_fm", gi)], writes=[("c_o", gi)], dma=True,
                        tag=f"{tagc}sc")

            TSL = [(0, 512), (512, 512), (1024, 128)]

            def emit_fsub(ft, sub, tagx):
                fofs = (ft % 4) * 128
                for cc_i in range(32):
                    for si, (t0, tw) in enumerate(TSL):
                        bi = sub * 3 + si
                        p.add("tensor", lambda te, cc_i=cc_i, fofs=fofs, t0=t0, tw=tw, bi=bi: te.matmul(
                            bank[bi][:, 0:tw],
                            stash[cc_i][:, fofs:fofs + 128],
                            xn3fm[:, cc_i * T_PAD + t0: cc_i * T_PAD + t0 + tw],
                            start=(cc_i == 0), stop=(cc_i == 31)),
                            reads=[("stash", cc_i), ("xn3fm", cc_i)], writes=[BK[bi]],
                            tag=f"{tagx}m{si}_{cc_i}")
                if ft < 80:
                    for si, (t0, tw) in enumerate(TSL):
                        bi = sub * 3 + si
                        p.add("scalar", lambda sc, bi=bi, t0=t0, tw=tw: sc.activation(
                            xbcr[:, t0:t0 + tw], bank[bi][:, 0:tw], AF.Copy),
                            reads=[BK[bi]], writes=["xbcr"], tag=f"{tagx}e{si}")
                    emit_conv_silu(ft, tagx)
                else:
                    for si, (t0, tw) in enumerate(TSL):
                        bi = sub * 3 + si
                        p.add("scalar", lambda sc, bi=bi, t0=t0, tw=tw: sc.activation(
                            dt_raw[:, t0:t0 + tw], bank[bi][:, 0:tw], AF.Copy),
                            reads=[BK[bi]], writes=["dt_raw"], tag=f"{tagx}d{si}")

            for fg in range(20):
                frow0 = DIN + fg * 512
                for cc_i in range(32):
                    p.add("sync", lambda sy, frow0=frow0, cc_i=cc_i: sy.dma_start(
                        out=stash[cc_i][:, :],
                        in_=ipbf[cc_i * 128:(cc_i + 1) * 128, frow0:frow0 + 512]),
                        writes=[("stash", cc_i)], dma=True, tag=f"xw{fg}_{cc_i}")
                emit_fsub(fg * 4 + 0, 0, f"xa{fg}_0_")
                emit_fsub(fg * 4 + 1, 1, f"xa{fg}_1_")
                emit_fsub(fg * 4 + 2, 0, f"xb{fg}_2_")
                emit_fsub(fg * 4 + 3, 1, f"xb{fg}_3_")
            for cc_i in range(32):
                p.add("sync", lambda sy, cc_i=cc_i: sy.dma_start(
                    out=stash[cc_i][:, 0:128],
                    in_=ipbf[cc_i * 128:(cc_i + 1) * 128, DIN + CONV_DIM:PROJ]),
                    writes=[("stash", cc_i)], dma=True, tag=f"dw_{cc_i}")
            emit_fsub(80, 0, "dt_")

        acs_ti = ctx.sbuf("acs_ti", [128, 8 * 128], F32)
        dt_ti = ctx.sbuf("dt_ti", [128, 8 * 128], F32)
        dtds_ti = ctx.sbuf("dtds_ti", [128, 8 * 128], F32)
        ttile = ctx.sbuf("ttile", [128, NCH], F32)
        expt_bc = ctx.sbuf("expt_bc", [128, NCH * 128], F32)
        with ctx.scope():
            dtb_s = ctx.sbuf("dtb_s", [128, 1], F32)
            al_s = ctx.sbuf("al_s", [128, 1], F32)
            p.add("sync", lambda sy: sy.dma_start(out=dtb_s[:, :], in_=dtb_col[:, :]),
                  writes=["dtb_s"], dma=True, tag="ld_dtb")
            p.add("sync", lambda sy: sy.dma_start(out=al_s[:, :], in_=alog_col[:, :]),
                  writes=["al_s"], dma=True, tag="ld_al")
            dt_ht = ctx.sbuf("dt_ht", [128, T_OWN], F32)
            sp_a = ctx.sbuf("sp_a", [128, T_OWN], F32)
            sp_b = ctx.sbuf("sp_b", [128, T_OWN], F32)
            nega = ctx.sbuf("nega", [128, 2], F32)
            adt_ht = ctx.sbuf("adt_ht", [128, T_OWN], F32)
            acs_ht = ctx.sbuf("acs_ht", [128, T_OWN], F32)
            ds_ht = ctx.sbuf("ds_ht", [128, T_OWN], F32)
            dtds_ht = ctx.sbuf("dtds_ht", [128, T_OWN], F32)
            zeros_t = ctx.sbuf("zeros_t", [128, CS], F32)
            p.add("vector", lambda v: v.memset(zeros_t[:, :], 0.0),
                  writes=["zeros_t"], tag="zt")
            p.add("vector", lambda v: v.tensor_scalar(
                dt_ht[:, :], dt_raw[:, 3:3 + T_OWN], dtb_s[:, 0:1], 0.0, ALU.add, ALU.add),
                reads=["dt_raw", "dtb_s"], writes=["dt_ht"], tag="sp_z")
            p.add("scalar", lambda sc: sc.activation(sp_a[:, :], dt_ht[:, :], AF.Abs),
                  reads=["dt_ht"], writes=["sp_a"], tag="sp_abs")
            p.add("scalar", lambda sc: sc.activation(sp_b[:, :], sp_a[:, :], AF.Exp, scale=-1.0),
                  reads=["sp_a"], writes=["sp_b"], tag="sp_exp")
            p.add("scalar", lambda sc: sc.activation(sp_a[:, :], sp_b[:, :], AF.Ln, bias=1.0),
                  reads=["sp_b"], writes=["sp_a"], tag="sp_ln")
            p.add("vector", lambda v: v.tensor_scalar(
                sp_b[:, :], dt_ht[:, :], 0.0, 0.0, ALU.max, ALU.add),
                reads=["dt_ht"], writes=["sp_b"], tag="sp_relu")
            p.add("vector", lambda v: v.tensor_tensor(
                dt_ht[:, :], sp_a[:, :], sp_b[:, :], ALU.add),
                reads=["sp_a", "sp_b"], writes=["dt_ht"], tag="sp_add")
            p.add("scalar", lambda sc: sc.activation(nega[:, 0:1], al_s[:, 0:1], AF.Exp),
                  reads=["al_s"], writes=["nega"], tag="na_e")
            p.add("vector", lambda v: v.tensor_scalar(
                nega[:, 1:2], nega[:, 0:1], -1.0, 0.0, ALU.mult, ALU.add),
                reads=["nega"], writes=["nega"], tag="na_n")
            p.add("vector", lambda v: v.tensor_scalar(
                adt_ht[:, :], dt_ht[:, :], nega[:, 1:2], 0.0, ALU.mult, ALU.add),
                reads=["dt_ht", "nega"], writes=["adt_ht"], tag="adt")
            for c in range(NCH):
                p.add("vector", lambda v, c=c: v.tensor_tensor_scan(
                    acs_ht[:, c * CS:(c + 1) * CS], adt_ht[:, c * CS:(c + 1) * CS],
                    zeros_t[:, :], 0.0, ALU.add, ALU.add),
                    reads=["adt_ht", "zeros_t"], writes=["acs_ht"], tag=f"scan{c}")
            p.add("gpsimd", lambda g: g.dma_start(out=acsht_d[:, :], in_=acs_ht[:, :]),
                  reads=["acs_ht"], writes=["acsht_d"], dma=True, tag="st_acsht")
            for c in range(NCH):
                p.add("vector", lambda v, c=c: v.tensor_copy(
                    ttile[:, c:c + 1], acs_ht[:, (c + 1) * CS - 1:(c + 1) * CS]),
                    reads=["acs_ht"], writes=["ttile"], tag=f"tc{c}")
            p.add("gpsimd", lambda g: g.dma_start(out=t_out[:, :], in_=ttile[:, :]),
                  reads=["ttile"], writes=["t_o"], dma=True, tag="st_t")
            for c in range(NCH):
                p.add("scalar", lambda sc, c=c: sc.activation(
                    ds_ht[:, c * CS:(c + 1) * CS], acs_ht[:, c * CS:(c + 1) * CS],
                    AF.Exp, scale=-1.0, bias=ttile[:, c:c + 1]),
                    reads=["acs_ht", "ttile"], writes=["ds_ht"], tag=f"ds{c}")
            p.add("vector", lambda v: v.tensor_tensor(
                dtds_ht[:, :], dt_ht[:, :], ds_ht[:, :], ALU.mult),
                reads=["dt_ht", "ds_ht"], writes=["dtds_ht"], tag="dtds")
            for srcap, srck, dst, kk in ((acs_ht, "acs_ht", acs_ti, "acs_ti"),
                                         (dt_ht, "dt_ht", dt_ti, "dt_ti"),
                                         (dtds_ht, "dtds_ht", dtds_ti, "dtds_ti")):
                for tt in range(8):
                    p.add("tensor", lambda te, srcap=srcap, tt=tt: te.transpose(
                        bank[6][:, 0:128], srcap[:, tt * 128:(tt + 1) * 128], identf[:, :]),
                        reads=[srck, "identf"], writes=[BK[6]], tag=f"{kk}t{tt}")
                    p.add("vector", lambda v, dst=dst, tt=tt, kk=kk: v.tensor_copy(
                        dst[:, tt * 128:(tt + 1) * 128], bank[6][:, 0:128]),
                        reads=[BK[6]], writes=[(kk, tt)], tag=f"{kk}c{tt}")
            for tt in range(8):
                p.add("gpsimd", lambda g, tt=tt: g.dma_start(
                    out=acs_out[tt * 128:(tt + 1) * 128, :],
                    in_=acs_ti[:, tt * 128:(tt + 1) * 128]),
                    reads=[("acs_ti", tt)], writes=[("acs_o", tt)], dma=True,
                    tag=f"sa{tt}")
            expt_r = ctx.sbuf("expt_r", [1, 128], F32)
            for c in range(NCH):
                p.add("gpsimd", lambda g, c=c: g.dma_start(
                    out=expt_r[0:1, 0:128],
                    in_=acs_out[(c + 1) * CS - 1:(c + 1) * CS, :]),
                    reads=[("acs_o", 2 * c + 1)], writes=["expt_r"], dma=True,
                    tag=f"er{c}")
                p.add("scalar", lambda sc: sc.activation(
                    expt_r[0:1, 0:128], expt_r[0:1, 0:128], AF.Exp),
                    reads=["expt_r"], writes=["expt_r"], tag=f"ee{c}")
                p.add("tensor", lambda te, c=c: te.matmul(
                    bank[6][:, 0:128], ones_f[:, :], expt_r[0:1, 0:128],
                    start=True, stop=True),
                    reads=["expt_r", "ones_f"], writes=[BK[6]], tag=f"em{c}")
                p.add("vector", lambda v, c=c: v.tensor_copy(
                    expt_bc[:, c * 128:(c + 1) * 128], bank[6][:, 0:128]),
                    reads=[BK[6]], writes=[("expt_bc", c)], tag=f"ec{c}")

        with ctx.scope():
            tri_s = ctx.sbuf("tri_s", [128, 512], F32)
            p.add("sync", lambda sy: sy.dma_start(out=tri_s[:, 0:256], in_=tri0[:, :]),
                  writes=["tri_s"], dma=True, tag="ld_tri0")
            p.add("sync", lambda sy: sy.dma_start(out=tri_s[:, 256:512], in_=tri1[:, :]),
                  writes=["tri_s"], dma=True, tag="ld_tri1")
            xstage = [ctx.sbuf(f"xstage{h2}", [128, 1024], BF16) for h2 in range(4)]
            btm = [ctx.sbuf(f"btm{j}", [128, 256], BF16) for j in range(2)]
            gmt = [ctx.sbuf(f"gmt{j}", [128, 512], F32) for j in range(2)]
            ltg = [ctx.sbuf(f"ltg{j}", [128, 512], BF16) for j in range(2)]
            lt1 = [ctx.sbuf(f"lt1_{j}", [128, 512], F32) for j in range(2)]
            arow = ctx.sbuf("arow", [1, 4096], F32)
            xs1 = [ctx.sbuf(f"xs1_{j}", [128, 128], BF16) for j in range(2)]
            xs2 = [ctx.sbuf(f"xs2_{j}", [128, 128], BF16) for j in range(2)]
            piece = ctx.sbuf("piece", [128, 1024], F32)
            dxl = ctx.sbuf("dxl", [128, 1024], F32)
            dex_t = [ctx.sbuf(f"dex_t{j}", [128, 1024], F32) for j in range(2)]
            slocal = ctx.sbuf("slocal", [128, DIN], F32)
            srun_s = ctx.sbuf("srun_s", [128, DIN], F32)
            p.add("vector", lambda v: v.memset(srun_s[:, :], 0.0),
                  writes=["srun_s"], tag="sr0")
            for c in range(NCH):
                if c > 0:
                    p.add("gpsimd", lambda g, c=c: g.dma_start(
                        out=sloc_out[(c - 1) * NST:c * NST, :], in_=srun_s[:, :]),
                        reads=["srun_s"], writes=[("sloc_o", c)], dma=True,
                        tag=f"ssl{c}")
                for g_i in range(G):
                    g2 = g_i % 2
                    for h2 in range(2):
                        p.add("sync", lambda sc, c=c, g_i=g_i, h2=h2, g2=g2: sc.dma_start(
                            out=xstage[g2 * 2 + h2][:, :],
                            in_=xsil_d[g_i * 1024:(g_i + 1) * 1024,
                                       c * CS + h2 * 128: c * CS + (h2 + 1) * 128],
                            transpose=True),
                            reads=[("xsil", g_i * 8 + k) for k in range(8)],
                            writes=[("xstage", g2, h2)], dma=True, tag=f"xs{c}_{g_i}_{h2}")
                    p.add("sync", lambda sy, g_i=g_i, g2=g2: sy.dma_start(
                        out=dex_t[g2][:, :], in_=d_exp[:, g_i * 1024:(g_i + 1) * 1024]),
                        writes=[("dex_t", g2)], dma=True, tag=f"dx{c}_{g_i}")
                    for h2 in range(2):
                        p.add("tensor", lambda te, c=c, g_i=g_i, h2=h2: te.transpose(
                            bkt[:, 0:128],
                            B_fm[:, g_i * T_OWN + c * CS + h2 * 128: g_i * T_OWN + c * CS + (h2 + 1) * 128],
                            identb[:, :]),
                            reads=[("B_fm", g_i), "identb"], writes=[BKT],
                            tag=f"bt{c}_{g_i}_{h2}")
                        p.add("vector", lambda v, h2=h2, g2=g2: v.tensor_copy(
                            btm[g2][:, h2 * 128:(h2 + 1) * 128], bkt[:, 0:128]),
                            reads=[BKT], writes=[("btm", g2, h2)], tag=f"bc{c}_{g_i}_{h2}")
                    for jt in range(2):
                        p.add("tensor", lambda te, c=c, g_i=g_i, jt=jt: te.matmul(
                            bank[6][:, 0:256],
                            B_fm[:, g_i * T_OWN + c * CS + jt * 128: g_i * T_OWN + c * CS + (jt + 1) * 128],
                            C_fm[:, g_i * T_OWN + c * CS: g_i * T_OWN + (c + 1) * CS],
                            start=True, stop=True),
                            reads=[("B_fm", g_i), ("C_fm", g_i)], writes=[BK[6]],
                            tag=f"gt{c}_{g_i}_{jt}")
                        p.add("vector", lambda v, jt=jt, g2=g2: v.tensor_tensor(
                            gmt[g2][:, jt * 256:(jt + 1) * 256], bank[6][:, 0:256],
                            tri_s[:, jt * 256:(jt + 1) * 256], ALU.mult),
                            reads=[BK[6], "tri_s"], writes=[("gmt", g2, jt)],
                            tag=f"ge{c}_{g_i}_{jt}")
                    p.add("gpsimd", lambda g, c=c, g_i=g_i: g.dma_start(
                        out=arow[0:1, :].rearrange("p (h t) -> p h t", h=16),
                        in_=acsht_d[g_i * 16:(g_i + 1) * 16,
                                    c * CS:(c + 1) * CS].unsqueeze(0)),
                        reads=["acsht_d"], writes=["arow"], dma=True,
                        tag=f"ar{c}_{g_i}")
                    for hh in range(16):
                        h = g_i * 16 + hh
                        slot = hh % 8
                        rbk = 3 + hh % 4
                        lj = hh % 2
                        p.add("tensor", lambda te, hh=hh, rbk=rbk: te.matmul(
                            bank[rbk][:, 0:256], ones_f[:, :],
                            arow[0:1, hh * 256:(hh + 1) * 256],
                            start=True, stop=True),
                            reads=["arow", "ones_f"], writes=[BK[rbk]], tag=f"rm{c}_{h}")
                        for jt in range(2):
                            p.add("scalar", lambda sc, c=c, h=h, jt=jt, rbk=rbk, lj=lj: sc.activation(
                                lt1[lj][:, jt * 256:jt * 256 + 256], bank[rbk][:, 0:256],
                                AF.Relu, scale=-1.0,
                                bias=acs_ti[:, (2 * c + jt) * 128 + h:(2 * c + jt) * 128 + h + 1]),
                                reads=[BK[rbk], ("acs_ti", 2 * c + jt)], writes=[("lt1", lj)],
                                tag=f"lr{c}_{h}_{jt}")
                            p.add("scalar", lambda sc, jt=jt, lj=lj: sc.activation(
                                lt1[lj][:, jt * 256:jt * 256 + 256],
                                lt1[lj][:, jt * 256:jt * 256 + 256], AF.Exp, scale=-1.0),
                                reads=[("lt1", lj)], writes=[("lt1", lj)], tag=f"le{c}_{h}_{jt}")
                            p.add("vector", lambda v, jt=jt, lj=lj, g2=g2: v.tensor_tensor(
                                ltg[lj][:, jt * 256:(jt + 1) * 256],
                                lt1[lj][:, jt * 256:jt * 256 + 256],
                                gmt[g2][:, jt * 256:(jt + 1) * 256], ALU.mult),
                                reads=[("lt1", lj), ("gmt", g2, jt)], writes=[("ltg", lj)],
                                tag=f"lg{c}_{h}_{jt}")
                        for h2 in range(2):
                            p.add("vector", lambda v, h2=h2, hh=hh, c=c, h=h, lj=lj, g2=g2: v.tensor_scalar(
                                xs1[lj][:, h2 * 64:(h2 + 1) * 64],
                                xstage[g2 * 2 + h2][:, hh * 64:(hh + 1) * 64],
                                dt_ti[:, (2 * c + h2) * 128 + h:(2 * c + h2) * 128 + h + 1],
                                0.0, ALU.mult, ALU.add),
                                reads=[("xstage", g2, h2), ("dt_ti", 2 * c + h2)],
                                writes=[("xs1", lj)], tag=f"x1_{c}_{h}_{h2}")
                            p.add("vector", lambda v, h2=h2, hh=hh, c=c, h=h, lj=lj, g2=g2: v.tensor_scalar(
                                xs2[lj][:, h2 * 64:(h2 + 1) * 64],
                                xstage[g2 * 2 + h2][:, hh * 64:(hh + 1) * 64],
                                dtds_ti[:, (2 * c + h2) * 128 + h:(2 * c + h2) * 128 + h + 1],
                                0.0, ALU.mult, ALU.add),
                                reads=[("xstage", g2, h2), ("dtds_ti", 2 * c + h2)],
                                writes=[("xs2", lj)], tag=f"x2_{c}_{h}_{h2}")
                        for it in range(2):
                            bi = it
                            for jt in range(2):
                                p.add("tensor", lambda te, it=it, jt=jt, bi=bi, slot=slot, lj=lj: te.matmul(
                                    bank[bi][:, slot * 64:(slot + 1) * 64],
                                    ltg[lj][:, jt * 256 + it * 128: jt * 256 + (it + 1) * 128],
                                    xs1[lj][:, jt * 64:(jt + 1) * 64],
                                    start=(slot == 0 and jt == 0),
                                    stop=(slot == 7 and jt == 1)),
                                    reads=[("ltg", lj), ("xs1", lj)], writes=[BK[bi]],
                                    tag=f"yd{c}_{h}_{it}_{jt}")
                        for h2 in range(2):
                            p.add("tensor", lambda te, h2=h2, slot=slot, lj=lj, g2=g2: te.matmul(
                                bank[2][:, slot * 64:(slot + 1) * 64],
                                btm[g2][:, h2 * 128:(h2 + 1) * 128],
                                xs2[lj][:, h2 * 64:(h2 + 1) * 64],
                                start=(slot == 0 and h2 == 0),
                                stop=(slot == 7 and h2 == 1)),
                                reads=[("btm", g2, h2), ("xs2", lj)], writes=[BK[2]],
                                tag=f"sm{c}_{h}_{h2}")
                        if slot == 7:
                            hb = hh // 8
                            h0 = g_i * 16 + hb * 8
                            p.add("vector", lambda v, h0=h0: v.tensor_copy(
                                slocal[:, h0 * 64:(h0 + 8) * 64], bank[2][:, :]),
                                reads=[BK[2]], writes=[("slocal", h0)],
                                tag=f"se{c}_{h0}")
                            for it in range(2):
                                bi2 = it
                                p.add("vector", lambda v, it=it, hb=hb, g2=g2: v.tensor_tensor(
                                    dxl[:, 0:512],
                                    xstage[g2 * 2 + it][:, hb * 512:(hb + 1) * 512],
                                    dex_t[g2][:, hb * 512:(hb + 1) * 512], ALU.mult),
                                    reads=[("xstage", g2, it), ("dex_t", g2)], writes=["dxl"],
                                    tag=f"dx{c}_{h0}_{it}")
                                p.add("vector", lambda v, bi2=bi2: v.tensor_tensor(
                                    piece[:, 0:512], bank[bi2][:, :], dxl[:, 0:512],
                                    ALU.add),
                                    reads=[BK[bi2], "dxl"], writes=["piece"],
                                    tag=f"yp{c}_{h0}_{it}")
                                p.add("gpsimd", lambda g, c=c, it=it, h0=h0: g.dma_start(
                                    out=y_part[c * CS + it * 128: c * CS + (it + 1) * 128,
                                               h0 * 64:(h0 + 8) * 64],
                                    in_=piece[:, 0:512]),
                                    reads=["piece"], writes=[("yp_o", c, it, h0)],
                                    dma=True, tag=f"ys{c}_{h0}_{it}")
                ex_view = expt_bc[:, c * 128:(c + 1) * 128].unsqueeze(-1).broadcast_to([128, 128, 64])
                sr_view = srun_s[:, :].rearrange("p (h d) -> p h d", d=64)
                p.add("vector", lambda v, ex_view=ex_view, sr_view=sr_view: v.tensor_tensor(
                    sr_view, sr_view, ex_view, ALU.mult),
                    reads=["srun_s", ("expt_bc", c)], writes=["srun_s"], tag=f"su{c}")
                p.add("vector", lambda v: v.tensor_tensor(
                    srun_s[:, :], srun_s[:, :], slocal[:, :], ALU.add),
                    reads=["srun_s", "slocal"] + [("slocal", g2 * 16) for g2 in range(G)] + [("slocal", g2 * 16 + 8) for g2 in range(G)],
                    writes=["srun_s"], tag=f"sa{c}")
            p.add("gpsimd", lambda g: g.dma_start(out=s_out[:, :], in_=srun_s[:, :]),
                  reads=["srun_s"], writes=["s_o"], dma=True, tag="st_send")

        p.emit(nc)
    return nc




def build_L3():
    nc = bass.Bass("TRN2", target_bir_lowering=False, debug=False)
    p = Prog()

    x2_in = nc.dram_tensor("x2_in", [T_OWN, DM], F32, kind="ExternalInput").ap()
    ypart = nc.dram_tensor("ypart", [T_OWN, DIN], F32, kind="ExternalInput").ap()
    gate = nc.dram_tensor("gate", [T_OWN, DIN], F32, kind="ExternalInput").ap()
    cfm_in = nc.dram_tensor("cfm_in", [G * NST, T_OWN], BF16, kind="ExternalInput").ap()
    acs_in = nc.dram_tensor("acs_in", [T_OWN, NH], F32, kind="ExternalInput").ap()
    sall = nc.dram_tensor("sall", [NH, 8 * NST * HD], F32, kind="ExternalInput").ap()
    sloc_h = nc.dram_tensor("sloc_h", [NH, 3 * NST * HD], F32, kind="ExternalInput").ap()
    townT = nc.dram_tensor("townT", [NH, NCH], F32, kind="ExternalInput").ap()
    tallT = nc.dram_tensor("tallT", [NH, 8 * NCH], F32, kind="ExternalInput").ap()
    m_lt = nc.dram_tensor("m_lt", [128, 8], F32, kind="ExternalInput").ap()
    mexp = nc.dram_tensor("mexp", [128, 64], F32, kind="ExternalInput").ap()
    wffn = nc.dram_tensor("wffn", [DM, DM], BF16, kind="ExternalInput").ap()
    fbias = nc.dram_tensor("fbias", [1, DM], F32, kind="ExternalInput").ap()
    wop = nc.dram_tensor("wop", [DIN, DM], BF16, kind="ExternalInput").ap()
    norw_b = nc.dram_tensor("norw_b", [128, DIN], F32, kind="ExternalInput").ap()

    out = nc.dram_tensor("out", [T_OWN, DM], F32, kind="ExternalOutput").ap()

    sinh_d = nc.dram_tensor("sinh_d", [NCH * NH, NST * HD], F32).ap()
    ynorm_d = nc.dram_tensor("ynorm_d", [T_OWN, DIN], BF16).ap()
    x3_d = nc.dram_tensor("x3_d", [T_OWN, DM], F32).ap()
    xn4_d = nc.dram_tensor("xn4_d", [T_OWN, DM], BF16).ap()

    with contextlib.ExitStack() as stack:
        ctx = Ctx(nc, p, stack)
        bank = [ctx.psum(f"bank{i}", [128, 512], F32) for i in range(7)]
        bkt = ctx.psum("bankT", [128, 1024], BF16)
        BK = [("bank", i) for i in range(7)]
        ones_b = ctx.sbuf("ones_b", [1, 512], BF16)
        p.add("vector", lambda v: v.memset(ones_b[:, :], 1.0), writes=["ones_b"], tag="onesb")
        ev_f = [ctx.sbuf(f"ev_f{j}", [128, 512], F32) for j in range(4)]

        with ctx.scope():
            tt_s = ctx.sbuf("tt_s", [128, 8 * NCH], F32)
            to_s = ctx.sbuf("to_s", [128, NCH], F32)
            mlt_s = ctx.sbuf("mlt_s", [128, 8], F32)
            mex_s = ctx.sbuf("mex_s", [128, 64], F32)
            ttot = ctx.sbuf("ttot", [128, 8], F32)
            tmp8 = ctx.sbuf("tmp8", [128, 8], F32)
            wl = ctx.sbuf("wl", [128, 8], F32)
            wfac = ctx.sbuf("wfac", [128, 8], F32)
            pic = ctx.sbuf("pic", [128, NCH], F32)
            cum = ctx.sbuf("cum", [128, NCH], F32)
            zz = ctx.sbuf("zz", [128, NCH], F32)
            sacc = [ctx.sbuf(f"sacc{j}", [128, DIN], F32) for j in range(2)]
            sld = [ctx.sbuf(f"sld{j}", [128, DIN], F32) for j in range(2)]
            p.add("sync", lambda sy: sy.dma_start(out=tt_s[:, :], in_=tallT[:, :]),
                  writes=["tt_s"], dma=True, tag="ld_tt")
            p.add("sync", lambda sy: sy.dma_start(out=to_s[:, :], in_=townT[:, :]),
                  writes=["to_s"], dma=True, tag="ld_to")
            p.add("sync", lambda sy: sy.dma_start(out=mlt_s[:, :], in_=m_lt[:, :]),
                  writes=["mlt_s"], dma=True, tag="ld_mlt")
            p.add("sync", lambda sy: sy.dma_start(out=mex_s[:, :], in_=mexp[:, :]),
                  writes=["mex_s"], dma=True, tag="ld_mex")
            ttv = tt_s[:, :].rearrange("p (j c) -> p j c", c=NCH)
            p.add("vector", lambda v: v.tensor_tensor(
                ttot[:, :], ttv[:, :, 0], ttv[:, :, 1], ALU.add),
                reads=["tt_s"], writes=["ttot"], tag="tt0")
            p.add("vector", lambda v: v.tensor_tensor(
                ttot[:, :], ttot[:, :], ttv[:, :, 2], ALU.add),
                reads=["tt_s", "ttot"], writes=["ttot"], tag="tt1")
            p.add("vector", lambda v: v.tensor_tensor(
                ttot[:, :], ttot[:, :], ttv[:, :, 3], ALU.add),
                reads=["tt_s", "ttot"], writes=["ttot"], tag="tt2")
            for j in range(8):
                p.add("vector", lambda v, j=j: v.tensor_tensor(
                    tmp8[:, :], ttot[:, :], mex_s[:, j * 8:(j + 1) * 8], ALU.mult),
                    reads=["ttot", "mex_s"], writes=["tmp8"], tag=f"wm{j}")
                p.add("vector", lambda v, j=j: v.tensor_reduce(
                    wl[:, j:j + 1], tmp8[:, :], AX.X, ALU.add),
                    reads=["tmp8"], writes=["wl"], tag=f"ws{j}")
            p.add("scalar", lambda sc: sc.activation(wfac[:, :], wl[:, :], AF.Exp),
                  reads=["wl"], writes=["wfac"], tag="we")
            p.add("vector", lambda v: v.tensor_tensor(
                wfac[:, :], wfac[:, :], mlt_s[:, :], ALU.mult),
                reads=["wfac", "mlt_s"], writes=["wfac"], tag="wmk")
            for j in range(8):
                sj = j % 2
                p.add("sync", lambda sy, j=j, sj=sj: sy.dma_start(
                    out=sld[sj][:, :], in_=sall[:, j * DIN:(j + 1) * DIN]),
                    writes=[("sld", sj)], dma=True, tag=f"lds{j}")
                if j == 0:
                    p.add("vector", lambda v, sj=sj: v.tensor_scalar(
                        sacc[0][:, :], sld[sj][:, :], wfac[:, 0:1], 0.0, ALU.mult, ALU.add),
                        reads=[("sld", sj), "wfac"], writes=[("sacc", 0)], tag="sg0")
                else:
                    p.add("vector", lambda v, j=j, sj=sj: v.scalar_tensor_tensor(
                        sacc[j % 2][:, :], sld[sj][:, :], wfac[:, j:j + 1],
                        sacc[(j - 1) % 2][:, :], ALU.mult, ALU.add),
                        reads=[("sld", sj), "wfac", ("sacc", (j - 1) % 2)],
                        writes=[("sacc", j % 2)], tag=f"sg{j}")
            p.add("vector", lambda v: v.memset(zz[:, :], 0.0), writes=["zz"], tag="zz")
            p.add("vector", lambda v: v.tensor_tensor_scan(
                cum[:, :], to_s[:, :], zz[:, :], 0.0, ALU.add, ALU.add),
                reads=["to_s", "zz"], writes=["cum"], tag="cumt")
            p.add("vector", lambda v: v.memset(pic[:, 0:1], 1.0), writes=["pic"], tag="pic0")
            p.add("scalar", lambda sc: sc.activation(
                pic[:, 1:4], cum[:, 0:3], AF.Exp),
                reads=["cum"], writes=["pic"], tag="pice")
            for c in range(NCH):
                sj = c % 2
                p.add("vector", lambda v, c=c, sj=sj: v.tensor_scalar(
                    sld[sj][:, :], sacc[1][:, :], pic[:, c:c + 1], 0.0, ALU.mult, ALU.add),
                    reads=[("sacc", 1), "pic"], writes=[("sld", sj)], tag=f"si{c}")
                if c > 0:
                    p.add("vector", lambda v, c=c: v.tensor_tensor(
                        sld[:, :], sld[:, :],
                        sloc_h[0:0, 0:0] if False else sld[:, :], ALU.add),
                        reads=["sld"], writes=["sld"], tag=f"dummy{c}") if False else None
                    p.add("sync", lambda sy, c=c: sy.dma_start(
                        out=sacc[0][:, :], in_=sloc_h[:, (c - 1) * DIN:c * DIN]),
                        writes=[("sacc", 0)], dma=True, tag=f"ldsl{c}")
                    p.add("vector", lambda v, sj=sj: v.tensor_tensor(
                        sld[sj][:, :], sld[sj][:, :], sacc[0][:, :], ALU.add),
                        reads=[("sld", sj), ("sacc", 0)], writes=[("sld", sj)], tag=f"sia{c}")
                p.add("gpsimd", lambda g, c=c, sj=sj: g.dma_start(
                    out=sinh_d[c * NH:(c + 1) * NH, :], in_=sld[sj][:, :]),
                    reads=[("sld", sj)], writes=[("sinh", c)], dma=True, tag=f"ssi{c}")

        with ctx.scope():
            cfm = ctx.sbuf("cfm", [128, 8 * T_OWN], BF16)
            for gi in range(8):
                p.add("sync", lambda sy, gi=gi: sy.dma_start(
                    out=cfm[:, gi * T_OWN:(gi + 1) * T_OWN],
                    in_=cfm_in[gi * 128:(gi + 1) * 128, :]),
                    writes=[("cfm", gi)], dma=True, tag=f"ldc{gi}")
            eacs = ctx.sbuf("eacs", [128, 8 * 128], F32)
            acst = ctx.sbuf("acst", [128, 8 * 128], F32)
            for tt in range(8):
                p.add("sync", lambda sy, tt=tt: sy.dma_start(
                    out=acst[:, tt * 128:(tt + 1) * 128],
                    in_=acs_in[tt * 128:(tt + 1) * 128, :]),
                    writes=[("acst", tt)], dma=True, tag=f"lda{tt}")
            p.add("scalar", lambda sc: sc.activation(eacs[:, :], acst[:, :], AF.Exp),
                  reads=[("acst", tt) for tt in range(8)], writes=["eacs"], tag="eacs")
            norw_s = ctx.sbuf("norw_s", [128, DIN], F32)
            p.add("sync", lambda sy: sy.dma_start(out=norw_s[:, :], in_=norw_b[:, :]),
                  writes=["norw_s"], dma=True, tag="ld_nw")
            sraw = [ctx.sbuf(f"sraw{j}", [128, 512], F32) for j in range(2)]
            sbf = [ctx.sbuf(f"sbf{j}", [128, 512], BF16) for j in range(2)]
            ybuf = [ctx.sbuf(f"ybuf{it}", [128, DIN], F32) for it in range(2)]
            gbuf = ctx.sbuf("gbuf", [128, DIN], F32)
            yrow = [ctx.sbuf(f"yrow{j}", [128, 512], F32) for j in range(4)]
            t1 = [ctx.sbuf(f"t1_{j}", [128, 512], F32) for j in range(4)]
            vcol = ctx.sbuf("vcol", [128, 8], F32)
            sqs = ctx.sbuf("sqs", [128, GSZ], F32)
            ynb = ctx.sbuf("ynb", [128, DIN], BF16)
            for c in range(NCH):
                for hs in range(16):
                    j = hs % 2
                    p.add("gpsimd", lambda g, c=c, hs=hs, j=j: g.dma_start(
                        out=sraw[j][:, :].rearrange("p (h d) -> p h d", h=8),
                        in_=sinh_d[c * NH + hs * 8:c * NH + (hs + 1) * 8, :].rearrange(
                            "h (n d) -> n h d", n=NST)),
                        reads=[("sinh", c)], writes=[("sraw", j)], dma=True,
                        tag=f"lsi{c}_{hs}")
                    p.add("vector", lambda v, j=j: v.tensor_copy(sbf[j][:, :], sraw[j][:, :]),
                          reads=[("sraw", j)], writes=[("sbf", j)], tag=f"cbf{c}_{hs}")
                    for it in range(2):
                        bi = (hs % 2) * 2 + it
                        for slot in range(8):
                            h = hs * 8 + slot
                            gi = h // 16
                            p.add("tensor", lambda te, c=c, it=it, gi=gi, slot=slot, bi=bi, j=j: te.matmul(
                                bank[bi][:, slot * 64:(slot + 1) * 64],
                                cfm[:, gi * T_OWN + c * CS + it * 128: gi * T_OWN + c * CS + (it + 1) * 128],
                                sbf[j][:, slot * 64:(slot + 1) * 64],
                                start=(slot == 0), stop=(slot == 7)),
                                reads=[("cfm", gi), ("sbf", j)], writes=[BK[bi]],
                                tag=f"yo{c}_{it}_{hs}_{slot}")
                        jr = (hs % 2) * 2 + it
                        p.add("scalar", lambda sc, c=c, it=it, hs=hs, jr=jr: sc.dma_start(
                            out=yrow[jr][:, :],
                            in_=ypart[c * CS + it * 128: c * CS + (it + 1) * 128,
                                      hs * 512:(hs + 1) * 512]),
                            writes=[("yrow", jr)], dma=True, tag=f"lyp{c}_{it}_{hs}")
                        exv = eacs[:, (2 * c + it) * 128 + hs * 8:(2 * c + it) * 128 + (hs + 1) * 8] \
                            .unsqueeze(-1).broadcast_to([128, 8, 64])
                        p.add("vector", lambda v, bi=bi, exv=exv, jr=jr: v.tensor_tensor(
                            t1[jr][:, :].rearrange("p (a b) -> p a b", a=8),
                            bank[bi][:, :].rearrange("p (a b) -> p a b", a=8),
                            exv, ALU.mult),
                            reads=[BK[bi], "eacs"], writes=[("t1", jr)], tag=f"ysc{c}_{it}_{hs}")
                        p.add("vector", lambda v, hs=hs, it=it, jr=jr: v.tensor_tensor(
                            ybuf[it][:, hs * 512:(hs + 1) * 512], t1[jr][:, :],
                            yrow[jr][:, :], ALU.add),
                            reads=[("t1", jr), ("yrow", jr)],
                            writes=[("ybuf", it, hs), ("ybufE", it)],
                            tag=f"yad{c}_{it}_{hs}")
                for it in range(2):
                    YB = [("ybuf", it, hs) for hs in range(16)] + [("ybufE", it)]
                    p.add("gpsimd", lambda g, c=c, it=it: g.dma_start(
                        out=gbuf[:, :],
                        in_=gate[c * CS + it * 128: c * CS + (it + 1) * 128, :]),
                        writes=["gbuf"], dma=True, tag=f"lg{c}_{it}")
                    p.add("vector", lambda v, it=it: v.tensor_tensor(
                        ybuf[it][:, :], ybuf[it][:, :], gbuf[:, :], ALU.mult),
                        reads=YB + ["gbuf"], writes=YB, tag=f"ygm{c}_{it}")
                    for gi in range(8):
                        p.add("scalar", lambda sc, gi=gi, it=it: sc.activation(
                            sqs[:, :], ybuf[it][:, gi * GSZ:(gi + 1) * GSZ], AF.Square,
                            accum_out=vcol[:, gi:gi + 1]),
                            reads=YB, writes=["sqs", "vcol"], tag=f"ysq{c}_{it}_{gi}")
                    p.add("vector", lambda v: v.tensor_scalar(
                        vcol[:, :], vcol[:, :], 1.0 / GSZ, EPS, ALU.mult, ALU.add),
                        reads=["vcol"], writes=["vcol"], tag=f"yv{c}_{it}")
                    p.add("scalar", lambda sc: sc.activation(vcol[:, :], vcol[:, :], AF.Sqrt),
                          reads=["vcol"], writes=["vcol"], tag=f"yvs{c}_{it}")
                    p.add("vector", lambda v: v.reciprocal(vcol[:, :], vcol[:, :]),
                          reads=["vcol"], writes=["vcol"], tag=f"yvr{c}_{it}")
                    for gi in range(8):
                        p.add("vector", lambda v, gi=gi, it=it: v.tensor_scalar(
                            ybuf[it][:, gi * GSZ:(gi + 1) * GSZ],
                            ybuf[it][:, gi * GSZ:(gi + 1) * GSZ],
                            vcol[:, gi:gi + 1], 0.0, ALU.mult, ALU.add),
                            reads=YB + ["vcol"], writes=YB, tag=f"yn{c}_{it}_{gi}")
                    p.add("vector", lambda v, it=it: v.tensor_tensor(
                        ynb[:, :], ybuf[it][:, :], norw_s[:, :], ALU.mult),
                        reads=YB + ["norw_s"], writes=["ynb"], tag=f"ynw{c}_{it}")
                    p.add("gpsimd", lambda g, c=c, it=it: g.dma_start(
                        out=ynorm_d[c * CS + it * 128: c * CS + (it + 1) * 128, :],
                        in_=ynb[:, :]),
                        reads=["ynb"], writes=[("ynd", 2 * c + it)], dma=True,
                        tag=f"syn{c}_{it}")
        YNK = [("ynd", i) for i in range(8)]

        with ctx.scope():
            yfm = ctx.sbuf("yfm", [128, 64 * 512], BF16)
            wcol = [ctx.sbuf(f"wcol{j}", [128, DM], BF16) for j in range(4)]
            opacc = [ctx.sbuf(f"opacc{t}", [128, DM], F32) for t in range(4)]
            x2_t = [ctx.sbuf(f"x2_t{j}", [128, DM], F32) for j in range(1)]
            xo_t = [ctx.sbuf(f"xo_t{j}", [128, DM], F32) for j in range(1)]
            for tg0 in (0, 4):
                for ch in range(64):
                    p.add("sync", lambda sc, ch=ch, tg0=tg0: sc.dma_start(
                        out=yfm[:, ch * 512:(ch + 1) * 512],
                        in_=ynorm_d[tg0 * 128:(tg0 + 4) * 128, ch * 128:(ch + 1) * 128],
                        transpose=True),
                        reads=YNK, writes=[("yfm", ch)], dma=True, tag=f"tyf{tg0}_{ch}")
                for cg in range(16):
                    for cc in range(4):
                        ch = cg * 4 + cc
                        p.add("sync", lambda sy, ch=ch, cc=cc: sy.dma_start(
                            out=wcol[cc][:, :],
                            in_=wop[ch * 128:(ch + 1) * 128, :]),
                            writes=[("wcol", cc)], dma=True, tag=f"two{tg0}_{cg}_{cc}")
                    for bi in range(4):
                        for f5 in range(8):
                            for cc in range(4):
                                ch = cg * 4 + cc
                                p.add("tensor", lambda te, ch=ch, cc=cc, bi=bi, f5=f5: te.matmul(
                                    bank[bi][:, :],
                                    yfm[:, ch * 512 + bi * 128: ch * 512 + (bi + 1) * 128],
                                    wcol[cc][:, f5 * 512:(f5 + 1) * 512],
                                    start=(cc == 0), stop=(cc == 3)),
                                    reads=[("wcol", cc), ("yfm", ch)], writes=[BK[bi]],
                                    tag=f"om{tg0}_{cg}_{bi}_{f5}_{cc}")
                            if cg == 0:
                                p.add("scalar", lambda sc, bi=bi, f5=f5: sc.activation(
                                    opacc[bi][:, f5 * 512:(f5 + 1) * 512],
                                    bank[bi][:, :], AF.Copy),
                                    reads=[BK[bi]], writes=[("opacc", bi)],
                                    tag=f"oe{tg0}_{cg}_{bi}_{f5}")
                            else:
                                p.add("vector", lambda v, bi=bi, f5=f5: v.tensor_tensor(
                                    opacc[bi][:, f5 * 512:(f5 + 1) * 512],
                                    bank[bi][:, :],
                                    opacc[bi][:, f5 * 512:(f5 + 1) * 512], ALU.add),
                                    reads=[BK[bi], ("opacc", bi)], writes=[("opacc", bi)],
                                    tag=f"oa{tg0}_{cg}_{bi}_{f5}")
                for bi in range(4):
                    t = tg0 + bi
                    j = bi % 2
                    p.add("scalar", lambda sc, t=t: sc.dma_start(
                        out=x2_t[0][:, :], in_=x2_in[t * 128:(t + 1) * 128, :]),
                        writes=[("x2_t", 0)], dma=True, tag=f"lx2{tg0}_{t}")
                    p.add("vector", lambda v, bi=bi: v.tensor_tensor(
                        xo_t[0][:, :], opacc[bi][:, :], x2_t[0][:, :], ALU.add),
                        reads=[("opacc", bi), ("x2_t", 0)], writes=[("xo_t", 0)],
                        tag=f"or{tg0}_{bi}")
                    p.add("gpsimd", lambda g, t=t: g.dma_start(
                        out=x3_d[t * 128:(t + 1) * 128, :], in_=xo_t[0][:, :]),
                        reads=[("xo_t", 0)], writes=[("x3d", t)], dma=True,
                        tag=f"sx3{tg0}_{t}")
        X3K = [("x3d", t) for t in range(8)]

        with ctx.scope():
            emit_ln_stream(ctx, x3_d, xn4_d, "xn4d", T_OWN, "ln4", src_keys=X3K)
        XN4K = [("xn4d", i) for i in range(8)]
        with ctx.scope():
            xn4fm = ctx.sbuf("xn4fm", [128, 32 * T_OWN], BF16)
            for cc_i in range(32):
                p.add("sync", lambda sc, cc_i=cc_i: sc.dma_start(
                    out=xn4fm[:, cc_i * T_OWN:(cc_i + 1) * T_OWN],
                    in_=xn4_d[:, cc_i * 128:(cc_i + 1) * 128], transpose=True),
                    reads=XN4K, writes=[("xn4fm", cc_i)], dma=True, tag=f"t4_{cc_i}")
            fb_b = ctx.sbuf("fb_b", [1, DM], BF16)
            p.add("gpsimd", lambda g: g.dma_start(out=fb_b[:, :], in_=fbias[:, :]),
                  writes=["fb_b"], dma=True, tag="ld_fb")
            wf_t = [ctx.sbuf(f"wf_t{j}", [128, 512], BF16) for j in range(8)]
            xr_t = [ctx.sbuf(f"xr_t{j}", [128, 512], F32) for j in range(4)]
            for tg in (range(0, 4), range(4, 8)):
                for og in range(8):
                    for cc_i in range(32):
                        j = cc_i % 8
                        p.add("sync", lambda sy, og=og, cc_i=cc_i, j=j: sy.dma_start(
                            out=wf_t[j][:, :],
                            in_=wffn[cc_i * 128:(cc_i + 1) * 128, og * 512:(og + 1) * 512]),
                            writes=[("f2wf", j)], dma=True, tag=f"f2w{tg[0]}_{og}_{cc_i}")
                        for bi, qt in enumerate(tg):
                            p.add("tensor", lambda te, cc_i=cc_i, j=j, bi=bi, qt=qt: te.matmul(
                                bank[bi][:, :],
                                xn4fm[:, cc_i * T_OWN + qt * 128: cc_i * T_OWN + (qt + 1) * 128],
                                wf_t[j][:, :], start=(cc_i == 0), stop=False),
                                reads=[("f2wf", j), ("xn4fm", cc_i)], writes=[BK[bi]],
                                tag=f"f2m{og}_{qt}_{cc_i}")
                    for bi, qt in enumerate(tg):
                        p.add("tensor", lambda te, bi=bi, og=og: te.matmul(
                            bank[bi][:, :], ones_b[:, 0:128],
                            fb_b[:, og * 512:(og + 1) * 512], start=False, stop=True),
                            reads=["ones_b", "fb_b"], writes=[BK[bi]], tag=f"f2b{og}_{qt}")
                        j = bi % 2
                        p.add("scalar", lambda sc, qt=qt, og=og, j=j: sc.dma_start(
                            out=xr_t[j][:, :],
                            in_=x3_d[qt * 128:(qt + 1) * 128, og * 512:(og + 1) * 512]),
                            reads=X3K, writes=[("f2x", j)], dma=True, tag=f"f2l{og}_{qt}")
                        p.add("vector", lambda v, bi=bi, j=j: v.tensor_tensor(
                            ev_f[j][:, :], bank[bi][:, :], xr_t[j][:, :], ALU.add),
                            reads=[BK[bi], ("f2x", j)], writes=[("ev_f", j)],
                            tag=f"f2r{og}_{qt}")
                        p.add("gpsimd", lambda g, qt=qt, og=og, j=j: g.dma_start(
                            out=out[qt * 128:(qt + 1) * 128, og * 512:(og + 1) * 512],
                            in_=ev_f[j][:, :]),
                            reads=[("ev_f", j)], writes=[("out", qt, og)], dma=True,
                            tag=f"f2s{og}_{qt}")
        p.emit(nc)
    return nc




def kernel(**inputs):
    x = np.asarray(inputs["L0_attn_a_x"])
    Wqkv = np.asarray(inputs["L0_attn_a_Wqkv"])
    bqkv = np.asarray(inputs["L0_attn_a_bqkv"])
    Wo = np.asarray(inputs["L0_attn_a_Wo"])
    bo = np.asarray(inputs["L0_attn_a_bo"])
    inw = np.asarray(inputs["L0_attn_b_in_proj_w"])
    convw = np.asarray(inputs["L0_attn_b_conv_w"])
    convb = np.asarray(inputs["L0_attn_b_conv_b"])
    dtb = np.asarray(inputs["L0_attn_b_dt_bias"])
    alog = np.asarray(inputs["L0_attn_b_A_log"])
    Dvec = np.asarray(inputs["L0_attn_b_D"])
    normw = np.asarray(inputs["L0_attn_b_norm_w"])
    outw = np.asarray(inputs["L0_attn_b_out_proj_w"])
    q3 = np.asarray(inputs["L0_ffn_weight_q3"])
    wnorm = np.asarray(inputs["L0_ffn_weight_norm"])
    fbias = np.asarray(inputs["L0_ffn_bias"])

    x2d = np.ascontiguousarray(x[0])

    nc1 = build_L1()
    in_maps = []
    for i in range(NCORES):
        sl = slice(i * T_OWN, (i + 1) * T_OWN)
        in_maps.append({
            "x0": np.ascontiguousarray(x2d[sl]),
            "wqkv": Wqkv,
            "bqkv": np.ascontiguousarray(bqkv.reshape(1, -1)),
            "q3": np.ascontiguousarray(
                q3.view(np.uint8).reshape(512 * 128 * NCORES, 12)[
                    i * 512 * 128:(i + 1) * 512 * 128]),
            "wnorm": np.ascontiguousarray(
                wnorm.reshape(-1, 1)[i * 512 * 128:(i + 1) * 512 * 128]),
            "inproj_f32": np.ascontiguousarray(
                inw[i * (PROJ // 8):(i + 1) * (PROJ // 8)]),
            "outproj_f32": np.ascontiguousarray(
                outw[i * (DM // 8):(i + 1) * (DM // 8)]),
            "id_bf": np.eye(128).astype(BF),
        })
    res1 = run_bass_kernel_spmd(nc1, in_maps, core_ids=list(range(NCORES)))
    r1 = res1.results
    Q_fm = np.concatenate([r1[i]["q_out"] for i in range(NCORES)], axis=1)
    K_fm = np.concatenate([r1[i]["k_out"] for i in range(NCORES)], axis=1)
    V_tm = np.concatenate([r1[i]["v_out"] for i in range(NCORES)], axis=0)
    w_ffn = np.concatenate([r1[i]["wffnt_out"] for i in range(NCORES)], axis=1)
    inproj_b = np.concatenate([r1[i]["ipt_out"] for i in range(NCORES)], axis=1)
    outproj_b = np.concatenate([r1[i]["opt_out"] for i in range(NCORES)], axis=1)

    nc2 = build_L2()
    trim0 = np.triu(np.ones((128, 256), np.float32))
    trim1 = np.triu(np.ones((128, 256), np.float32), k=128)
    identb_np = np.eye(128).astype(BF)
    identf_np = np.eye(128).astype(np.float32)
    convw_rr = np.ascontiguousarray(
        convw[:, 0, :].reshape(80, 128, 4).transpose(1, 0, 2).reshape(128, 320))
    convb_rr = np.ascontiguousarray(convb.reshape(80, 128).T)
    dexp_np = np.ascontiguousarray(
        np.broadcast_to(np.repeat(Dvec, HD)[None, :], (128, DIN))).astype(np.float32)
    in_maps2 = []
    for i in range(NCORES):
        s = i * T_OWN
        x0ext = np.zeros((T_EXT, DM), np.float32)
        if i > 0:
            x0ext[0:HALO] = x2d[s - HALO:s]
        x0ext[HALO:] = x2d[s:s + T_OWN]
        qx = np.zeros((DH, T_EXT), BF)
        if i > 0:
            qx[:, 0:HALO] = Q_fm[:, s - HALO:s]
        qx[:, HALO:] = Q_fm[:, s:s + T_OWN]
        hmask = np.ones((128, 4), np.float32)
        if i == 0:
            hmask[:, 0:3] = 0.0
        in_maps2.append({
            "x0e": x0ext,
            "q_ext": np.ascontiguousarray(qx),
            "k_fm": K_fm, "v_tm": V_tm,
            "wo": Wo, "bo": np.ascontiguousarray(bo.reshape(1, -1)),
            "wffn": w_ffn, "fbias": np.ascontiguousarray(fbias.reshape(1, -1)),
            "ipbf": inproj_b,
            "convw_r": convw_rr, "convb_r": convb_rr,
            "dtb_col": np.ascontiguousarray(dtb.reshape(128, 1)),
            "alog_col": np.ascontiguousarray(alog.reshape(128, 1)),
            "d_exp": dexp_np,
            "halo_m": hmask,
            "tri0": trim0, "tri1": trim1,
            "id_bf": identb_np, "id_f": identf_np,
        })
    res2 = run_bass_kernel_spmd(nc2, in_maps2, core_ids=list(range(NCORES)))
    r2 = res2.results

    nc3 = build_L3()
    s_all_h = np.stack([r2[j]["s_out"].reshape(NST, NH, HD).transpose(1, 0, 2).reshape(NH, NST * HD)
                        for j in range(NCORES)], axis=1).reshape(NH, NCORES * NST * HD)
    t_all = np.concatenate([r2[j]["t_out"] for j in range(NCORES)], axis=1)
    norw_np = np.ascontiguousarray(
        np.broadcast_to(normw.reshape(1, DIN), (128, DIN))).astype(np.float32)
    in_maps3 = []
    for i in range(NCORES):
        slh = r2[i]["sloc_out"].reshape(3, NST, NH, HD).transpose(2, 0, 1, 3).reshape(NH, 3 * NST * HD)
        mlt = np.zeros((128, 8), np.float32)
        mlt[:, :i] = 1.0
        mex = np.zeros((8, 8), np.float32)
        for j in range(i):
            for k2 in range(j + 1, i):
                mex[j, k2] = 1.0
        in_maps3.append({
            "x2_in": np.ascontiguousarray(r2[i]["x2_out"][HALO:HALO + T_OWN]),
            "ypart": r2[i]["y_part"],
            "gate": r2[i]["gate_out"],
            "cfm_in": r2[i]["c_out"],
            "acs_in": r2[i]["acs_out"],
            "sall": np.ascontiguousarray(s_all_h),
            "sloc_h": np.ascontiguousarray(slh),
            "townT": np.ascontiguousarray(r2[i]["t_out"]),
            "tallT": np.ascontiguousarray(t_all),
            "m_lt": mlt,
            "mexp": np.ascontiguousarray(
                np.broadcast_to(mex.reshape(1, 64), (128, 64))).astype(np.float32),
            "wffn": w_ffn,
            "fbias": np.ascontiguousarray(fbias.reshape(1, -1)),
            "wop": outproj_b,
            "norw_b": norw_np,
        })
    res3 = run_bass_kernel_spmd(nc3, in_maps3, core_ids=list(range(NCORES)))
    r3 = res3.results
    out = np.concatenate([r3[i]["out"] for i in range(NCORES)], axis=0)
    return out.reshape(1, 8192, 4096)


_build_L1 = build_L1


# version 09
# speedup vs baseline: 1.1084x; 1.1084x over previous
import contextlib
import numpy as np
import ml_dtypes
import concourse.bass as bass
import concourse.mybir as mybir
from concourse.bass_utils import run_bass_kernel_spmd

F32 = mybir.dt.float32
BF16 = mybir.dt.bfloat16
I32 = mybir.dt.int32
U8 = mybir.dt.uint8
AF = mybir.ActivationFunctionType
ALU = mybir.AluOpType
AX = mybir.AxisListType
BF = ml_dtypes.bfloat16

NCORES = 8
T_OWN = 1024
HALO = 3
T_EXT = T_OWN + HALO
DM = 4096
DH = 1024
NH = 128
HD = 64
NST = 128
G = 8
CS = 256
NCH = T_OWN // CS
DIN = NH * HD
CONV_DIM = DIN + 2 * G * NST
PROJ = DIN + CONV_DIM + NH
EPS = 1e-5
GSZ = DIN // G

ENGINES = ("tensor", "vector", "scalar", "gpsimd", "sync")
DMA_QUEUES = ("sync", "scalar", "gpsimd", "vector")


class Op:
    __slots__ = ("eng", "fn", "reads", "writes", "dma", "idx", "deps",
                 "need_inc", "inc_val", "tag", "lane")

    def __init__(self, eng, fn, reads, writes, dma, idx, tag):
        self.eng, self.fn = eng, fn
        self.reads, self.writes = reads, writes
        self.dma, self.idx, self.tag = dma, idx, tag
        self.deps = []
        self.need_inc = False
        self.inc_val = None
        self.lane = None


class Prog:

    def __init__(self):
        self.ops = []
        self.last_write = {}
        self.readers = {}

    def barrier(self):
        if not hasattr(self, "last_eng"):
            self.last_eng = {e: None for e in ENGINES}
        if not hasattr(self, "last_dma"):
            self.last_dma = {}
        last = dict(self.last_eng)
        for k, d in self.last_dma.items():
            if d is not None:
                last[("dma",) + k] = d
        nops = []
        for e in ENGINES:
            op = Op(e, lambda eng: eng.nop(), (), (), False, len(self.ops), f"bar_{e}")
            for d in last.values():
                if d is not None:
                    op.deps.append(d)
            self.ops.append(op)
            self.last_eng[e] = op
            nops.append(op)
        self.barrier_ops = {op.eng: op for op in nops}

    def add(self, eng, fn, reads=(), writes=(), dma=False, tag=""):
        op = Op(eng, fn, tuple(reads), tuple(writes), dma, len(self.ops), tag)
        if not hasattr(self, "last_eng"):
            self.last_eng = {e: None for e in ENGINES}
        if not hasattr(self, "last_dma"):
            self.last_dma = {}
        if not hasattr(self, "barrier_ops"):
            self.barrier_ops = {}
        if not hasattr(self, "qcnt"):
            self.qcnt = {e: 0 for e in ENGINES}
            self.lane_cnt = {}
        if dma:
            i = self.qcnt[eng]
            self.qcnt[eng] += 1
            op.lane = i % self.NLANES
            seq = self.lane_cnt.get((eng, op.lane), 0) + 1
            self.lane_cnt[(eng, op.lane)] = seq
            op.inc_val = 16 * seq
            op.need_inc = True
        deps = {}
        b = self.barrier_ops.get(eng)
        if b is not None:
            deps[b.idx] = b
        for k in op.reads:
            w = self.last_write.get(k)
            if w is not None:
                deps[w.idx] = w
        for k in op.writes:
            w = self.last_write.get(k)
            if w is not None:
                deps[w.idx] = w
            for r in self.readers.get(k, ()):
                deps[r.idx] = r
        for d in deps.values():
            if d.eng == "tensor" and op.eng == "tensor" and not d.dma and not op.dma:
                continue
            op.deps.append(d)
        for k in op.reads:
            self.readers.setdefault(k, []).append(op)
        for k in op.writes:
            self.last_write[k] = op
            self.readers[k] = []
        self.last_eng[eng] = op
        if dma:
            self.last_dma[(eng, op.lane)] = op
        self.ops.append(op)
        return op

    NLANES = 8

    def emit(self, nc):
        for op in self.ops:
            for d in op.deps:
                d.need_inc = True
        csem_count = {e: 0 for e in ENGINES}
        lane_cnt = getattr(self, "lane_cnt", {})
        for op in self.ops:
            if op.dma:
                pass
            elif op.need_inc:
                csem_count[op.eng] += 1
                op.inc_val = csem_count[op.eng]
        with contextlib.ExitStack() as stack:
            csem = {e: stack.enter_context(nc.semaphore(f"c_{e}")) for e in ENGINES}
            qsem = {(e, l): stack.enter_context(nc.semaphore(f"q_{e}{l}"))
                    for e in DMA_QUEUES for l in range(self.NLANES)}
            block = stack.enter_context(nc.Block())
            per_eng = {e: [] for e in ENGINES}
            for op in self.ops:
                per_eng[op.eng].append(op)

            def emit_engine(eng_name, eng):
                watermark = {}
                for op in per_eng[eng_name]:
                    needed = {}
                    for d in op.deps:
                        wk = (d.eng, d.lane if d.dma else None)
                        needed[wk] = max(needed.get(wk, 0), d.inc_val)
                    if op.dma and op.inc_val > 16:
                        wk = (op.eng, op.lane)
                        needed[wk] = max(needed.get(wk, 0), op.inc_val - 16)
                    for (deng, dlane), val in needed.items():
                        if watermark.get((deng, dlane), -1) >= val:
                            continue
                        sem = csem[deng] if dlane is None else qsem[(deng, dlane)]
                        eng.wait_ge(sem, val)
                        watermark[(deng, dlane)] = val
                    inst = op.fn(eng)
                    if op.need_inc:
                        if inst is None:
                            raise RuntimeError(f"op {op.tag}: fn returned None")
                        sem = qsem[(op.eng, op.lane)] if op.dma else csem[op.eng]
                        inst.then_inc(sem, 16 if op.dma else 1)
                if eng_name in DMA_QUEUES:
                    for l in range(self.NLANES):
                        tot = lane_cnt.get((eng_name, l), 0)
                        if tot and watermark.get((eng_name, l), -1) < 16 * tot:
                            eng.wait_ge(qsem[(eng_name, l)], 16 * tot)

            @block.tensor
            def _(te):
                emit_engine("tensor", te)

            @block.vector
            def _(v):
                emit_engine("vector", v)

            @block.scalar
            def _(sc):
                emit_engine("scalar", sc)

            @block.gpsimd
            def _(g):
                emit_engine("gpsimd", g)

            @block.sync
            def _(sy):
                emit_engine("sync", sy)


class Ctx:
    def __init__(self, nc, prog, stack):
        self.nc, self.p, self.stack = nc, prog, stack
        self._stacks = [stack]

    def sbuf(self, name, shape, dtype):
        return self._stacks[-1].enter_context(self.nc.sbuf_tensor(name, shape, dtype))

    def psum(self, name, shape, dtype):
        return self._stacks[-1].enter_context(self.nc.psum_tensor(name, shape, dtype))

    @contextlib.contextmanager
    def scope(self):
        st = contextlib.ExitStack()
        self._stacks.append(st)
        try:
            yield
        finally:
            self._stacks.pop()
            st.close()
            self.p.barrier()




def emit_ln_stream(ctx, src_rows_ap, dst_dram, dst_key, n_rows, tag,
                   also_keep=None, src_keys=None):
    p = ctx.p
    nt = (n_rows + 127) // 128
    xt = [ctx.sbuf(f"{tag}xt{j}", [128, DM], F32) for j in range(2)]
    sq = ctx.sbuf(f"{tag}sq", [128, DM], F32)
    cc = ctx.sbuf(f"{tag}cc", [128, 8], F32)
    xnb = [ctx.sbuf(f"{tag}xnb{j}", [128, DM], BF16) for j in range(2)]
    for i in range(nt):
        j = i % 2
        r = min(128, n_rows - i * 128)
        xk = (f"{tag}xt", j)
        rdeps = [] if src_keys is None else src_keys
        p.add("sync", lambda sy, i=i, j=j, r=r: sy.dma_start(
            out=xt[j][0:r, :], in_=src_rows_ap[i * 128:i * 128 + r, :]),
            reads=rdeps, writes=[xk], dma=True, tag=f"{tag}ld{i}")
        emit_ln_tile(ctx, xt[j], xk, r, xnb[j], (f"{tag}xnb", j), sq,
                     (f"{tag}sq",), cc, (f"{tag}cc",), f"{tag}{i}")
        p.add("gpsimd", lambda g, i=i, j=j, r=r: g.dma_start(
            out=dst_dram[i * 128:i * 128 + r, :], in_=xnb[j][0:r, :]),
            reads=[(f"{tag}xnb", j)], writes=[(dst_key, i)], dma=True,
            tag=f"{tag}st{i}")
        if also_keep is not None:
            also_keep(i, r, xt[j], xk)


def emit_ln_tile(ctx, xap, xkey, r, out_ap, out_key, sq, sqk, cc, cck, tag):
    p = ctx.p
    p.add("vector", lambda v: v.tensor_reduce(
        cc[0:r, 0:1], xap[0:r, :], AX.X, ALU.add),
        reads=[xkey], writes=[cck], tag=f"{tag}_mean")
    p.add("scalar", lambda sc: sc.activation(
        sq[0:r, :], xap[0:r, :], AF.Square, accum_out=cc[0:r, 1:2]),
        reads=[xkey], writes=[sqk, cck], tag=f"{tag}_sq")
    p.add("vector", lambda v: v.tensor_scalar(
        cc[0:r, 2:3], cc[0:r, 0:1], 1.0 / DM, 0.0, ALU.mult, ALU.add),
        reads=[cck], writes=[cck], tag=f"{tag}_m")
    p.add("vector", lambda v: v.tensor_tensor(
        cc[0:r, 3:4], cc[0:r, 2:3], cc[0:r, 2:3], ALU.mult),
        reads=[cck], writes=[cck], tag=f"{tag}_m2")
    p.add("vector", lambda v: v.tensor_scalar(
        cc[0:r, 4:5], cc[0:r, 1:2], 1.0 / DM, EPS, ALU.mult, ALU.add),
        reads=[cck], writes=[cck], tag=f"{tag}_ex2")
    p.add("vector", lambda v: v.tensor_tensor(
        cc[0:r, 5:6], cc[0:r, 4:5], cc[0:r, 3:4], ALU.subtract),
        reads=[cck], writes=[cck], tag=f"{tag}_var")
    p.add("scalar", lambda sc: sc.activation(
        cc[0:r, 6:7], cc[0:r, 5:6], AF.Sqrt),
        reads=[cck], writes=[cck], tag=f"{tag}_sqrt")
    p.add("vector", lambda v: v.reciprocal(
        cc[0:r, 7:8], cc[0:r, 6:7]),
        reads=[cck], writes=[cck], tag=f"{tag}_rstd")
    p.add("vector", lambda v: v.tensor_tensor(
        cc[0:r, 2:3], cc[0:r, 2:3], cc[0:r, 7:8], ALU.mult),
        reads=[cck], writes=[cck], tag=f"{tag}_mr")
    p.add("vector", lambda v: v.tensor_scalar(
        cc[0:r, 2:3], cc[0:r, 2:3], -1.0, 0.0, ALU.mult, ALU.add),
        reads=[cck], writes=[cck], tag=f"{tag}_neg")
    p.add("vector", lambda v: v.tensor_scalar(
        out_ap[0:r, :], xap[0:r, :], cc[0:r, 7:8], cc[0:r, 2:3],
        ALU.mult, ALU.add),
        reads=[xkey, cck], writes=[out_key], tag=f"{tag}_xn")




def build_L1():
    nc = bass.Bass("TRN2", target_bir_lowering=False, debug=False)
    p = Prog()

    x0 = nc.dram_tensor("x0", [T_OWN, DM], F32, kind="ExternalInput").ap()
    wqkv = nc.dram_tensor("wqkv", [DM, 3 * DH], F32, kind="ExternalInput").ap()
    bqkv = nc.dram_tensor("bqkv", [1, 3 * DH], F32, kind="ExternalInput").ap()
    q3 = nc.dram_tensor("q3", [512 * 128, 12], U8, kind="ExternalInput").ap()
    wnorm = nc.dram_tensor("wnorm", [512 * 128, 1], F32, kind="ExternalInput").ap()
    inproj_f32 = nc.dram_tensor("inproj_f32", [PROJ // 8, DM], F32, kind="ExternalInput").ap()
    outproj_f32 = nc.dram_tensor("outproj_f32", [DM // 8, DIN], F32, kind="ExternalInput").ap()

    q_out = nc.dram_tensor("q_out", [DH, T_OWN], BF16, kind="ExternalOutput").ap()
    k_out = nc.dram_tensor("k_out", [DH, T_OWN], BF16, kind="ExternalOutput").ap()
    v_out = nc.dram_tensor("v_out", [T_OWN, DH], BF16, kind="ExternalOutput").ap()
    wffnt_out = nc.dram_tensor("wffnt_out", [DM, 512], BF16, kind="ExternalOutput").ap()
    ipt_out = nc.dram_tensor("ipt_out", [DM, PROJ // 8], BF16, kind="ExternalOutput").ap()
    opt_out = nc.dram_tensor("opt_out", [DIN, DM // 8], BF16, kind="ExternalOutput").ap()
    id_bf = nc.dram_tensor("id_bf", [128, 128], BF16, kind="ExternalInput").ap()

    xn1 = nc.dram_tensor("xn1", [T_OWN, DM], BF16).ap()
    wqkv_bf = nc.dram_tensor("wqkv_bf", [DM, 3 * DH], BF16).ap()

    with contextlib.ExitStack() as stack:
        ctx = Ctx(nc, p, stack)

        def cast_chunks(dst, src, rows, tag, nchunks=4):
            step = (rows + nchunks - 1) // nchunks
            for i in range(0, rows, step):
                r = min(step, rows - i)
                p.add("gpsimd", lambda g, i=i, r=r: g.dma_start(
                    out=dst[i:i + r, :], in_=src[i:i + r, :]),
                    reads=[], writes=[(tag, i)], dma=True, tag=f"{tag}{i}")

        cast_chunks(wqkv_bf, wqkv, DM, "cast_qkv", 4)
        wqkv_cast_keys = [("cast_qkv", i) for i in range(0, DM, DM // 4)]

        with ctx.scope():
            emit_ln_stream(ctx, x0, xn1, "xn1", T_OWN, "ln1")
        xn1_keys = [("xn1", i) for i in range(8)]

        xn1fm = ctx.sbuf("xn1fm", [128, 32 * T_OWN], BF16)
        for cc_i in range(32):
            p.add("sync", lambda sc, cc_i=cc_i: sc.dma_start(
                out=xn1fm[:, cc_i * T_OWN:(cc_i + 1) * T_OWN],
                in_=xn1[:, cc_i * 128:(cc_i + 1) * 128], transpose=True),
                reads=xn1_keys, writes=[("xn1fm", cc_i)], dma=True,
                tag=f"trl_xn1_{cc_i}")
        xn1fm_keys = [("xn1fm", i) for i in range(32)]

        bqb = ctx.sbuf("bqb", [1, 3 * DH], BF16)
        ones = ctx.sbuf("ones", [1, 512], BF16)
        p.add("gpsimd", lambda g: g.dma_start(out=bqb[:, :], in_=bqkv[:, :]),
              reads=[], writes=["bqb"], dma=True, tag="ld_bqb")
        p.add("vector", lambda v: v.memset(ones[:, :], 1.0),
              reads=[], writes=["ones"], tag="ones")

        wtile = [ctx.sbuf(f"wt{j}", [128, 768], BF16) for j in range(4)]
        psb = [ctx.psum(f"ps{j}", [128, 512], F32) for j in range(7)]
        bkt = ctx.psum("bankT", [128, 1024], BF16)
        evb = [ctx.sbuf(f"ev{j}", [128, 512], BF16) for j in range(2)]
        identb = ctx.sbuf("identb", [128, 128], BF16)
        p.add("sync", lambda sy: sy.dma_start(out=identb[:, :], in_=id_bf[:, :]),
              writes=["identb"], dma=True, tag="ld_idb")
        trev = [ctx.sbuf(f"trev{j}", [128, 128], BF16) for j in range(4)]
        f_groups = [list(range(0, 6)), list(range(6, 12)), list(range(12, 16))]
        for th in range(2):
            tsl = slice(th * 512, (th + 1) * 512)
            for gi, fg in enumerate(f_groups):
                f0 = fg[0] * 128
                fw = len(fg) * 128
                for cc_i in range(32):
                    j = cc_i % 4
                    wk = (f"wt", j)
                    p.add("sync", lambda sy, cc_i=cc_i, j=j, f0=f0, fw=fw: sy.dma_start(
                        out=wtile[j][:, 0:fw],
                        in_=wqkv_bf[cc_i * 128:(cc_i + 1) * 128, f0:f0 + fw]),
                        reads=wqkv_cast_keys, writes=[wk], dma=True,
                        tag=f"qk_w{th}_{gi}_{cc_i}")
                    for fi, f in enumerate(fg):
                        bank = fi
                        p.add("tensor", lambda te, cc_i=cc_i, j=j, fi=fi, bank=bank, tsl=tsl: te.matmul(
                            psb[bank][:, :],
                            wtile[j][:, fi * 128:(fi + 1) * 128],
                            xn1fm[:, cc_i * T_OWN:cc_i * T_OWN + T_OWN][:, tsl],
                            start=(cc_i == 0), stop=False),
                            reads=[wk, ("xn1fm", cc_i)], writes=[("bank", bank)],
                            tag=f"qk_mm{th}_{gi}_{cc_i}_{fi}")
                for fi, f in enumerate(fg):
                    bank = fi
                    p.add("tensor", lambda te, f=f, bank=bank: te.matmul(
                        psb[bank][:, :], bqb[:, f * 128:(f + 1) * 128],
                        ones[:, :], start=False, stop=True),
                        reads=["bqb", "ones"], writes=[("bank", bank)],
                        tag=f"qk_bias{th}_{gi}_{fi}")
                for fi, f in enumerate(fg):
                    bank = fi
                    j = fi % 2
                    p.add("scalar", lambda sc, bank=bank, j=j: sc.activation(
                        evb[j][:, :], psb[bank][:, :], AF.Copy),
                        reads=[("bank", bank)], writes=[("ev", j)],
                        tag=f"qk_ev{th}_{gi}_{fi}")
                    dst = q_out if f < 8 else k_out
                    fr = f * 128 if f < 8 else (f - 8) * 128
                    p.add("gpsimd", lambda g, dst=dst, fr=fr, j=j, th=th: g.dma_start(
                        out=dst[fr:fr + 128, th * 512:(th + 1) * 512],
                        in_=evb[j][:, :]),
                        reads=[("ev", j)], writes=[("qkout", f, th)], dma=True,
                        tag=f"qk_st{th}_{gi}_{fi}")

        wvt = [ctx.sbuf(f"wvt{j}", [128, 1024], BF16) for j in range(4)]
        for tp in range(4):
            for cc_i in range(32):
                j = cc_i % 4
                wk = (f"wvt", j)
                p.add("sync", lambda sy, cc_i=cc_i, j=j: sy.dma_start(
                    out=wvt[j][:, :],
                    in_=wqkv_bf[cc_i * 128:(cc_i + 1) * 128, 2048:3072]),
                    reads=wqkv_cast_keys, writes=[wk], dma=True,
                    tag=f"v_w{tp}_{cc_i}")
                for ti in range(2):
                    t = tp * 2 + ti
                    for fh in range(2):
                        bank = ti * 2 + fh
                        p.add("tensor", lambda te, cc_i=cc_i, j=j, t=t, fh=fh, bank=bank: te.matmul(
                            psb[bank][:, :],
                            xn1fm[:, cc_i * T_OWN + t * 128: cc_i * T_OWN + (t + 1) * 128],
                            wvt[j][:, fh * 512:(fh + 1) * 512],
                            start=(cc_i == 0), stop=False),
                            reads=[wk, ("xn1fm", cc_i)], writes=[("bank", bank)],
                            tag=f"v_mm{t}_{fh}_{cc_i}")
            for ti in range(2):
                t = tp * 2 + ti
                for fh in range(2):
                    bank = ti * 2 + fh
                    p.add("tensor", lambda te, fh=fh, bank=bank, t=t: te.matmul(
                        psb[bank][:, :], ones[:, 0:128],
                        bqb[:, 2048 + fh * 512:2048 + (fh + 1) * 512],
                        start=False, stop=True),
                        reads=["bqb", "ones"], writes=[("bank", bank)],
                        tag=f"v_bias{t}_{fh}")
                    j = fh
                    p.add("scalar", lambda sc, bank=bank, j=j: sc.activation(
                        evb[j][:, :], psb[bank][:, :], AF.Copy),
                        reads=[("bank", bank)], writes=[("ev", j)],
                        tag=f"v_ev{t}_{fh}")
                    p.add("gpsimd", lambda g, t=t, fh=fh, j=j: g.dma_start(
                        out=v_out[t * 128:(t + 1) * 128, fh * 512:(fh + 1) * 512],
                        in_=evb[j][:, :]),
                        reads=[("ev", j)], writes=[("vout", t, fh)], dma=True,
                        tag=f"v_st{t}_{fh}")

        with ctx.scope():
            q3v = q3.rearrange("(o g) b -> o (g b)", g=128)
            wnv = wnorm.rearrange("(o g) c -> o (g c)", g=128)
            dqb = ctx.sbuf("dqb", [128, 1536], U8)
            di = [ctx.sbuf(f"di{j}", [128, 512], I32) for j in range(3)]
            dn = ctx.sbuf("dn", [128, 512], I32)
            dq = ctx.sbuf("dq", [128, DM], I32)
            df = ctx.sbuf("df", [128, DM], F32)
            dnrm = ctx.sbuf("dnrm", [128, 128], F32)
            dout = ctx.sbuf("dout", [128, DM], BF16)
            for blk in range(4):
                p.add("sync", lambda sy, blk=blk: sy.dma_start(
                    out=dqb[:, :], in_=q3v[blk * 128:(blk + 1) * 128, :]),
                    reads=[("dq_done", blk - 1)] if blk else [],
                    writes=["dqb"], dma=True, tag=f"dq_ld{blk}")
                p.add("sync", lambda sy, blk=blk: sy.dma_start(
                    out=dnrm[:, :], in_=wnv[blk * 128:(blk + 1) * 128, :]),
                    reads=[], writes=["dnrm"], dma=True, tag=f"dq_ldn{blk}")
                bv = dqb[:, :].rearrange("p (t k) -> p t k", k=3)
                for j in range(3):
                    p.add("vector", lambda v, j=j, bv=bv: v.tensor_copy(
                        di[j][:, :], bv[:, :, j]),
                        reads=["dqb"], writes=[("di", j)], tag=f"dq_b{blk}_{j}")
                p.add("vector", lambda v: v.tensor_scalar(
                    dn[:, :], di[1][:, :], 8, 0, ALU.logical_shift_left, ALU.bitwise_or),
                    reads=[("di", 1)], writes=["dn"], tag=f"dq_s1_{blk}")
                p.add("vector", lambda v: v.tensor_tensor(
                    dn[:, :], dn[:, :], di[0][:, :], ALU.bitwise_or),
                    reads=["dn", ("di", 0)], writes=["dn"], tag=f"dq_o1_{blk}")
                p.add("vector", lambda v: v.tensor_scalar(
                    di[1][:, :], di[2][:, :], 16, 0, ALU.logical_shift_left, ALU.bitwise_or),
                    reads=[("di", 2)], writes=[("di", 1)], tag=f"dq_s2_{blk}")
                p.add("vector", lambda v: v.tensor_tensor(
                    dn[:, :], dn[:, :], di[1][:, :], ALU.bitwise_or),
                    reads=["dn", ("di", 1)], writes=["dn"], tag=f"dq_o2_{blk}")
                dqv = dq[:, :].rearrange("p (t k) -> p t k", k=8)
                for pos in range(8):
                    p.add("vector", lambda v, pos=pos, dqv=dqv: v.tensor_scalar(
                        dqv[:, :, pos], dn[:, :], 3 * pos, 7,
                        ALU.logical_shift_right, ALU.bitwise_and),
                        reads=["dn"], writes=["dq"], tag=f"dq_d{blk}_{pos}")
                p.add("vector", lambda v: v.tensor_scalar(
                    dq[:, :], dq[:, :], 2, -7, ALU.mult, ALU.add),
                    reads=["dq"], writes=["dq"], tag=f"dq_aff{blk}")
                p.add("vector", lambda v: v.tensor_copy(df[:, :], dq[:, :]),
                    reads=["dq"], writes=["df"], tag=f"dq_f{blk}")
                nview = dnrm[:, :].unsqueeze(-1).broadcast_to([128, 128, 32])
                dfv = df[:, :].rearrange("p (a b) -> p a b", a=128)
                p.add("vector", lambda v, nview=nview, dfv=dfv: v.scalar_tensor_tensor(
                    dout[:, :].rearrange("p (a b) -> p a b", a=128),
                    dfv, 1.0 / 7.0, nview, ALU.mult, ALU.mult),
                    reads=["df", "dnrm"], writes=["dout"], tag=f"dq_w{blk}")
                for cb in range(32):
                    tj = cb % 4
                    p.add("tensor", lambda te, cb=cb: te.transpose(
                        bkt[:, 0:128], dout[:, cb * 128:(cb + 1) * 128], identb[:, :]),
                        reads=["dout", "identb"], writes=["bkt"], tag=f"dq_t{blk}_{cb}")
                    p.add("vector", lambda v, tj=tj: v.tensor_copy(
                        trev[tj][:, :], bkt[:, 0:128]),
                        reads=["bkt"], writes=[("trev", tj)], tag=f"dq_tc{blk}_{cb}")
                    p.add("gpsimd", lambda g, blk=blk, cb=cb, tj=tj: g.dma_start(
                        out=wffnt_out[cb * 128:(cb + 1) * 128, blk * 128:(blk + 1) * 128],
                        in_=trev[tj][:, :]),
                        reads=[("trev", tj)], writes=[("dq_done", blk)], dma=True,
                        tag=f"dq_st{blk}_{cb}")

            wtc = [ctx.sbuf(f"wtc{j}", [128, DIN], BF16) for j in range(2)]
            NIP = PROJ // 8
            for ft in range((NIP + 127) // 128):
                r = min(128, NIP - ft * 128)
                j = ft % 2
                p.add("gpsimd", lambda g, ft=ft, r=r, j=j: g.dma_start(
                    out=wtc[j][0:r, 0:DM], in_=inproj_f32[ft * 128:ft * 128 + r, :]),
                    writes=[("wtc", j)], dma=True, tag=f"ipl{ft}")
                for cb in range(32):
                    tj = cb % 4
                    p.add("tensor", lambda te, r=r, j=j, cb=cb: te.transpose(
                        bkt[:, 0:r], wtc[j][0:r, cb * 128:(cb + 1) * 128],
                        identb[0:r, 0:r]),
                        reads=[("wtc", j), "identb"], writes=["bkt"], tag=f"ipt{ft}_{cb}")
                    p.add("vector", lambda v, tj=tj, r=r: v.tensor_copy(
                        trev[tj][:, 0:r], bkt[:, 0:r]),
                        reads=["bkt"], writes=[("trev", tj)], tag=f"iptc{ft}_{cb}")
                    p.add("scalar", lambda v, ft=ft, cb=cb, tj=tj, r=r: v.dma_start(
                        out=ipt_out[cb * 128:(cb + 1) * 128, ft * 128:ft * 128 + r],
                        in_=trev[tj][:, 0:r]),
                        reads=[("trev", tj)], writes=[("ipt_o", ft, cb)], dma=True,
                        tag=f"ips{ft}_{cb}")
            for ft in range(4):
                j = ft % 2
                p.add("gpsimd", lambda g, ft=ft, j=j: g.dma_start(
                    out=wtc[j][:, :], in_=outproj_f32[ft * 128:(ft + 1) * 128, :]),
                    writes=[("wtc", j)], dma=True, tag=f"opl{ft}")
                for cb in range(64):
                    tj = cb % 4
                    p.add("tensor", lambda te, j=j, cb=cb: te.transpose(
                        bkt[:, 0:128], wtc[j][:, cb * 128:(cb + 1) * 128], identb[:, :]),
                        reads=[("wtc", j), "identb"], writes=["bkt"], tag=f"opt{ft}_{cb}")
                    p.add("vector", lambda v, tj=tj: v.tensor_copy(
                        trev[tj][:, :], bkt[:, 0:128]),
                        reads=["bkt"], writes=[("trev", tj)], tag=f"optc{ft}_{cb}")
                    p.add("scalar", lambda v, ft=ft, cb=cb, tj=tj: v.dma_start(
                        out=opt_out[cb * 128:(cb + 1) * 128, ft * 128:(ft + 1) * 128],
                        in_=trev[tj][:, :]),
                        reads=[("trev", tj)], writes=[("opt_o", ft, cb)], dma=True,
                        tag=f"ops{ft}_{cb}")

        p.emit(nc)
    return nc



T_PAD = 1152
NQT = 9
QT_ROWS = [128] * 8 + [3]


def build_L2():
    nc = bass.Bass("TRN2", target_bir_lowering=False, debug=False)
    p = Prog()

    x0e = nc.dram_tensor("x0e", [T_EXT, DM], F32, kind="ExternalInput").ap()
    q_ext = nc.dram_tensor("q_ext", [DH, T_EXT], BF16, kind="ExternalInput").ap()
    k_fm = nc.dram_tensor("k_fm", [DH, 8192], BF16, kind="ExternalInput").ap()
    v_tm = nc.dram_tensor("v_tm", [8192, DH], BF16, kind="ExternalInput").ap()
    wo = nc.dram_tensor("wo", [DH, DM], F32, kind="ExternalInput").ap()
    bo = nc.dram_tensor("bo", [1, DM], F32, kind="ExternalInput").ap()
    wffn = nc.dram_tensor("wffn", [DM, DM], BF16, kind="ExternalInput").ap()
    fbias = nc.dram_tensor("fbias", [1, DM], F32, kind="ExternalInput").ap()
    ipbf = nc.dram_tensor("ipbf", [DM, PROJ], BF16, kind="ExternalInput").ap()
    convw_r = nc.dram_tensor("convw_r", [128, 320], F32, kind="ExternalInput").ap()
    convb_r = nc.dram_tensor("convb_r", [128, 80], F32, kind="ExternalInput").ap()
    dtb_col = nc.dram_tensor("dtb_col", [128, 1], F32, kind="ExternalInput").ap()
    alog_col = nc.dram_tensor("alog_col", [128, 1], F32, kind="ExternalInput").ap()
    d_exp = nc.dram_tensor("d_exp", [128, DIN], F32, kind="ExternalInput").ap()
    halo_m = nc.dram_tensor("halo_m", [128, 4], F32, kind="ExternalInput").ap()
    tri0 = nc.dram_tensor("tri0", [128, 256], F32, kind="ExternalInput").ap()
    tri1 = nc.dram_tensor("tri1", [128, 256], F32, kind="ExternalInput").ap()
    id_bf = nc.dram_tensor("id_bf", [128, 128], BF16, kind="ExternalInput").ap()
    id_f = nc.dram_tensor("id_f", [128, 128], F32, kind="ExternalInput").ap()

    x2_out = nc.dram_tensor("x2_out", [T_EXT, DM], F32, kind="ExternalOutput").ap()
    y_part = nc.dram_tensor("y_part", [T_OWN, DIN], F32, kind="ExternalOutput").ap()
    gate_out = nc.dram_tensor("gate_out", [T_OWN, DIN], F32, kind="ExternalOutput").ap()
    c_out = nc.dram_tensor("c_out", [G * NST, T_OWN], BF16, kind="ExternalOutput").ap()
    acs_out = nc.dram_tensor("acs_out", [T_OWN, NH], F32, kind="ExternalOutput").ap()
    t_out = nc.dram_tensor("t_out", [NH, NCH], F32, kind="ExternalOutput").ap()
    s_out = nc.dram_tensor("s_out", [NST, NH * HD], F32, kind="ExternalOutput").ap()
    sloc_out = nc.dram_tensor("sloc_out", [(NCH - 1) * NST, NH * HD], F32, kind="ExternalOutput").ap()

    x1d = nc.dram_tensor("x1d", [T_EXT, DM], F32, kind="ExternalOutput").ap()
    xn2d = nc.dram_tensor("xn2d", [T_PAD, DM], BF16).ap()
    xn3d = nc.dram_tensor("xn3d", [T_PAD, DM], BF16).ap()
    xsil_d = nc.dram_tensor("xsil_d", [DIN, T_OWN], BF16).ap()
    acsht_d = nc.dram_tensor("acsht_d", [NH, T_OWN], F32).ap()

    with contextlib.ExitStack() as stack:
        ctx = Ctx(nc, p, stack)
        bank = [ctx.psum(f"bank{i}", [128, 512], F32) for i in range(7)]
        bkt = ctx.psum("bankT", [128, 1024], BF16)
        BK = [("bank", i) for i in range(7)]
        BKT = ("bankT",)

        ones_b = ctx.sbuf("ones_b", [1, 512], BF16)
        ones_f = ctx.sbuf("ones_f", [1, 128], F32)
        identb = ctx.sbuf("identb", [128, 128], BF16)
        identf = ctx.sbuf("identf", [128, 128], F32)
        ev_f = [ctx.sbuf(f"ev_f{j}", [128, 512], F32) for j in range(4)]
        p.add("vector", lambda v: v.memset(ones_b[:, :], 1.0), writes=["ones_b"], tag="onesb")
        p.add("vector", lambda v: v.memset(ones_f[:, :], 1.0), writes=["ones_f"], tag="onesf")
        p.add("sync", lambda sy: sy.dma_start(out=identb[:, :], in_=id_bf[:, :]),
              writes=["identb"], dma=True, tag="ld_idb")
        p.add("sync", lambda sy: sy.dma_start(out=identf[:, :], in_=id_f[:, :]),
              writes=["identf"], dma=True, tag="ld_idf")

        with ctx.scope():
            zb = ctx.sbuf("zb", [128, DM], BF16)
            p.add("vector", lambda v: v.memset(zb[:, :], 0.0), writes=["zb"], tag="zb")
            for nm, d in (("xn2d", xn2d), ("xn3d", xn3d)):
                p.add("gpsimd", lambda g, d=d: g.dma_start(
                    out=d[1024:1152, :], in_=zb[:, :]),
                    reads=["zb"], writes=[(nm, 8)], dma=True, tag=f"zpad_{nm}")

        ao_keys = [("ao_fm", dc, qt) for dc in range(8) for qt in range(NQT)]
        with ctx.scope():
            ao_fm = ctx.sbuf("ao_fm", [128, 8 * T_EXT], BF16)
            with ctx.scope():
                q_sb = ctx.sbuf("q_sb", [128, 8 * T_EXT], BF16)
                for dc in range(8):
                    p.add("sync", lambda sy, dc=dc: sy.dma_start(
                        out=q_sb[:, dc * T_EXT:(dc + 1) * T_EXT],
                        in_=q_ext[dc * 128:(dc + 1) * 128, :]),
                        writes=[("q_sb", dc)], dma=True, tag=f"ldq{dc}")
                KQ = 1024
                k_sb = [ctx.sbuf(f"k_sb{j}", [128, 8 * KQ], BF16) for j in range(2)]
                v_sb = [ctx.sbuf(f"v_sb{j}", [128, (KQ // 128) * DH], BF16) for j in range(2)]
                S_f = [ctx.sbuf(f"S_f{q}", [128, KQ], F32) for q in range(3)]
                P_b = [ctx.sbuf(f"P_b{q}", [128, KQ], BF16) for q in range(3)]
                PT_b = [ctx.sbuf(f"PT_b{q}", [128, 16 * 128], BF16) for q in range(3)]
                oacc = [ctx.sbuf(f"oacc{t}", [128, DH], F32) for t in range(NQT)]
                mrun = ctx.sbuf("mrun", [128, NQT], F32)
                srun = ctx.sbuf("srun", [128, NQT], F32)
                tcolr = [ctx.sbuf(f"tcol{q}", [128, 6], F32) for q in range(3)]
                ao_tm = [ctx.sbuf(f"ao_tm{q}", [128, DH], BF16) for q in range(3)]
                SCALE = 1.0 / 32.0
                NKQ = 8192 // KQ
                NKC = KQ // 128

                def attnA(kq, qt, kp):
                    r = QT_ROWS[qt]
                    q0 = qt * 128
                    qp = qt % 3
                    tcol = tcolr[qp]
                    for sb4 in range(KQ // 512):
                        bk = sb4 % 4
                        for dc in range(8):
                            p.add("tensor", lambda te, dc=dc, sb4=sb4, bk=bk, q0=q0, r=r, kp=kp: te.matmul(
                                bank[bk][0:r, :],
                                q_sb[:, dc * T_EXT + q0: dc * T_EXT + q0 + r],
                                k_sb[kp][:, dc * KQ + sb4 * 512: dc * KQ + (sb4 + 1) * 512],
                                start=(dc == 0), stop=(dc == 7)),
                                reads=[("q_sb", dc), ("k_sb", kp, dc)], writes=[("bank", bk)],
                                tag=f"s_mm{kq}_{qt}_{sb4}_{dc}")
                        p.add("scalar", lambda sc, sb4=sb4, bk=bk, r=r, qp=qp: sc.activation(
                            S_f[qp][0:r, sb4 * 512:(sb4 + 1) * 512], bank[bk][0:r, :], AF.Copy),
                            reads=[("bank", bk)], writes=[("S_f", qp, sb4)],
                            tag=f"s_ev{kq}_{qt}_{sb4}")
                    SFK = [("S_f", qp, i) for i in range(KQ // 512)]
                    p.add("vector", lambda v, r=r, tcol=tcol, qp=qp: v.tensor_reduce(
                        tcol[0:r, 0:1], S_f[qp][0:r, :], AX.X, ALU.max, negate=True),
                        reads=SFK, writes=[("tcol", qp, 0)], tag=f"qmax{kq}_{qt}")
                    if kq == 0:
                        p.add("vector", lambda v, r=r, qt=qt, tcol=tcol, qp=qp: v.tensor_scalar(
                            mrun[0:r, qt:qt + 1], tcol[0:r, 0:1], -1.0, 0.0,
                            ALU.mult, ALU.add),
                            reads=[("tcol", qp, 0)], writes=[("mrun", qt)],
                            tag=f"mr0_{kq}_{qt}")
                    else:
                        p.add("vector", lambda v, r=r, tcol=tcol: v.tensor_scalar(
                            tcol[0:r, 1:2], tcol[0:r, 0:1], -1.0, 0.0, ALU.mult, ALU.add),
                            reads=[("tcol", qp, 0)], writes=[("tcol", qp, 1)], tag=f"qm_{kq}_{qt}")
                        p.add("vector", lambda v, r=r, qt=qt, tcol=tcol: v.tensor_tensor(
                            tcol[0:r, 2:3], mrun[0:r, qt:qt + 1], tcol[0:r, 1:2], ALU.max),
                            reads=[("tcol", qp, 1), ("mrun", qt)], writes=[("tcol", qp, 2)],
                            tag=f"mn_{kq}_{qt}")
                        p.add("vector", lambda v, r=r, qt=qt, tcol=tcol: v.tensor_tensor(
                            tcol[0:r, 3:4], mrun[0:r, qt:qt + 1], tcol[0:r, 2:3], ALU.subtract),
                            reads=[("tcol", qp, 2), ("mrun", qt)], writes=[("tcol", qp, 3)],
                            tag=f"md_{kq}_{qt}")
                        p.add("scalar", lambda sc, r=r, tcol=tcol: sc.activation(
                            tcol[0:r, 4:5], tcol[0:r, 3:4], AF.Exp, scale=SCALE),
                            reads=[("tcol", qp, 3)], writes=[("tcol", qp, 4)], tag=f"al_{kq}_{qt}")
                        p.add("vector", lambda v, r=r, qt=qt, tcol=tcol: v.tensor_copy(
                            mrun[0:r, qt:qt + 1], tcol[0:r, 2:3]),
                            reads=[("tcol", qp, 2)], writes=[("mrun", qt)], tag=f"mu_{kq}_{qt}")
                    p.add("vector", lambda v, r=r, qt=qt, tcol=tcol: v.tensor_scalar(
                        tcol[0:r, 5:6], mrun[0:r, qt:qt + 1], -SCALE, 0.0, ALU.mult, ALU.add),
                        reads=[("mrun", qt)], writes=[("tcol", qp, 5)], tag=f"pb_{kq}_{qt}")
                    p.add("scalar", lambda sc, r=r, tcol=tcol, qp=qp: sc.activation(
                        P_b[qp][0:r, :], S_f[qp][0:r, :], AF.Exp, scale=SCALE,
                        bias=tcol[0:r, 5:6], accum_out=tcol[0:r, 0:1]),
                        reads=SFK + [("tcol", qp, 5)], writes=[("P_b", qp), ("tcol", qp, 0)],
                        tag=f"pe_{kq}_{qt}")

                def attnB(kq, qt, kp):
                    r = QT_ROWS[qt]
                    qp = qt % 3
                    tcol = tcolr[qp]
                    if kq == 0:
                        p.add("vector", lambda v, r=r, qt=qt, tcol=tcol: v.tensor_copy(
                            srun[0:r, qt:qt + 1], tcol[0:r, 0:1]),
                            reads=[("tcol", qp, 0)], writes=[("srun", qt)], tag=f"s0_{kq}_{qt}")
                    else:
                        p.add("vector", lambda v, r=r, qt=qt, tcol=tcol: v.scalar_tensor_tensor(
                            srun[0:r, qt:qt + 1], srun[0:r, qt:qt + 1], tcol[0:r, 4:5],
                            tcol[0:r, 0:1], ALU.mult, ALU.add),
                            reads=[("tcol", qp, 0), ("tcol", qp, 4), ("srun", qt)],
                            writes=[("srun", qt)], tag=f"sr_{kq}_{qt}")
                        p.add("vector", lambda v, r=r, qt=qt, tcol=tcol: v.tensor_scalar(
                            oacc[qt][0:r, :], oacc[qt][0:r, :], tcol[0:r, 4:5], 0.0,
                            ALU.mult, ALU.add),
                            reads=[("oacc", qt), ("tcol", qp, 4)], writes=[("oacc", qt)],
                            tag=f"oa_{kq}_{qt}")
                    for kc in range(NKC):
                        p.add("tensor", lambda te, kc=kc, r=r, qp=qp: te.transpose(
                            bkt[:, 0:r], P_b[qp][0:r, kc * 128:(kc + 1) * 128], identb[0:r, 0:r]),
                            reads=[("P_b", qp), "identb"], writes=[BKT], tag=f"pt_{kq}_{qt}_{kc}")
                        p.add("vector", lambda v, kc=kc, r=r, qp=qp: v.tensor_copy(
                            PT_b[qp][:, kc * 128:kc * 128 + r], bkt[:, 0:r]),
                            reads=[BKT], writes=[("PT_b", qp, kc)], tag=f"pc_{kq}_{qt}_{kc}")
                    for fh in range(2):
                        bk = 4 + fh
                        for kc in range(NKC):
                            p.add("tensor", lambda te, kc=kc, fh=fh, bk=bk, r=r, qp=qp, kp=kp: te.matmul(
                                bank[bk][0:r, :],
                                PT_b[qp][:, kc * 128:kc * 128 + r],
                                v_sb[kp][:, kc * DH + fh * 512: kc * DH + (fh + 1) * 512],
                                start=(kc == 0), stop=(kc == NKC - 1)),
                                reads=[("PT_b", qp, kc), ("v_sb", kp, kc)], writes=[("bank", bk)],
                                tag=f"pv_{kq}_{qt}_{fh}_{kc}")
                        if kq == 0:
                            p.add("scalar", lambda sc, fh=fh, bk=bk, r=r, qt=qt: sc.activation(
                                oacc[qt][0:r, fh * 512:(fh + 1) * 512],
                                bank[bk][0:r, :], AF.Copy),
                                reads=[("bank", bk)], writes=[("oacc", qt)],
                                tag=f"pz_{kq}_{qt}_{fh}")
                        else:
                            p.add("vector", lambda v, fh=fh, bk=bk, r=r, qt=qt: v.tensor_tensor(
                                oacc[qt][0:r, fh * 512:(fh + 1) * 512],
                                oacc[qt][0:r, fh * 512:(fh + 1) * 512],
                                bank[bk][0:r, :], ALU.add),
                                reads=[("bank", bk), ("oacc", qt)], writes=[("oacc", qt)],
                                tag=f"pa_{kq}_{qt}_{fh}")

                LA = 2
                for kq in range(NKQ):
                    kp = kq % 2
                    for dc in range(8):
                        p.add("sync", lambda sy, dc=dc, kq=kq, kp=kp: sy.dma_start(
                            out=k_sb[kp][:, dc * KQ:(dc + 1) * KQ],
                            in_=k_fm[dc * 128:(dc + 1) * 128, kq * KQ:(kq + 1) * KQ]),
                            writes=[("k_sb", kp, dc)], dma=True, tag=f"ldk{kq}_{dc}")
                    for kc in range(NKC):
                        p.add("scalar", lambda sc, kc=kc, kq=kq, kp=kp: sc.dma_start(
                            out=v_sb[kp][:, kc * DH:(kc + 1) * DH],
                            in_=v_tm[kq * KQ + kc * 128: kq * KQ + (kc + 1) * 128, :]),
                            writes=[("v_sb", kp, kc)], dma=True, tag=f"ldv{kq}_{kc}")
                    for qt in range(NQT + LA):
                        if qt < NQT:
                            attnA(kq, qt, kp)
                        if qt >= LA:
                            attnB(kq, qt - LA, kp)
                for qt in range(NQT):
                    r = QT_ROWS[qt]
                    qp = qt % 3
                    tcol = tcolr[qp]
                    p.add("vector", lambda v, r=r, qt=qt, tcol=tcol: v.reciprocal(
                        tcol[0:r, 0:1], srun[0:r, qt:qt + 1]),
                        reads=[("srun", qt)], writes=[("tcol", qp, 0)], tag=f"fr{qt}")
                    p.add("vector", lambda v, r=r, qt=qt, tcol=tcol, qp=qp: v.tensor_scalar(
                        ao_tm[qp][0:r, :], oacc[qt][0:r, :], tcol[0:r, 0:1], 0.0,
                        ALU.mult, ALU.add),
                        reads=[("oacc", qt), ("tcol", qp, 0)], writes=[("ao_tm", qp)], tag=f"fs{qt}")
                    for dc in range(8):
                        p.add("tensor", lambda te, dc=dc, r=r, qp=qp: te.transpose(
                            bkt[:, 0:r], ao_tm[qp][0:r, dc * 128:(dc + 1) * 128], identb[0:r, 0:r]),
                            reads=[("ao_tm", qp), "identb"], writes=[BKT], tag=f"at{qt}_{dc}")
                        p.add("vector", lambda v, dc=dc, r=r, qt=qt: v.tensor_copy(
                            ao_fm[:, dc * T_EXT + qt * 128: dc * T_EXT + qt * 128 + r],
                            bkt[:, 0:r]),
                            reads=[BKT], writes=[("ao_fm", dc, qt)], tag=f"af{qt}_{dc}")
            with ctx.scope():
                wo_t = [ctx.sbuf(f"wo_t{j}", [128, 512], BF16) for j in range(4)]
                bo_b = ctx.sbuf("bo_b", [1, DM], BF16)
                x0_t = [ctx.sbuf(f"x0_t{j}", [128, 512], F32) for j in range(2)]
                p.add("gpsimd", lambda g: g.dma_start(out=bo_b[:, :], in_=bo[:, :]),
                      writes=["bo_b"], dma=True, tag="ld_bo")
                for qtg in (range(0, 6), range(6, 9)):
                    for f5 in range(8):
                        for dc in range(8):
                            j = dc % 4
                            p.add("gpsimd", lambda g, f5=f5, dc=dc, j=j: g.dma_start(
                                out=wo_t[j][:, :],
                                in_=wo[dc * 128:(dc + 1) * 128, f5 * 512:(f5 + 1) * 512]),
                                writes=[("wo_t", j)], dma=True, tag=f"lw{f5}_{dc}")
                            for bi, qt in enumerate(qtg):
                                r = QT_ROWS[qt]
                                p.add("tensor", lambda te, dc=dc, j=j, bi=bi, qt=qt, r=r: te.matmul(
                                    bank[bi][0:r, :],
                                    ao_fm[:, dc * T_EXT + qt * 128: dc * T_EXT + qt * 128 + r],
                                    wo_t[j][:, :], start=(dc == 0), stop=False),
                                    reads=[("wo_t", j), ("ao_fm", dc, qt)], writes=[BK[bi]],
                                    tag=f"wm{f5}_{qt}_{dc}")
                        for bi, qt in enumerate(qtg):
                            r = QT_ROWS[qt]
                            p.add("tensor", lambda te, bi=bi, r=r, f5=f5: te.matmul(
                                bank[bi][0:r, :], ones_b[:, 0:r],
                                bo_b[:, f5 * 512:(f5 + 1) * 512], start=False, stop=True),
                                reads=["ones_b", "bo_b"], writes=[BK[bi]], tag=f"wb{f5}_{qt}")
                            j = bi % 2
                            p.add("sync", lambda sy, qt=qt, f5=f5, j=j, r=r: sy.dma_start(
                                out=x0_t[j][0:r, :],
                                in_=x0e[qt * 128:qt * 128 + r, f5 * 512:(f5 + 1) * 512]),
                                writes=[("x0_t", j)], dma=True, tag=f"lx{f5}_{qt}")
                            p.add("vector", lambda v, bi=bi, j=j, r=r: v.tensor_tensor(
                                ev_f[j][0:r, :], bank[bi][0:r, :], x0_t[j][0:r, :], ALU.add),
                                reads=[BK[bi], ("x0_t", j)], writes=[("ev_f", j)],
                                tag=f"wr{f5}_{qt}")
                            p.add("gpsimd", lambda g, qt=qt, f5=f5, j=j, r=r: g.dma_start(
                                out=x1d[qt * 128:qt * 128 + r, f5 * 512:(f5 + 1) * 512],
                                in_=ev_f[j][0:r, :]),
                                reads=[("ev_f", j)], writes=[("x1d", qt, f5)], dma=True,
                                tag=f"sx{f5}_{qt}")
                emit_ln_stream(ctx, x1d, xn2d, "xn2d", T_EXT, "ln2",
                               src_keys=[("x1d", qt, f5) for qt in range(NQT) for f5 in range(8)])
        X1K = [("x1d", qt, f5) for qt in range(NQT) for f5 in range(8)]
        XN2K = [("xn2d", i) for i in range(9)]
        with ctx.scope():
            xn2fm = ctx.sbuf("xn2fm", [128, 32 * T_PAD], BF16)
            for cc_i in range(32):
                p.add("sync", lambda sc, cc_i=cc_i: sc.dma_start(
                    out=xn2fm[:, cc_i * T_PAD:(cc_i + 1) * T_PAD],
                    in_=xn2d[:, cc_i * 128:(cc_i + 1) * 128], transpose=True),
                    reads=XN2K, writes=[("xn2fm", cc_i)], dma=True, tag=f"t2_{cc_i}")
            fb_b = ctx.sbuf("fb_b", [1, DM], BF16)
            p.add("gpsimd", lambda g: g.dma_start(out=fb_b[:, :], in_=fbias[:, :]),
                  writes=["fb_b"], dma=True, tag="ld_fb")
            wf_t = [ctx.sbuf(f"wf_t{j}", [128, 512], BF16) for j in range(8)]
            xr_t = [ctx.sbuf(f"xr_t{j}", [128, 512], F32) for j in range(4)]

            def ffn_pass(tgs, fm_sb, fm_key, src_dram, src_keys, dst_dram,
                         dst_key, rows, tagp):
                for tg in tgs:
                    for og in range(8):
                        for cc_i in range(32):
                            j = cc_i % 8
                            p.add("sync", lambda sy, og=og, cc_i=cc_i, j=j: sy.dma_start(
                                out=wf_t[j][:, :],
                                in_=wffn[cc_i * 128:(cc_i + 1) * 128, og * 512:(og + 1) * 512]),
                                writes=[(f"{tagp}wf", j)], dma=True,
                                tag=f"{tagp}w{tg[0]}_{og}_{cc_i}")
                            for bi, qt in enumerate(tg):
                                r = rows[qt]
                                p.add("tensor", lambda te, cc_i=cc_i, j=j, bi=bi, qt=qt, r=r: te.matmul(
                                    bank[bi][0:r, :],
                                    fm_sb[:, cc_i * T_PAD + qt * 128: cc_i * T_PAD + qt * 128 + r],
                                    wf_t[j][:, :], start=(cc_i == 0), stop=False),
                                    reads=[(f"{tagp}wf", j), (fm_key, cc_i)],
                                    writes=[BK[bi]], tag=f"{tagp}m{og}_{qt}_{cc_i}")
                        for bi, qt in enumerate(tg):
                            r = rows[qt]
                            p.add("tensor", lambda te, bi=bi, r=r, og=og: te.matmul(
                                bank[bi][0:r, :], ones_b[:, 0:r],
                                fb_b[:, og * 512:(og + 1) * 512], start=False, stop=True),
                                reads=["ones_b", "fb_b"], writes=[BK[bi]],
                                tag=f"{tagp}b{og}_{qt}")
                            j = bi % 2
                            p.add("scalar", lambda sc, qt=qt, og=og, j=j, r=r: sc.dma_start(
                                out=xr_t[j][0:r, :],
                                in_=src_dram[qt * 128:qt * 128 + r, og * 512:(og + 1) * 512]),
                                reads=src_keys, writes=[(f"{tagp}x", j)], dma=True,
                                tag=f"{tagp}l{og}_{qt}")
                            p.add("vector", lambda v, bi=bi, j=j, r=r: v.tensor_tensor(
                                ev_f[j][0:r, :], bank[bi][0:r, :], xr_t[j][0:r, :], ALU.add),
                                reads=[BK[bi], (f"{tagp}x", j)], writes=[("ev_f", j)],
                                tag=f"{tagp}r{og}_{qt}")
                            p.add("gpsimd", lambda g, qt=qt, og=og, j=j, r=r: g.dma_start(
                                out=dst_dram[qt * 128:qt * 128 + r, og * 512:(og + 1) * 512],
                                in_=ev_f[j][0:r, :]),
                                reads=[("ev_f", j)], writes=[(dst_key, qt, og)], dma=True,
                                tag=f"{tagp}s{og}_{qt}")

            ffn_pass([range(0, 5), range(5, 9)], xn2fm, "xn2fm",
                     x1d, X1K, x2_out, "x2o", QT_ROWS, "f1")
            emit_ln_stream(ctx, x2_out, xn3d, "xn3d", T_EXT, "ln3",
                           src_keys=[("x2o", qt, f5) for qt in range(NQT) for f5 in range(8)])
        X2K = [("x2o", qt, f5) for qt in range(NQT) for f5 in range(8)]

        XN3K = [("xn3d", i) for i in range(9)]

        B_fm = ctx.sbuf("B_fm", [128, 8 * T_OWN], BF16)
        C_fm = ctx.sbuf("C_fm", [128, 8 * T_OWN], BF16)
        dt_raw = ctx.sbuf("dt_raw", [128, T_PAD], F32)
        with ctx.scope():
            xn3fm = ctx.sbuf("xn3fm", [128, 32 * T_PAD], BF16)
            for cc_i in range(32):
                p.add("sync", lambda sc, cc_i=cc_i: sc.dma_start(
                    out=xn3fm[:, cc_i * T_PAD:(cc_i + 1) * T_PAD],
                    in_=xn3d[:, cc_i * 128:(cc_i + 1) * 128], transpose=True),
                    reads=XN3K, writes=[("xn3fm", cc_i)], dma=True, tag=f"t3_{cc_i}")
            ip_t = [ctx.sbuf(f"ip_t{j}", [128, 512], BF16) for j in range(8)]
            gate_ev = [ctx.sbuf(f"gate_ev{j}", [128, 512], F32) for j in range(2)]
            for tg in (range(0, 4), range(4, 8)):
                for f5 in range(16):
                    for cc_i in range(32):
                        j = cc_i % 8
                        p.add("sync", lambda sy, f5=f5, cc_i=cc_i, j=j: sy.dma_start(
                            out=ip_t[j][:, :],
                            in_=ipbf[cc_i * 128:(cc_i + 1) * 128, f5 * 512:(f5 + 1) * 512]),
                            writes=[("ip_t", j)], dma=True, tag=f"gw{tg[0]}_{f5}_{cc_i}")
                        for bi, t in enumerate(tg):
                            p.add("tensor", lambda te, cc_i=cc_i, j=j, bi=bi, t=t: te.matmul(
                                bank[bi][:, :],
                                xn3fm[:, cc_i * T_PAD + 3 + t * 128: cc_i * T_PAD + 3 + (t + 1) * 128],
                                ip_t[j][:, :], start=(cc_i == 0), stop=(cc_i == 31)),
                                reads=[("ip_t", j), ("xn3fm", cc_i)], writes=[BK[bi]],
                                tag=f"gm{f5}_{t}_{cc_i}")
                    for bi, t in enumerate(tg):
                        j = bi % 2
                        p.add("scalar", lambda sc, bi=bi, j=j, f5=f5, t=t: sc.activation(
                            gate_ev[j][:, :], bank[bi][:, :], AF.Silu),
                            reads=[BK[bi]], writes=[("gate_ev", j)], tag=f"ge{f5}_{t}")
                        p.add("gpsimd", lambda g, t=t, f5=f5, j=j: g.dma_start(
                            out=gate_out[t * 128:(t + 1) * 128, f5 * 512:(f5 + 1) * 512],
                            in_=gate_ev[j][:, :]),
                            reads=[("gate_ev", j)], writes=[("gate_o", t, f5)], dma=True,
                            tag=f"gs{f5}_{t}")
            stash = [ctx.sbuf(f"stash{c}", [128, 512], BF16) for c in range(32)]
            xbcr = ctx.sbuf("xbcr", [128, T_PAD], F32)
            conv_a = ctx.sbuf("conv_a", [128, T_OWN], F32)
            conv_b2 = ctx.sbuf("conv_b2", [128, T_OWN], F32)
            cwr = ctx.sbuf("cwr", [128, 320], F32)
            cbr = ctx.sbuf("cbr", [128, 80], F32)
            hm = ctx.sbuf("hm", [128, 4], F32)
            xsil_b = ctx.sbuf("xsil_b", [128, T_OWN], BF16)
            p.add("sync", lambda sy: sy.dma_start(out=cwr[:, :], in_=convw_r[:, :]),
                  writes=["cwr"], dma=True, tag="ld_cwr")
            p.add("sync", lambda sy: sy.dma_start(out=cbr[:, :], in_=convb_r[:, :]),
                  writes=["cbr"], dma=True, tag="ld_cbr")
            p.add("sync", lambda sy: sy.dma_start(out=hm[:, :], in_=halo_m[:, :]),
                  writes=["hm"], dma=True, tag="ld_hm")

            def emit_conv_silu(ft, tagc):
                p.add("vector", lambda v: v.tensor_tensor(
                    xbcr[:, 0:3], xbcr[:, 0:3], hm[:, 0:3], ALU.mult),
                    reads=["xbcr", "hm"], writes=["xbcr"], tag=f"{tagc}hm")
                p.add("vector", lambda v, ft=ft: v.tensor_scalar(
                    conv_a[:, :], xbcr[:, 0:T_OWN], cwr[:, ft * 4:ft * 4 + 1],
                    cbr[:, ft:ft + 1], ALU.mult, ALU.add),
                    reads=["xbcr", "cwr", "cbr"], writes=["conv_a"], tag=f"{tagc}c0")
                p.add("vector", lambda v, ft=ft: v.scalar_tensor_tensor(
                    conv_b2[:, :], xbcr[:, 1:1 + T_OWN], cwr[:, ft * 4 + 1:ft * 4 + 2],
                    conv_a[:, :], ALU.mult, ALU.add),
                    reads=["xbcr", "cwr", "conv_a"], writes=["conv_b2"], tag=f"{tagc}c1")
                p.add("vector", lambda v, ft=ft: v.scalar_tensor_tensor(
                    conv_a[:, :], xbcr[:, 2:2 + T_OWN], cwr[:, ft * 4 + 2:ft * 4 + 3],
                    conv_b2[:, :], ALU.mult, ALU.add),
                    reads=["xbcr", "cwr", "conv_b2"], writes=["conv_a"], tag=f"{tagc}c2")
                p.add("vector", lambda v, ft=ft: v.scalar_tensor_tensor(
                    conv_b2[:, :], xbcr[:, 3:3 + T_OWN], cwr[:, ft * 4 + 3:ft * 4 + 4],
                    conv_a[:, :], ALU.mult, ALU.add),
                    reads=["xbcr", "cwr", "conv_a"], writes=["conv_b2"], tag=f"{tagc}c3")
                if ft < 64:
                    p.add("scalar", lambda sc: sc.activation(
                        xsil_b[:, :], conv_b2[:, :], AF.Silu),
                        reads=["conv_b2"], writes=["xsil_b"], tag=f"{tagc}si")
                    p.add("gpsimd", lambda g, ft=ft: g.dma_start(
                        out=xsil_d[ft * 128:(ft + 1) * 128, :], in_=xsil_b[:, :]),
                        reads=["xsil_b"], writes=[("xsil", ft)], dma=True, tag=f"{tagc}sx")
                elif ft < 72:
                    gi = ft - 64
                    p.add("scalar", lambda sc, gi=gi: sc.activation(
                        B_fm[:, gi * T_OWN:(gi + 1) * T_OWN], conv_b2[:, :], AF.Silu),
                        reads=["conv_b2"], writes=[("B_fm", gi)], tag=f"{tagc}si")
                else:
                    gi = ft - 72
                    p.add("scalar", lambda sc, gi=gi: sc.activation(
                        C_fm[:, gi * T_OWN:(gi + 1) * T_OWN], conv_b2[:, :], AF.Silu),
                        reads=["conv_b2"], writes=[("C_fm", gi)], tag=f"{tagc}si")
                    p.add("gpsimd", lambda g, gi=gi: g.dma_start(
                        out=c_out[gi * 128:(gi + 1) * 128, :],
                        in_=C_fm[:, gi * T_OWN:(gi + 1) * T_OWN]),
                        reads=[("C_fm", gi)], writes=[("c_o", gi)], dma=True,
                        tag=f"{tagc}sc")

            TSL = [(0, 512), (512, 512), (1024, 128)]

            def emit_fsub(ft, sub, tagx):
                fofs = (ft % 4) * 128
                for cc_i in range(32):
                    for si, (t0, tw) in enumerate(TSL):
                        bi = sub * 3 + si
                        p.add("tensor", lambda te, cc_i=cc_i, fofs=fofs, t0=t0, tw=tw, bi=bi: te.matmul(
                            bank[bi][:, 0:tw],
                            stash[cc_i][:, fofs:fofs + 128],
                            xn3fm[:, cc_i * T_PAD + t0: cc_i * T_PAD + t0 + tw],
                            start=(cc_i == 0), stop=(cc_i == 31)),
                            reads=[("stash", cc_i), ("xn3fm", cc_i)], writes=[BK[bi]],
                            tag=f"{tagx}m{si}_{cc_i}")
                if ft < 80:
                    for si, (t0, tw) in enumerate(TSL):
                        bi = sub * 3 + si
                        p.add("scalar", lambda sc, bi=bi, t0=t0, tw=tw: sc.activation(
                            xbcr[:, t0:t0 + tw], bank[bi][:, 0:tw], AF.Copy),
                            reads=[BK[bi]], writes=["xbcr"], tag=f"{tagx}e{si}")
                    emit_conv_silu(ft, tagx)
                else:
                    for si, (t0, tw) in enumerate(TSL):
                        bi = sub * 3 + si
                        p.add("scalar", lambda sc, bi=bi, t0=t0, tw=tw: sc.activation(
                            dt_raw[:, t0:t0 + tw], bank[bi][:, 0:tw], AF.Copy),
                            reads=[BK[bi]], writes=["dt_raw"], tag=f"{tagx}d{si}")

            for fg in range(20):
                frow0 = DIN + fg * 512
                for cc_i in range(32):
                    p.add("sync", lambda sy, frow0=frow0, cc_i=cc_i: sy.dma_start(
                        out=stash[cc_i][:, :],
                        in_=ipbf[cc_i * 128:(cc_i + 1) * 128, frow0:frow0 + 512]),
                        writes=[("stash", cc_i)], dma=True, tag=f"xw{fg}_{cc_i}")
                emit_fsub(fg * 4 + 0, 0, f"xa{fg}_0_")
                emit_fsub(fg * 4 + 1, 1, f"xa{fg}_1_")
                emit_fsub(fg * 4 + 2, 0, f"xb{fg}_2_")
                emit_fsub(fg * 4 + 3, 1, f"xb{fg}_3_")
            for cc_i in range(32):
                p.add("sync", lambda sy, cc_i=cc_i: sy.dma_start(
                    out=stash[cc_i][:, 0:128],
                    in_=ipbf[cc_i * 128:(cc_i + 1) * 128, DIN + CONV_DIM:PROJ]),
                    writes=[("stash", cc_i)], dma=True, tag=f"dw_{cc_i}")
            emit_fsub(80, 0, "dt_")

        acs_ti = ctx.sbuf("acs_ti", [128, 8 * 128], F32)
        dt_ti = ctx.sbuf("dt_ti", [128, 8 * 128], F32)
        dtds_ti = ctx.sbuf("dtds_ti", [128, 8 * 128], F32)
        ttile = ctx.sbuf("ttile", [128, NCH], F32)
        expt_bc = ctx.sbuf("expt_bc", [128, NCH * 128], F32)
        with ctx.scope():
            dtb_s = ctx.sbuf("dtb_s", [128, 1], F32)
            al_s = ctx.sbuf("al_s", [128, 1], F32)
            p.add("sync", lambda sy: sy.dma_start(out=dtb_s[:, :], in_=dtb_col[:, :]),
                  writes=["dtb_s"], dma=True, tag="ld_dtb")
            p.add("sync", lambda sy: sy.dma_start(out=al_s[:, :], in_=alog_col[:, :]),
                  writes=["al_s"], dma=True, tag="ld_al")
            dt_ht = ctx.sbuf("dt_ht", [128, T_OWN], F32)
            sp_a = ctx.sbuf("sp_a", [128, T_OWN], F32)
            sp_b = ctx.sbuf("sp_b", [128, T_OWN], F32)
            nega = ctx.sbuf("nega", [128, 2], F32)
            adt_ht = ctx.sbuf("adt_ht", [128, T_OWN], F32)
            acs_ht = ctx.sbuf("acs_ht", [128, T_OWN], F32)
            ds_ht = ctx.sbuf("ds_ht", [128, T_OWN], F32)
            dtds_ht = ctx.sbuf("dtds_ht", [128, T_OWN], F32)
            zeros_t = ctx.sbuf("zeros_t", [128, CS], F32)
            p.add("vector", lambda v: v.memset(zeros_t[:, :], 0.0),
                  writes=["zeros_t"], tag="zt")
            p.add("vector", lambda v: v.tensor_scalar(
                dt_ht[:, :], dt_raw[:, 3:3 + T_OWN], dtb_s[:, 0:1], 0.0, ALU.add, ALU.add),
                reads=["dt_raw", "dtb_s"], writes=["dt_ht"], tag="sp_z")
            p.add("scalar", lambda sc: sc.activation(sp_a[:, :], dt_ht[:, :], AF.Abs),
                  reads=["dt_ht"], writes=["sp_a"], tag="sp_abs")
            p.add("scalar", lambda sc: sc.activation(sp_b[:, :], sp_a[:, :], AF.Exp, scale=-1.0),
                  reads=["sp_a"], writes=["sp_b"], tag="sp_exp")
            p.add("scalar", lambda sc: sc.activation(sp_a[:, :], sp_b[:, :], AF.Ln, bias=1.0),
                  reads=["sp_b"], writes=["sp_a"], tag="sp_ln")
            p.add("vector", lambda v: v.tensor_scalar(
                sp_b[:, :], dt_ht[:, :], 0.0, 0.0, ALU.max, ALU.add),
                reads=["dt_ht"], writes=["sp_b"], tag="sp_relu")
            p.add("vector", lambda v: v.tensor_tensor(
                dt_ht[:, :], sp_a[:, :], sp_b[:, :], ALU.add),
                reads=["sp_a", "sp_b"], writes=["dt_ht"], tag="sp_add")
            p.add("scalar", lambda sc: sc.activation(nega[:, 0:1], al_s[:, 0:1], AF.Exp),
                  reads=["al_s"], writes=["nega"], tag="na_e")
            p.add("vector", lambda v: v.tensor_scalar(
                nega[:, 1:2], nega[:, 0:1], -1.0, 0.0, ALU.mult, ALU.add),
                reads=["nega"], writes=["nega"], tag="na_n")
            p.add("vector", lambda v: v.tensor_scalar(
                adt_ht[:, :], dt_ht[:, :], nega[:, 1:2], 0.0, ALU.mult, ALU.add),
                reads=["dt_ht", "nega"], writes=["adt_ht"], tag="adt")
            for c in range(NCH):
                p.add("vector", lambda v, c=c: v.tensor_tensor_scan(
                    acs_ht[:, c * CS:(c + 1) * CS], adt_ht[:, c * CS:(c + 1) * CS],
                    zeros_t[:, :], 0.0, ALU.add, ALU.add),
                    reads=["adt_ht", "zeros_t"], writes=["acs_ht"], tag=f"scan{c}")
            p.add("gpsimd", lambda g: g.dma_start(out=acsht_d[:, :], in_=acs_ht[:, :]),
                  reads=["acs_ht"], writes=["acsht_d"], dma=True, tag="st_acsht")
            for c in range(NCH):
                p.add("vector", lambda v, c=c: v.tensor_copy(
                    ttile[:, c:c + 1], acs_ht[:, (c + 1) * CS - 1:(c + 1) * CS]),
                    reads=["acs_ht"], writes=["ttile"], tag=f"tc{c}")
            p.add("gpsimd", lambda g: g.dma_start(out=t_out[:, :], in_=ttile[:, :]),
                  reads=["ttile"], writes=["t_o"], dma=True, tag="st_t")
            for c in range(NCH):
                p.add("scalar", lambda sc, c=c: sc.activation(
                    ds_ht[:, c * CS:(c + 1) * CS], acs_ht[:, c * CS:(c + 1) * CS],
                    AF.Exp, scale=-1.0, bias=ttile[:, c:c + 1]),
                    reads=["acs_ht", "ttile"], writes=["ds_ht"], tag=f"ds{c}")
            p.add("vector", lambda v: v.tensor_tensor(
                dtds_ht[:, :], dt_ht[:, :], ds_ht[:, :], ALU.mult),
                reads=["dt_ht", "ds_ht"], writes=["dtds_ht"], tag="dtds")
            for srcap, srck, dst, kk in ((acs_ht, "acs_ht", acs_ti, "acs_ti"),
                                         (dt_ht, "dt_ht", dt_ti, "dt_ti"),
                                         (dtds_ht, "dtds_ht", dtds_ti, "dtds_ti")):
                for tt in range(8):
                    p.add("tensor", lambda te, srcap=srcap, tt=tt: te.transpose(
                        bank[6][:, 0:128], srcap[:, tt * 128:(tt + 1) * 128], identf[:, :]),
                        reads=[srck, "identf"], writes=[BK[6]], tag=f"{kk}t{tt}")
                    p.add("vector", lambda v, dst=dst, tt=tt, kk=kk: v.tensor_copy(
                        dst[:, tt * 128:(tt + 1) * 128], bank[6][:, 0:128]),
                        reads=[BK[6]], writes=[(kk, tt)], tag=f"{kk}c{tt}")
            for tt in range(8):
                p.add("gpsimd", lambda g, tt=tt: g.dma_start(
                    out=acs_out[tt * 128:(tt + 1) * 128, :],
                    in_=acs_ti[:, tt * 128:(tt + 1) * 128]),
                    reads=[("acs_ti", tt)], writes=[("acs_o", tt)], dma=True,
                    tag=f"sa{tt}")
            expt_r = ctx.sbuf("expt_r", [1, 128], F32)
            for c in range(NCH):
                p.add("gpsimd", lambda g, c=c: g.dma_start(
                    out=expt_r[0:1, 0:128],
                    in_=acs_out[(c + 1) * CS - 1:(c + 1) * CS, :]),
                    reads=[("acs_o", 2 * c + 1)], writes=["expt_r"], dma=True,
                    tag=f"er{c}")
                p.add("scalar", lambda sc: sc.activation(
                    expt_r[0:1, 0:128], expt_r[0:1, 0:128], AF.Exp),
                    reads=["expt_r"], writes=["expt_r"], tag=f"ee{c}")
                p.add("tensor", lambda te, c=c: te.matmul(
                    bank[6][:, 0:128], ones_f[:, :], expt_r[0:1, 0:128],
                    start=True, stop=True),
                    reads=["expt_r", "ones_f"], writes=[BK[6]], tag=f"em{c}")
                p.add("vector", lambda v, c=c: v.tensor_copy(
                    expt_bc[:, c * 128:(c + 1) * 128], bank[6][:, 0:128]),
                    reads=[BK[6]], writes=[("expt_bc", c)], tag=f"ec{c}")

        tri_s = ctx.sbuf("tri_s", [128, 512], F32)
        p.add("sync", lambda sy: sy.dma_start(out=tri_s[:, 0:256], in_=tri0[:, :]),
              writes=["tri_s"], dma=True, tag="ld_tri0")
        p.add("sync", lambda sy: sy.dma_start(out=tri_s[:, 256:512], in_=tri1[:, :]),
              writes=["tri_s"], dma=True, tag="ld_tri1")
        xstage = [ctx.sbuf(f"xstage{h2}", [128, 1024], BF16) for h2 in range(4)]
        btm = [ctx.sbuf(f"btm{j}", [128, 256], BF16) for j in range(2)]
        gmt = [ctx.sbuf(f"gmt{j}", [128, 512], F32) for j in range(2)]
        ltg = [ctx.sbuf(f"ltg{j}", [128, 512], BF16) for j in range(3)]
        lt1 = [ctx.sbuf(f"lt1_{j}", [128, 512], F32) for j in range(3)]
        arow = ctx.sbuf("arow", [1, 4096], F32)
        xs1 = [ctx.sbuf(f"xs1_{j}", [128, 128], BF16) for j in range(3)]
        xs2 = [ctx.sbuf(f"xs2_{j}", [128, 128], BF16) for j in range(3)]
        piece = ctx.sbuf("piece", [128, 1024], F32)
        dxl = ctx.sbuf("dxl", [128, 1024], F32)
        dex_t = [ctx.sbuf(f"dex_t{j}", [128, 1024], F32) for j in range(2)]
        slocal = ctx.sbuf("slocal", [128, DIN], F32)
        srun_s = ctx.sbuf("srun_s", [128, DIN], F32)
        p.add("vector", lambda v: v.memset(srun_s[:, :], 0.0),
              writes=["srun_s"], tag="sr0")
        for c in range(NCH):
            if c > 0:
                p.add("gpsimd", lambda g, c=c: g.dma_start(
                    out=sloc_out[(c - 1) * NST:c * NST, :], in_=srun_s[:, :]),
                    reads=["srun_s"], writes=[("sloc_o", c)], dma=True,
                    tag=f"ssl{c}")
            for g_i in range(G):
                g2 = g_i % 2
                for h2 in range(2):
                    p.add("sync", lambda sc, c=c, g_i=g_i, h2=h2, g2=g2: sc.dma_start(
                        out=xstage[g2 * 2 + h2][:, :],
                        in_=xsil_d[g_i * 1024:(g_i + 1) * 1024,
                                   c * CS + h2 * 128: c * CS + (h2 + 1) * 128],
                        transpose=True),
                        reads=[("xsil", g_i * 8 + k) for k in range(8)],
                        writes=[("xstage", g2, h2)], dma=True, tag=f"xs{c}_{g_i}_{h2}")
                p.add("sync", lambda sy, g_i=g_i, g2=g2: sy.dma_start(
                    out=dex_t[g2][:, :], in_=d_exp[:, g_i * 1024:(g_i + 1) * 1024]),
                    writes=[("dex_t", g2)], dma=True, tag=f"dx{c}_{g_i}")
                for h2 in range(2):
                    p.add("tensor", lambda te, c=c, g_i=g_i, h2=h2: te.transpose(
                        bkt[:, 0:128],
                        B_fm[:, g_i * T_OWN + c * CS + h2 * 128: g_i * T_OWN + c * CS + (h2 + 1) * 128],
                        identb[:, :]),
                        reads=[("B_fm", g_i), "identb"], writes=[BKT],
                        tag=f"bt{c}_{g_i}_{h2}")
                    p.add("vector", lambda v, h2=h2, g2=g2: v.tensor_copy(
                        btm[g2][:, h2 * 128:(h2 + 1) * 128], bkt[:, 0:128]),
                        reads=[BKT], writes=[("btm", g2, h2)], tag=f"bc{c}_{g_i}_{h2}")
                for jt in range(2):
                    p.add("tensor", lambda te, c=c, g_i=g_i, jt=jt: te.matmul(
                        bank[6][:, 0:256],
                        B_fm[:, g_i * T_OWN + c * CS + jt * 128: g_i * T_OWN + c * CS + (jt + 1) * 128],
                        C_fm[:, g_i * T_OWN + c * CS: g_i * T_OWN + (c + 1) * CS],
                        start=True, stop=True),
                        reads=[("B_fm", g_i), ("C_fm", g_i)], writes=[BK[6]],
                        tag=f"gt{c}_{g_i}_{jt}")
                    p.add("vector", lambda v, jt=jt, g2=g2: v.tensor_tensor(
                        gmt[g2][:, jt * 256:(jt + 1) * 256], bank[6][:, 0:256],
                        tri_s[:, jt * 256:(jt + 1) * 256], ALU.mult),
                        reads=[BK[6], "tri_s"], writes=[("gmt", g2, jt)],
                        tag=f"ge{c}_{g_i}_{jt}")
                p.add("gpsimd", lambda g, c=c, g_i=g_i: g.dma_start(
                    out=arow[0:1, :].rearrange("p (h t) -> p h t", h=16),
                    in_=acsht_d[g_i * 16:(g_i + 1) * 16,
                                c * CS:(c + 1) * CS].unsqueeze(0)),
                    reads=["acsht_d"], writes=["arow"], dma=True,
                    tag=f"ar{c}_{g_i}")

                def ssdA(hh, c=c, g_i=g_i, g2=g2):
                    h = g_i * 16 + hh
                    rbk = 3 + hh % 4
                    lj = hh % 3
                    p.add("tensor", lambda te, hh=hh, rbk=rbk: te.matmul(
                        bank[rbk][:, 0:256], ones_f[:, :],
                        arow[0:1, hh * 256:(hh + 1) * 256],
                        start=True, stop=True),
                        reads=["arow", "ones_f"], writes=[("bank", rbk)], tag=f"rm{c}_{h}")
                    for jt in range(2):
                        p.add("scalar", lambda sc, c=c, h=h, jt=jt, rbk=rbk, lj=lj: sc.activation(
                            lt1[lj][:, jt * 256:jt * 256 + 256], bank[rbk][:, 0:256],
                            AF.Relu, scale=-1.0,
                            bias=acs_ti[:, (2 * c + jt) * 128 + h:(2 * c + jt) * 128 + h + 1]),
                            reads=[("bank", rbk), ("acs_ti", 2 * c + jt)], writes=[("lt1", lj)],
                            tag=f"lr{c}_{h}_{jt}")
                        p.add("scalar", lambda sc, jt=jt, lj=lj: sc.activation(
                            lt1[lj][:, jt * 256:jt * 256 + 256],
                            lt1[lj][:, jt * 256:jt * 256 + 256], AF.Exp, scale=-1.0),
                            reads=[("lt1", lj)], writes=[("lt1", lj)], tag=f"le{c}_{h}_{jt}")
                        p.add("vector", lambda v, jt=jt, lj=lj, g2=g2: v.tensor_tensor(
                            ltg[lj][:, jt * 256:(jt + 1) * 256],
                            lt1[lj][:, jt * 256:jt * 256 + 256],
                            gmt[g2][:, jt * 256:(jt + 1) * 256], ALU.mult),
                            reads=[("lt1", lj), ("gmt", g2, jt)], writes=[("ltg", lj)],
                            tag=f"lg{c}_{h}_{jt}")
                    for h2 in range(2):
                        p.add("vector", lambda v, h2=h2, hh=hh, c=c, h=h, lj=lj, g2=g2: v.tensor_scalar(
                            xs1[lj][:, h2 * 64:(h2 + 1) * 64],
                            xstage[g2 * 2 + h2][:, hh * 64:(hh + 1) * 64],
                            dt_ti[:, (2 * c + h2) * 128 + h:(2 * c + h2) * 128 + h + 1],
                            0.0, ALU.mult, ALU.add),
                            reads=[("xstage", g2, h2), ("dt_ti", 2 * c + h2)],
                            writes=[("xs1", lj)], tag=f"x1_{c}_{h}_{h2}")
                        p.add("vector", lambda v, h2=h2, hh=hh, c=c, h=h, lj=lj, g2=g2: v.tensor_scalar(
                            xs2[lj][:, h2 * 64:(h2 + 1) * 64],
                            xstage[g2 * 2 + h2][:, hh * 64:(hh + 1) * 64],
                            dtds_ti[:, (2 * c + h2) * 128 + h:(2 * c + h2) * 128 + h + 1],
                            0.0, ALU.mult, ALU.add),
                            reads=[("xstage", g2, h2), ("dtds_ti", 2 * c + h2)],
                            writes=[("xs2", lj)], tag=f"x2_{c}_{h}_{h2}")

                def ssdB(hh, c=c, g_i=g_i, g2=g2):
                    h = g_i * 16 + hh
                    slot = hh % 8
                    lj = hh % 3
                    for it in range(2):
                        bi = it
                        for jt in range(2):
                            p.add("tensor", lambda te, it=it, jt=jt, bi=bi, slot=slot, lj=lj: te.matmul(
                                bank[bi][:, slot * 64:(slot + 1) * 64],
                                ltg[lj][:, jt * 256 + it * 128: jt * 256 + (it + 1) * 128],
                                xs1[lj][:, jt * 64:(jt + 1) * 64],
                                start=(slot == 0 and jt == 0),
                                stop=(slot == 7 and jt == 1)),
                                reads=[("ltg", lj), ("xs1", lj)], writes=[("bank", bi)],
                                tag=f"yd{c}_{h}_{it}_{jt}")
                    for h2 in range(2):
                        p.add("tensor", lambda te, h2=h2, slot=slot, lj=lj, g2=g2: te.matmul(
                            bank[2][:, slot * 64:(slot + 1) * 64],
                            btm[g2][:, h2 * 128:(h2 + 1) * 128],
                            xs2[lj][:, h2 * 64:(h2 + 1) * 64],
                            start=(slot == 0 and h2 == 0),
                            stop=(slot == 7 and h2 == 1)),
                            reads=[("btm", g2, h2), ("xs2", lj)], writes=[("bank", 2)],
                            tag=f"sm{c}_{h}_{h2}")
                    if slot == 7:
                        hb = hh // 8
                        h0 = g_i * 16 + hb * 8
                        p.add("vector", lambda v, h0=h0: v.tensor_copy(
                            slocal[:, h0 * 64:(h0 + 8) * 64], bank[2][:, :]),
                            reads=[("bank", 2)], writes=[("slocal", h0)],
                            tag=f"se{c}_{h0}")
                        for it in range(2):
                            bi2 = it
                            p.add("vector", lambda v, it=it, hb=hb, g2=g2: v.tensor_tensor(
                                dxl[:, 0:512],
                                xstage[g2 * 2 + it][:, hb * 512:(hb + 1) * 512],
                                dex_t[g2][:, hb * 512:(hb + 1) * 512], ALU.mult),
                                reads=[("xstage", g2, it), ("dex_t", g2)], writes=["dxl"],
                                tag=f"dx{c}_{h0}_{it}")
                            p.add("vector", lambda v, bi2=bi2: v.tensor_tensor(
                                piece[:, 0:512], bank[bi2][:, :], dxl[:, 0:512],
                                ALU.add),
                                reads=[("bank", bi2), "dxl"], writes=["piece"],
                                tag=f"yp{c}_{h0}_{it}")
                            p.add("gpsimd", lambda g, c=c, it=it, h0=h0: g.dma_start(
                                out=y_part[c * CS + it * 128: c * CS + (it + 1) * 128,
                                           h0 * 64:(h0 + 8) * 64],
                                in_=piece[:, 0:512]),
                                reads=["piece"], writes=[("yp_o", c, it, h0)],
                                dma=True, tag=f"ys{c}_{h0}_{it}")

                for hh in range(16 + 2):
                    if hh < 16:
                        ssdA(hh)
                    if hh >= 2:
                        ssdB(hh - 2)
            ex_view = expt_bc[:, c * 128:(c + 1) * 128].unsqueeze(-1).broadcast_to([128, 128, 64])
            sr_view = srun_s[:, :].rearrange("p (h d) -> p h d", d=64)
            p.add("vector", lambda v, ex_view=ex_view, sr_view=sr_view: v.tensor_tensor(
                sr_view, sr_view, ex_view, ALU.mult),
                reads=["srun_s", ("expt_bc", c)], writes=["srun_s"], tag=f"su{c}")
            p.add("vector", lambda v: v.tensor_tensor(
                srun_s[:, :], srun_s[:, :], slocal[:, :], ALU.add),
                reads=["srun_s", "slocal"] + [("slocal", g2 * 16) for g2 in range(G)] + [("slocal", g2 * 16 + 8) for g2 in range(G)],
                writes=["srun_s"], tag=f"sa{c}")
        p.add("gpsimd", lambda g: g.dma_start(out=s_out[:, :], in_=srun_s[:, :]),
              reads=["srun_s"], writes=["s_o"], dma=True, tag="st_send")

        p.emit(nc)
    return nc




def build_L3():
    nc = bass.Bass("TRN2", target_bir_lowering=False, debug=False)
    p = Prog()

    x2_in = nc.dram_tensor("x2_in", [T_OWN, DM], F32, kind="ExternalInput").ap()
    ypart = nc.dram_tensor("ypart", [T_OWN, DIN], F32, kind="ExternalInput").ap()
    gate = nc.dram_tensor("gate", [T_OWN, DIN], F32, kind="ExternalInput").ap()
    cfm_in = nc.dram_tensor("cfm_in", [G * NST, T_OWN], BF16, kind="ExternalInput").ap()
    acs_in = nc.dram_tensor("acs_in", [T_OWN, NH], F32, kind="ExternalInput").ap()
    sall = nc.dram_tensor("sall", [NH, 8 * NST * HD], F32, kind="ExternalInput").ap()
    sloc_h = nc.dram_tensor("sloc_h", [NH, 3 * NST * HD], F32, kind="ExternalInput").ap()
    townT = nc.dram_tensor("townT", [NH, NCH], F32, kind="ExternalInput").ap()
    tallT = nc.dram_tensor("tallT", [NH, 8 * NCH], F32, kind="ExternalInput").ap()
    m_lt = nc.dram_tensor("m_lt", [128, 8], F32, kind="ExternalInput").ap()
    mexp = nc.dram_tensor("mexp", [128, 64], F32, kind="ExternalInput").ap()
    wffn = nc.dram_tensor("wffn", [DM, DM], BF16, kind="ExternalInput").ap()
    fbias = nc.dram_tensor("fbias", [1, DM], F32, kind="ExternalInput").ap()
    wop = nc.dram_tensor("wop", [DIN, DM], BF16, kind="ExternalInput").ap()
    norw_b = nc.dram_tensor("norw_b", [128, DIN], F32, kind="ExternalInput").ap()

    out = nc.dram_tensor("out", [T_OWN, DM], F32, kind="ExternalOutput").ap()

    sinh_d = nc.dram_tensor("sinh_d", [NCH * NH, NST * HD], F32).ap()
    ynorm_d = nc.dram_tensor("ynorm_d", [T_OWN, DIN], BF16).ap()
    x3_d = nc.dram_tensor("x3_d", [T_OWN, DM], F32).ap()
    xn4_d = nc.dram_tensor("xn4_d", [T_OWN, DM], BF16).ap()

    with contextlib.ExitStack() as stack:
        ctx = Ctx(nc, p, stack)
        bank = [ctx.psum(f"bank{i}", [128, 512], F32) for i in range(7)]
        bkt = ctx.psum("bankT", [128, 1024], BF16)
        BK = [("bank", i) for i in range(7)]
        ones_b = ctx.sbuf("ones_b", [1, 512], BF16)
        p.add("vector", lambda v: v.memset(ones_b[:, :], 1.0), writes=["ones_b"], tag="onesb")
        ev_f = [ctx.sbuf(f"ev_f{j}", [128, 512], F32) for j in range(4)]

        with ctx.scope():
            tt_s = ctx.sbuf("tt_s", [128, 8 * NCH], F32)
            to_s = ctx.sbuf("to_s", [128, NCH], F32)
            mlt_s = ctx.sbuf("mlt_s", [128, 8], F32)
            mex_s = ctx.sbuf("mex_s", [128, 64], F32)
            ttot = ctx.sbuf("ttot", [128, 8], F32)
            tmp8 = ctx.sbuf("tmp8", [128, 8], F32)
            wl = ctx.sbuf("wl", [128, 8], F32)
            wfac = ctx.sbuf("wfac", [128, 8], F32)
            pic = ctx.sbuf("pic", [128, NCH], F32)
            cum = ctx.sbuf("cum", [128, NCH], F32)
            zz = ctx.sbuf("zz", [128, NCH], F32)
            sacc = [ctx.sbuf(f"sacc{j}", [128, DIN], F32) for j in range(2)]
            sld = [ctx.sbuf(f"sld{j}", [128, DIN], F32) for j in range(2)]
            p.add("sync", lambda sy: sy.dma_start(out=tt_s[:, :], in_=tallT[:, :]),
                  writes=["tt_s"], dma=True, tag="ld_tt")
            p.add("sync", lambda sy: sy.dma_start(out=to_s[:, :], in_=townT[:, :]),
                  writes=["to_s"], dma=True, tag="ld_to")
            p.add("sync", lambda sy: sy.dma_start(out=mlt_s[:, :], in_=m_lt[:, :]),
                  writes=["mlt_s"], dma=True, tag="ld_mlt")
            p.add("sync", lambda sy: sy.dma_start(out=mex_s[:, :], in_=mexp[:, :]),
                  writes=["mex_s"], dma=True, tag="ld_mex")
            ttv = tt_s[:, :].rearrange("p (j c) -> p j c", c=NCH)
            p.add("vector", lambda v: v.tensor_tensor(
                ttot[:, :], ttv[:, :, 0], ttv[:, :, 1], ALU.add),
                reads=["tt_s"], writes=["ttot"], tag="tt0")
            p.add("vector", lambda v: v.tensor_tensor(
                ttot[:, :], ttot[:, :], ttv[:, :, 2], ALU.add),
                reads=["tt_s", "ttot"], writes=["ttot"], tag="tt1")
            p.add("vector", lambda v: v.tensor_tensor(
                ttot[:, :], ttot[:, :], ttv[:, :, 3], ALU.add),
                reads=["tt_s", "ttot"], writes=["ttot"], tag="tt2")
            for j in range(8):
                p.add("vector", lambda v, j=j: v.tensor_tensor(
                    tmp8[:, :], ttot[:, :], mex_s[:, j * 8:(j + 1) * 8], ALU.mult),
                    reads=["ttot", "mex_s"], writes=["tmp8"], tag=f"wm{j}")
                p.add("vector", lambda v, j=j: v.tensor_reduce(
                    wl[:, j:j + 1], tmp8[:, :], AX.X, ALU.add),
                    reads=["tmp8"], writes=["wl"], tag=f"ws{j}")
            p.add("scalar", lambda sc: sc.activation(wfac[:, :], wl[:, :], AF.Exp),
                  reads=["wl"], writes=["wfac"], tag="we")
            p.add("vector", lambda v: v.tensor_tensor(
                wfac[:, :], wfac[:, :], mlt_s[:, :], ALU.mult),
                reads=["wfac", "mlt_s"], writes=["wfac"], tag="wmk")
            for j in range(8):
                sj = j % 2
                p.add("sync", lambda sy, j=j, sj=sj: sy.dma_start(
                    out=sld[sj][:, :], in_=sall[:, j * DIN:(j + 1) * DIN]),
                    writes=[("sld", sj)], dma=True, tag=f"lds{j}")
                if j == 0:
                    p.add("vector", lambda v, sj=sj: v.tensor_scalar(
                        sacc[0][:, :], sld[sj][:, :], wfac[:, 0:1], 0.0, ALU.mult, ALU.add),
                        reads=[("sld", sj), "wfac"], writes=[("sacc", 0)], tag="sg0")
                else:
                    p.add("vector", lambda v, j=j, sj=sj: v.scalar_tensor_tensor(
                        sacc[j % 2][:, :], sld[sj][:, :], wfac[:, j:j + 1],
                        sacc[(j - 1) % 2][:, :], ALU.mult, ALU.add),
                        reads=[("sld", sj), "wfac", ("sacc", (j - 1) % 2)],
                        writes=[("sacc", j % 2)], tag=f"sg{j}")
            p.add("vector", lambda v: v.memset(zz[:, :], 0.0), writes=["zz"], tag="zz")
            p.add("vector", lambda v: v.tensor_tensor_scan(
                cum[:, :], to_s[:, :], zz[:, :], 0.0, ALU.add, ALU.add),
                reads=["to_s", "zz"], writes=["cum"], tag="cumt")
            p.add("vector", lambda v: v.memset(pic[:, 0:1], 1.0), writes=["pic"], tag="pic0")
            p.add("scalar", lambda sc: sc.activation(
                pic[:, 1:4], cum[:, 0:3], AF.Exp),
                reads=["cum"], writes=["pic"], tag="pice")
            for c in range(NCH):
                sj = c % 2
                p.add("vector", lambda v, c=c, sj=sj: v.tensor_scalar(
                    sld[sj][:, :], sacc[1][:, :], pic[:, c:c + 1], 0.0, ALU.mult, ALU.add),
                    reads=[("sacc", 1), "pic"], writes=[("sld", sj)], tag=f"si{c}")
                if c > 0:
                    p.add("vector", lambda v, c=c: v.tensor_tensor(
                        sld[:, :], sld[:, :],
                        sloc_h[0:0, 0:0] if False else sld[:, :], ALU.add),
                        reads=["sld"], writes=["sld"], tag=f"dummy{c}") if False else None
                    p.add("sync", lambda sy, c=c: sy.dma_start(
                        out=sacc[0][:, :], in_=sloc_h[:, (c - 1) * DIN:c * DIN]),
                        writes=[("sacc", 0)], dma=True, tag=f"ldsl{c}")
                    p.add("vector", lambda v, sj=sj: v.tensor_tensor(
                        sld[sj][:, :], sld[sj][:, :], sacc[0][:, :], ALU.add),
                        reads=[("sld", sj), ("sacc", 0)], writes=[("sld", sj)], tag=f"sia{c}")
                p.add("gpsimd", lambda g, c=c, sj=sj: g.dma_start(
                    out=sinh_d[c * NH:(c + 1) * NH, :], in_=sld[sj][:, :]),
                    reads=[("sld", sj)], writes=[("sinh", c)], dma=True, tag=f"ssi{c}")

        with ctx.scope():
            cfm = ctx.sbuf("cfm", [128, 8 * T_OWN], BF16)
            for gi in range(8):
                p.add("sync", lambda sy, gi=gi: sy.dma_start(
                    out=cfm[:, gi * T_OWN:(gi + 1) * T_OWN],
                    in_=cfm_in[gi * 128:(gi + 1) * 128, :]),
                    writes=[("cfm", gi)], dma=True, tag=f"ldc{gi}")
            eacs = ctx.sbuf("eacs", [128, 8 * 128], F32)
            acst = ctx.sbuf("acst", [128, 8 * 128], F32)
            for tt in range(8):
                p.add("sync", lambda sy, tt=tt: sy.dma_start(
                    out=acst[:, tt * 128:(tt + 1) * 128],
                    in_=acs_in[tt * 128:(tt + 1) * 128, :]),
                    writes=[("acst", tt)], dma=True, tag=f"lda{tt}")
            p.add("scalar", lambda sc: sc.activation(eacs[:, :], acst[:, :], AF.Exp),
                  reads=[("acst", tt) for tt in range(8)], writes=["eacs"], tag="eacs")
            norw_s = ctx.sbuf("norw_s", [128, DIN], F32)
            p.add("sync", lambda sy: sy.dma_start(out=norw_s[:, :], in_=norw_b[:, :]),
                  writes=["norw_s"], dma=True, tag="ld_nw")
            sraw = [ctx.sbuf(f"sraw{j}", [128, 512], F32) for j in range(2)]
            sbf = [ctx.sbuf(f"sbf{j}", [128, 512], BF16) for j in range(2)]
            ybuf = [ctx.sbuf(f"ybuf{it}", [128, DIN], F32) for it in range(2)]
            gbuf = ctx.sbuf("gbuf", [128, DIN], F32)
            yrow = [ctx.sbuf(f"yrow{j}", [128, 512], F32) for j in range(4)]
            t1 = [ctx.sbuf(f"t1_{j}", [128, 512], F32) for j in range(4)]
            vcol = ctx.sbuf("vcol", [128, 8], F32)
            sqs = ctx.sbuf("sqs", [128, GSZ], F32)
            ynb = ctx.sbuf("ynb", [128, DIN], BF16)
            for c in range(NCH):
                for hs in range(16):
                    j = hs % 2
                    p.add("gpsimd", lambda g, c=c, hs=hs, j=j: g.dma_start(
                        out=sraw[j][:, :].rearrange("p (h d) -> p h d", h=8),
                        in_=sinh_d[c * NH + hs * 8:c * NH + (hs + 1) * 8, :].rearrange(
                            "h (n d) -> n h d", n=NST)),
                        reads=[("sinh", c)], writes=[("sraw", j)], dma=True,
                        tag=f"lsi{c}_{hs}")
                    p.add("vector", lambda v, j=j: v.tensor_copy(sbf[j][:, :], sraw[j][:, :]),
                          reads=[("sraw", j)], writes=[("sbf", j)], tag=f"cbf{c}_{hs}")
                    for it in range(2):
                        bi = (hs % 2) * 2 + it
                        for slot in range(8):
                            h = hs * 8 + slot
                            gi = h // 16
                            p.add("tensor", lambda te, c=c, it=it, gi=gi, slot=slot, bi=bi, j=j: te.matmul(
                                bank[bi][:, slot * 64:(slot + 1) * 64],
                                cfm[:, gi * T_OWN + c * CS + it * 128: gi * T_OWN + c * CS + (it + 1) * 128],
                                sbf[j][:, slot * 64:(slot + 1) * 64],
                                start=(slot == 0), stop=(slot == 7)),
                                reads=[("cfm", gi), ("sbf", j)], writes=[BK[bi]],
                                tag=f"yo{c}_{it}_{hs}_{slot}")
                        jr = (hs % 2) * 2 + it
                        p.add("scalar", lambda sc, c=c, it=it, hs=hs, jr=jr: sc.dma_start(
                            out=yrow[jr][:, :],
                            in_=ypart[c * CS + it * 128: c * CS + (it + 1) * 128,
                                      hs * 512:(hs + 1) * 512]),
                            writes=[("yrow", jr)], dma=True, tag=f"lyp{c}_{it}_{hs}")
                        exv = eacs[:, (2 * c + it) * 128 + hs * 8:(2 * c + it) * 128 + (hs + 1) * 8] \
                            .unsqueeze(-1).broadcast_to([128, 8, 64])
                        p.add("vector", lambda v, bi=bi, exv=exv, jr=jr: v.tensor_tensor(
                            t1[jr][:, :].rearrange("p (a b) -> p a b", a=8),
                            bank[bi][:, :].rearrange("p (a b) -> p a b", a=8),
                            exv, ALU.mult),
                            reads=[BK[bi], "eacs"], writes=[("t1", jr)], tag=f"ysc{c}_{it}_{hs}")
                        p.add("vector", lambda v, hs=hs, it=it, jr=jr: v.tensor_tensor(
                            ybuf[it][:, hs * 512:(hs + 1) * 512], t1[jr][:, :],
                            yrow[jr][:, :], ALU.add),
                            reads=[("t1", jr), ("yrow", jr)],
                            writes=[("ybuf", it, hs), ("ybufE", it)],
                            tag=f"yad{c}_{it}_{hs}")
                for it in range(2):
                    YB = [("ybuf", it, hs) for hs in range(16)] + [("ybufE", it)]
                    p.add("gpsimd", lambda g, c=c, it=it: g.dma_start(
                        out=gbuf[:, :],
                        in_=gate[c * CS + it * 128: c * CS + (it + 1) * 128, :]),
                        writes=["gbuf"], dma=True, tag=f"lg{c}_{it}")
                    p.add("vector", lambda v, it=it: v.tensor_tensor(
                        ybuf[it][:, :], ybuf[it][:, :], gbuf[:, :], ALU.mult),
                        reads=YB + ["gbuf"], writes=YB, tag=f"ygm{c}_{it}")
                    for gi in range(8):
                        p.add("scalar", lambda sc, gi=gi, it=it: sc.activation(
                            sqs[:, :], ybuf[it][:, gi * GSZ:(gi + 1) * GSZ], AF.Square,
                            accum_out=vcol[:, gi:gi + 1]),
                            reads=YB, writes=["sqs", "vcol"], tag=f"ysq{c}_{it}_{gi}")
                    p.add("vector", lambda v: v.tensor_scalar(
                        vcol[:, :], vcol[:, :], 1.0 / GSZ, EPS, ALU.mult, ALU.add),
                        reads=["vcol"], writes=["vcol"], tag=f"yv{c}_{it}")
                    p.add("scalar", lambda sc: sc.activation(vcol[:, :], vcol[:, :], AF.Sqrt),
                          reads=["vcol"], writes=["vcol"], tag=f"yvs{c}_{it}")
                    p.add("vector", lambda v: v.reciprocal(vcol[:, :], vcol[:, :]),
                          reads=["vcol"], writes=["vcol"], tag=f"yvr{c}_{it}")
                    for gi in range(8):
                        p.add("vector", lambda v, gi=gi, it=it: v.tensor_scalar(
                            ybuf[it][:, gi * GSZ:(gi + 1) * GSZ],
                            ybuf[it][:, gi * GSZ:(gi + 1) * GSZ],
                            vcol[:, gi:gi + 1], 0.0, ALU.mult, ALU.add),
                            reads=YB + ["vcol"], writes=YB, tag=f"yn{c}_{it}_{gi}")
                    p.add("vector", lambda v, it=it: v.tensor_tensor(
                        ynb[:, :], ybuf[it][:, :], norw_s[:, :], ALU.mult),
                        reads=YB + ["norw_s"], writes=["ynb"], tag=f"ynw{c}_{it}")
                    p.add("gpsimd", lambda g, c=c, it=it: g.dma_start(
                        out=ynorm_d[c * CS + it * 128: c * CS + (it + 1) * 128, :],
                        in_=ynb[:, :]),
                        reads=["ynb"], writes=[("ynd", 2 * c + it)], dma=True,
                        tag=f"syn{c}_{it}")
        YNK = [("ynd", i) for i in range(8)]

        with ctx.scope():
            yfm = ctx.sbuf("yfm", [128, 64 * 512], BF16)
            wcol = [ctx.sbuf(f"wcol{j}", [128, DM], BF16) for j in range(4)]
            opacc = [ctx.sbuf(f"opacc{t}", [128, DM], F32) for t in range(4)]
            x2_t = [ctx.sbuf(f"x2_t{j}", [128, DM], F32) for j in range(1)]
            xo_t = [ctx.sbuf(f"xo_t{j}", [128, DM], F32) for j in range(1)]
            for tg0 in (0, 4):
                for ch in range(64):
                    p.add("sync", lambda sc, ch=ch, tg0=tg0: sc.dma_start(
                        out=yfm[:, ch * 512:(ch + 1) * 512],
                        in_=ynorm_d[tg0 * 128:(tg0 + 4) * 128, ch * 128:(ch + 1) * 128],
                        transpose=True),
                        reads=YNK, writes=[("yfm", ch)], dma=True, tag=f"tyf{tg0}_{ch}")
                for cg in range(16):
                    for cc in range(4):
                        ch = cg * 4 + cc
                        p.add("sync", lambda sy, ch=ch, cc=cc: sy.dma_start(
                            out=wcol[cc][:, :],
                            in_=wop[ch * 128:(ch + 1) * 128, :]),
                            writes=[("wcol", cc)], dma=True, tag=f"two{tg0}_{cg}_{cc}")
                    for bi in range(4):
                        for f5 in range(8):
                            for cc in range(4):
                                ch = cg * 4 + cc
                                p.add("tensor", lambda te, ch=ch, cc=cc, bi=bi, f5=f5: te.matmul(
                                    bank[bi][:, :],
                                    yfm[:, ch * 512 + bi * 128: ch * 512 + (bi + 1) * 128],
                                    wcol[cc][:, f5 * 512:(f5 + 1) * 512],
                                    start=(cc == 0), stop=(cc == 3)),
                                    reads=[("wcol", cc), ("yfm", ch)], writes=[BK[bi]],
                                    tag=f"om{tg0}_{cg}_{bi}_{f5}_{cc}")
                            if cg == 0:
                                p.add("scalar", lambda sc, bi=bi, f5=f5: sc.activation(
                                    opacc[bi][:, f5 * 512:(f5 + 1) * 512],
                                    bank[bi][:, :], AF.Copy),
                                    reads=[BK[bi]], writes=[("opacc", bi)],
                                    tag=f"oe{tg0}_{cg}_{bi}_{f5}")
                            else:
                                p.add("vector", lambda v, bi=bi, f5=f5: v.tensor_tensor(
                                    opacc[bi][:, f5 * 512:(f5 + 1) * 512],
                                    bank[bi][:, :],
                                    opacc[bi][:, f5 * 512:(f5 + 1) * 512], ALU.add),
                                    reads=[BK[bi], ("opacc", bi)], writes=[("opacc", bi)],
                                    tag=f"oa{tg0}_{cg}_{bi}_{f5}")
                for bi in range(4):
                    t = tg0 + bi
                    j = bi % 2
                    p.add("scalar", lambda sc, t=t: sc.dma_start(
                        out=x2_t[0][:, :], in_=x2_in[t * 128:(t + 1) * 128, :]),
                        writes=[("x2_t", 0)], dma=True, tag=f"lx2{tg0}_{t}")
                    p.add("vector", lambda v, bi=bi: v.tensor_tensor(
                        xo_t[0][:, :], opacc[bi][:, :], x2_t[0][:, :], ALU.add),
                        reads=[("opacc", bi), ("x2_t", 0)], writes=[("xo_t", 0)],
                        tag=f"or{tg0}_{bi}")
                    p.add("gpsimd", lambda g, t=t: g.dma_start(
                        out=x3_d[t * 128:(t + 1) * 128, :], in_=xo_t[0][:, :]),
                        reads=[("xo_t", 0)], writes=[("x3d", t)], dma=True,
                        tag=f"sx3{tg0}_{t}")
        X3K = [("x3d", t) for t in range(8)]

        XN4K = [("xn4d", i) for i in range(8)]
        with ctx.scope():
            emit_ln_stream(ctx, x3_d, xn4_d, "xn4d", T_OWN, "ln4", src_keys=X3K)
            xn4fm = ctx.sbuf("xn4fm", [128, 32 * T_OWN], BF16)
            for cc_i in range(32):
                p.add("sync", lambda sc, cc_i=cc_i: sc.dma_start(
                    out=xn4fm[:, cc_i * T_OWN:(cc_i + 1) * T_OWN],
                    in_=xn4_d[:, cc_i * 128:(cc_i + 1) * 128], transpose=True),
                    reads=XN4K, writes=[("xn4fm", cc_i)], dma=True, tag=f"t4_{cc_i}")
            fb_b = ctx.sbuf("fb_b", [1, DM], BF16)
            p.add("gpsimd", lambda g: g.dma_start(out=fb_b[:, :], in_=fbias[:, :]),
                  writes=["fb_b"], dma=True, tag="ld_fb")
            wf_t = [ctx.sbuf(f"wf_t{j}", [128, 512], BF16) for j in range(8)]
            xr_t = [ctx.sbuf(f"xr_t{j}", [128, 512], F32) for j in range(4)]
            for tg in (range(0, 4), range(4, 8)):
                for og in range(8):
                    for cc_i in range(32):
                        j = cc_i % 8
                        p.add("sync", lambda sy, og=og, cc_i=cc_i, j=j: sy.dma_start(
                            out=wf_t[j][:, :],
                            in_=wffn[cc_i * 128:(cc_i + 1) * 128, og * 512:(og + 1) * 512]),
                            writes=[("f2wf", j)], dma=True, tag=f"f2w{tg[0]}_{og}_{cc_i}")
                        for bi, qt in enumerate(tg):
                            p.add("tensor", lambda te, cc_i=cc_i, j=j, bi=bi, qt=qt: te.matmul(
                                bank[bi][:, :],
                                xn4fm[:, cc_i * T_OWN + qt * 128: cc_i * T_OWN + (qt + 1) * 128],
                                wf_t[j][:, :], start=(cc_i == 0), stop=False),
                                reads=[("f2wf", j), ("xn4fm", cc_i)], writes=[BK[bi]],
                                tag=f"f2m{og}_{qt}_{cc_i}")
                    for bi, qt in enumerate(tg):
                        p.add("tensor", lambda te, bi=bi, og=og: te.matmul(
                            bank[bi][:, :], ones_b[:, 0:128],
                            fb_b[:, og * 512:(og + 1) * 512], start=False, stop=True),
                            reads=["ones_b", "fb_b"], writes=[BK[bi]], tag=f"f2b{og}_{qt}")
                        j = bi % 2
                        p.add("scalar", lambda sc, qt=qt, og=og, j=j: sc.dma_start(
                            out=xr_t[j][:, :],
                            in_=x3_d[qt * 128:(qt + 1) * 128, og * 512:(og + 1) * 512]),
                            reads=X3K, writes=[("f2x", j)], dma=True, tag=f"f2l{og}_{qt}")
                        p.add("vector", lambda v, bi=bi, j=j: v.tensor_tensor(
                            ev_f[j][:, :], bank[bi][:, :], xr_t[j][:, :], ALU.add),
                            reads=[BK[bi], ("f2x", j)], writes=[("ev_f", j)],
                            tag=f"f2r{og}_{qt}")
                        p.add("gpsimd", lambda g, qt=qt, og=og, j=j: g.dma_start(
                            out=out[qt * 128:(qt + 1) * 128, og * 512:(og + 1) * 512],
                            in_=ev_f[j][:, :]),
                            reads=[("ev_f", j)], writes=[("out", qt, og)], dma=True,
                            tag=f"f2s{og}_{qt}")
        p.emit(nc)
    return nc




def kernel(**inputs):
    x = np.asarray(inputs["L0_attn_a_x"])
    Wqkv = np.asarray(inputs["L0_attn_a_Wqkv"])
    bqkv = np.asarray(inputs["L0_attn_a_bqkv"])
    Wo = np.asarray(inputs["L0_attn_a_Wo"])
    bo = np.asarray(inputs["L0_attn_a_bo"])
    inw = np.asarray(inputs["L0_attn_b_in_proj_w"])
    convw = np.asarray(inputs["L0_attn_b_conv_w"])
    convb = np.asarray(inputs["L0_attn_b_conv_b"])
    dtb = np.asarray(inputs["L0_attn_b_dt_bias"])
    alog = np.asarray(inputs["L0_attn_b_A_log"])
    Dvec = np.asarray(inputs["L0_attn_b_D"])
    normw = np.asarray(inputs["L0_attn_b_norm_w"])
    outw = np.asarray(inputs["L0_attn_b_out_proj_w"])
    q3 = np.asarray(inputs["L0_ffn_weight_q3"])
    wnorm = np.asarray(inputs["L0_ffn_weight_norm"])
    fbias = np.asarray(inputs["L0_ffn_bias"])

    x2d = np.ascontiguousarray(x[0])

    nc1 = build_L1()
    in_maps = []
    for i in range(NCORES):
        sl = slice(i * T_OWN, (i + 1) * T_OWN)
        in_maps.append({
            "x0": np.ascontiguousarray(x2d[sl]),
            "wqkv": Wqkv,
            "bqkv": np.ascontiguousarray(bqkv.reshape(1, -1)),
            "q3": np.ascontiguousarray(
                q3.view(np.uint8).reshape(512 * 128 * NCORES, 12)[
                    i * 512 * 128:(i + 1) * 512 * 128]),
            "wnorm": np.ascontiguousarray(
                wnorm.reshape(-1, 1)[i * 512 * 128:(i + 1) * 512 * 128]),
            "inproj_f32": np.ascontiguousarray(
                inw[i * (PROJ // 8):(i + 1) * (PROJ // 8)]),
            "outproj_f32": np.ascontiguousarray(
                outw[i * (DM // 8):(i + 1) * (DM // 8)]),
            "id_bf": np.eye(128).astype(BF),
        })
    res1 = run_bass_kernel_spmd(nc1, in_maps, core_ids=list(range(NCORES)))
    r1 = res1.results
    Q_fm = np.concatenate([r1[i]["q_out"] for i in range(NCORES)], axis=1)
    K_fm = np.concatenate([r1[i]["k_out"] for i in range(NCORES)], axis=1)
    V_tm = np.concatenate([r1[i]["v_out"] for i in range(NCORES)], axis=0)
    w_ffn = np.concatenate([r1[i]["wffnt_out"] for i in range(NCORES)], axis=1)
    inproj_b = np.concatenate([r1[i]["ipt_out"] for i in range(NCORES)], axis=1)
    outproj_b = np.concatenate([r1[i]["opt_out"] for i in range(NCORES)], axis=1)

    nc2 = build_L2()
    trim0 = np.triu(np.ones((128, 256), np.float32))
    trim1 = np.triu(np.ones((128, 256), np.float32), k=128)
    identb_np = np.eye(128).astype(BF)
    identf_np = np.eye(128).astype(np.float32)
    convw_rr = np.ascontiguousarray(
        convw[:, 0, :].reshape(80, 128, 4).transpose(1, 0, 2).reshape(128, 320))
    convb_rr = np.ascontiguousarray(convb.reshape(80, 128).T)
    dexp_np = np.ascontiguousarray(
        np.broadcast_to(np.repeat(Dvec, HD)[None, :], (128, DIN))).astype(np.float32)
    in_maps2 = []
    for i in range(NCORES):
        s = i * T_OWN
        x0ext = np.zeros((T_EXT, DM), np.float32)
        if i > 0:
            x0ext[0:HALO] = x2d[s - HALO:s]
        x0ext[HALO:] = x2d[s:s + T_OWN]
        qx = np.zeros((DH, T_EXT), BF)
        if i > 0:
            qx[:, 0:HALO] = Q_fm[:, s - HALO:s]
        qx[:, HALO:] = Q_fm[:, s:s + T_OWN]
        hmask = np.ones((128, 4), np.float32)
        if i == 0:
            hmask[:, 0:3] = 0.0
        in_maps2.append({
            "x0e": x0ext,
            "q_ext": np.ascontiguousarray(qx),
            "k_fm": K_fm, "v_tm": V_tm,
            "wo": Wo, "bo": np.ascontiguousarray(bo.reshape(1, -1)),
            "wffn": w_ffn, "fbias": np.ascontiguousarray(fbias.reshape(1, -1)),
            "ipbf": inproj_b,
            "convw_r": convw_rr, "convb_r": convb_rr,
            "dtb_col": np.ascontiguousarray(dtb.reshape(128, 1)),
            "alog_col": np.ascontiguousarray(alog.reshape(128, 1)),
            "d_exp": dexp_np,
            "halo_m": hmask,
            "tri0": trim0, "tri1": trim1,
            "id_bf": identb_np, "id_f": identf_np,
        })
    res2 = run_bass_kernel_spmd(nc2, in_maps2, core_ids=list(range(NCORES)))
    r2 = res2.results

    nc3 = build_L3()
    s_all_h = np.stack([r2[j]["s_out"].reshape(NST, NH, HD).transpose(1, 0, 2).reshape(NH, NST * HD)
                        for j in range(NCORES)], axis=1).reshape(NH, NCORES * NST * HD)
    t_all = np.concatenate([r2[j]["t_out"] for j in range(NCORES)], axis=1)
    norw_np = np.ascontiguousarray(
        np.broadcast_to(normw.reshape(1, DIN), (128, DIN))).astype(np.float32)
    in_maps3 = []
    for i in range(NCORES):
        slh = r2[i]["sloc_out"].reshape(3, NST, NH, HD).transpose(2, 0, 1, 3).reshape(NH, 3 * NST * HD)
        mlt = np.zeros((128, 8), np.float32)
        mlt[:, :i] = 1.0
        mex = np.zeros((8, 8), np.float32)
        for j in range(i):
            for k2 in range(j + 1, i):
                mex[j, k2] = 1.0
        in_maps3.append({
            "x2_in": np.ascontiguousarray(r2[i]["x2_out"][HALO:HALO + T_OWN]),
            "ypart": r2[i]["y_part"],
            "gate": r2[i]["gate_out"],
            "cfm_in": r2[i]["c_out"],
            "acs_in": r2[i]["acs_out"],
            "sall": np.ascontiguousarray(s_all_h),
            "sloc_h": np.ascontiguousarray(slh),
            "townT": np.ascontiguousarray(r2[i]["t_out"]),
            "tallT": np.ascontiguousarray(t_all),
            "m_lt": mlt,
            "mexp": np.ascontiguousarray(
                np.broadcast_to(mex.reshape(1, 64), (128, 64))).astype(np.float32),
            "wffn": w_ffn,
            "fbias": np.ascontiguousarray(fbias.reshape(1, -1)),
            "wop": outproj_b,
            "norw_b": norw_np,
        })
    res3 = run_bass_kernel_spmd(nc3, in_maps3, core_ids=list(range(NCORES)))
    r3 = res3.results
    out = np.concatenate([r3[i]["out"] for i in range(NCORES)], axis=0)
    return out.reshape(1, 8192, 4096)


_build_L1 = build_L1


# version 10
# speedup vs baseline: 1.2473x; 1.1253x over previous
import contextlib
import numpy as np
import ml_dtypes
import concourse.bass as bass
import concourse.mybir as mybir
from concourse.bass_utils import run_bass_kernel_spmd

F32 = mybir.dt.float32
BF16 = mybir.dt.bfloat16
I32 = mybir.dt.int32
U8 = mybir.dt.uint8
AF = mybir.ActivationFunctionType
ALU = mybir.AluOpType
AX = mybir.AxisListType
BF = ml_dtypes.bfloat16

NCORES = 8
T_OWN = 1024
HALO = 3
T_EXT = T_OWN + HALO
DM = 4096
DH = 1024
NH = 128
HD = 64
NST = 128
G = 8
CS = 256
NCH = T_OWN // CS
DIN = NH * HD
CONV_DIM = DIN + 2 * G * NST
PROJ = DIN + CONV_DIM + NH
EPS = 1e-5
GSZ = DIN // G

ENGINES = ("tensor", "vector", "scalar", "gpsimd", "sync")
DMA_QUEUES = ("sync", "scalar", "gpsimd", "vector")


class Op:
    __slots__ = ("eng", "fn", "reads", "writes", "dma", "idx", "deps",
                 "need_inc", "inc_val", "tag", "lane")

    def __init__(self, eng, fn, reads, writes, dma, idx, tag):
        self.eng, self.fn = eng, fn
        self.reads, self.writes = reads, writes
        self.dma, self.idx, self.tag = dma, idx, tag
        self.deps = []
        self.need_inc = False
        self.inc_val = None
        self.lane = None


class Prog:

    def __init__(self):
        self.ops = []
        self.last_write = {}
        self.readers = {}

    def barrier(self):
        if not hasattr(self, "last_eng"):
            self.last_eng = {e: None for e in ENGINES}
        if not hasattr(self, "last_dma"):
            self.last_dma = {}
        last = dict(self.last_eng)
        for k, d in self.last_dma.items():
            if d is not None:
                last[("dma",) + k] = d
        nops = []
        for e in ENGINES:
            op = Op(e, lambda eng: eng.nop(), (), (), False, len(self.ops), f"bar_{e}")
            for d in last.values():
                if d is not None:
                    op.deps.append(d)
            self.ops.append(op)
            self.last_eng[e] = op
            nops.append(op)
        self.barrier_ops = {op.eng: op for op in nops}

    def add(self, eng, fn, reads=(), writes=(), dma=False, tag=""):
        op = Op(eng, fn, tuple(reads), tuple(writes), dma, len(self.ops), tag)
        if not hasattr(self, "last_eng"):
            self.last_eng = {e: None for e in ENGINES}
        if not hasattr(self, "last_dma"):
            self.last_dma = {}
        if not hasattr(self, "barrier_ops"):
            self.barrier_ops = {}
        if not hasattr(self, "qcnt"):
            self.qcnt = {e: 0 for e in ENGINES}
            self.lane_cnt = {}
        if dma:
            i = self.qcnt[eng]
            self.qcnt[eng] += 1
            op.lane = i % self.NLANES
            seq = self.lane_cnt.get((eng, op.lane), 0) + 1
            self.lane_cnt[(eng, op.lane)] = seq
            op.inc_val = 16 * seq
            op.need_inc = True
        deps = {}
        b = self.barrier_ops.get(eng)
        if b is not None:
            deps[b.idx] = b
        for k in op.reads:
            w = self.last_write.get(k)
            if w is not None:
                deps[w.idx] = w
        for k in op.writes:
            w = self.last_write.get(k)
            if w is not None:
                deps[w.idx] = w
            for r in self.readers.get(k, ()):
                deps[r.idx] = r
        for d in deps.values():
            if d.eng == "tensor" and op.eng == "tensor" and not d.dma and not op.dma:
                continue
            op.deps.append(d)
        for k in op.reads:
            self.readers.setdefault(k, []).append(op)
        for k in op.writes:
            self.last_write[k] = op
            self.readers[k] = []
        self.last_eng[eng] = op
        if dma:
            self.last_dma[(eng, op.lane)] = op
        self.ops.append(op)
        return op

    NLANES = 8

    def emit(self, nc):
        for op in self.ops:
            for d in op.deps:
                d.need_inc = True
        csem_count = {e: 0 for e in ENGINES}
        lane_cnt = getattr(self, "lane_cnt", {})
        for op in self.ops:
            if op.dma:
                pass
            elif op.need_inc:
                csem_count[op.eng] += 1
                op.inc_val = csem_count[op.eng]
        with contextlib.ExitStack() as stack:
            csem = {e: stack.enter_context(nc.semaphore(f"c_{e}")) for e in ENGINES}
            qsem = {(e, l): stack.enter_context(nc.semaphore(f"q_{e}{l}"))
                    for e in DMA_QUEUES for l in range(self.NLANES)}
            block = stack.enter_context(nc.Block())
            per_eng = {e: [] for e in ENGINES}
            for op in self.ops:
                per_eng[op.eng].append(op)

            def emit_engine(eng_name, eng):
                watermark = {}
                for op in per_eng[eng_name]:
                    needed = {}
                    for d in op.deps:
                        wk = (d.eng, d.lane if d.dma else None)
                        needed[wk] = max(needed.get(wk, 0), d.inc_val)
                    if op.dma and op.inc_val > 16:
                        wk = (op.eng, op.lane)
                        needed[wk] = max(needed.get(wk, 0), op.inc_val - 16)
                    for (deng, dlane), val in needed.items():
                        if watermark.get((deng, dlane), -1) >= val:
                            continue
                        sem = csem[deng] if dlane is None else qsem[(deng, dlane)]
                        eng.wait_ge(sem, val)
                        watermark[(deng, dlane)] = val
                    inst = op.fn(eng)
                    if op.need_inc:
                        if inst is None:
                            raise RuntimeError(f"op {op.tag}: fn returned None")
                        sem = qsem[(op.eng, op.lane)] if op.dma else csem[op.eng]
                        inst.then_inc(sem, 16 if op.dma else 1)
                if eng_name in DMA_QUEUES:
                    for l in range(self.NLANES):
                        tot = lane_cnt.get((eng_name, l), 0)
                        if tot and watermark.get((eng_name, l), -1) < 16 * tot:
                            eng.wait_ge(qsem[(eng_name, l)], 16 * tot)

            @block.tensor
            def _(te):
                emit_engine("tensor", te)

            @block.vector
            def _(v):
                emit_engine("vector", v)

            @block.scalar
            def _(sc):
                emit_engine("scalar", sc)

            @block.gpsimd
            def _(g):
                emit_engine("gpsimd", g)

            @block.sync
            def _(sy):
                emit_engine("sync", sy)


class Ctx:
    def __init__(self, nc, prog, stack):
        self.nc, self.p, self.stack = nc, prog, stack
        self._stacks = [stack]

    def sbuf(self, name, shape, dtype):
        return self._stacks[-1].enter_context(self.nc.sbuf_tensor(name, shape, dtype))

    def psum(self, name, shape, dtype):
        return self._stacks[-1].enter_context(self.nc.psum_tensor(name, shape, dtype))

    @contextlib.contextmanager
    def scope(self):
        st = contextlib.ExitStack()
        self._stacks.append(st)
        try:
            yield
        finally:
            self._stacks.pop()
            st.close()
            self.p.barrier()




def emit_ln_stream(ctx, src_rows_ap, dst_dram, dst_key, n_rows, tag,
                   also_keep=None, src_keys=None):
    p = ctx.p
    nt = (n_rows + 127) // 128
    xt = [ctx.sbuf(f"{tag}xt{j}", [128, DM], F32) for j in range(2)]
    sq = ctx.sbuf(f"{tag}sq", [128, DM], F32)
    cc = ctx.sbuf(f"{tag}cc", [128, 8], F32)
    xnb = [ctx.sbuf(f"{tag}xnb{j}", [128, DM], BF16) for j in range(2)]
    for i in range(nt):
        j = i % 2
        r = min(128, n_rows - i * 128)
        xk = (f"{tag}xt", j)
        rdeps = [] if src_keys is None else src_keys
        p.add("sync", lambda sy, i=i, j=j, r=r: sy.dma_start(
            out=xt[j][0:r, :], in_=src_rows_ap[i * 128:i * 128 + r, :]),
            reads=rdeps, writes=[xk], dma=True, tag=f"{tag}ld{i}")
        emit_ln_tile(ctx, xt[j], xk, r, xnb[j], (f"{tag}xnb", j), sq,
                     (f"{tag}sq",), cc, (f"{tag}cc",), f"{tag}{i}")
        p.add("gpsimd", lambda g, i=i, j=j, r=r: g.dma_start(
            out=dst_dram[i * 128:i * 128 + r, :], in_=xnb[j][0:r, :]),
            reads=[(f"{tag}xnb", j)], writes=[(dst_key, i)], dma=True,
            tag=f"{tag}st{i}")
        if also_keep is not None:
            also_keep(i, r, xt[j], xk)


def emit_ln_tile(ctx, xap, xkey, r, out_ap, out_key, sq, sqk, cc, cck, tag):
    p = ctx.p
    p.add("vector", lambda v: v.tensor_reduce(
        cc[0:r, 0:1], xap[0:r, :], AX.X, ALU.add),
        reads=[xkey], writes=[cck], tag=f"{tag}_mean")
    p.add("scalar", lambda sc: sc.activation(
        sq[0:r, :], xap[0:r, :], AF.Square, accum_out=cc[0:r, 1:2]),
        reads=[xkey], writes=[sqk, cck], tag=f"{tag}_sq")
    p.add("vector", lambda v: v.tensor_scalar(
        cc[0:r, 2:3], cc[0:r, 0:1], 1.0 / DM, 0.0, ALU.mult, ALU.add),
        reads=[cck], writes=[cck], tag=f"{tag}_m")
    p.add("vector", lambda v: v.tensor_tensor(
        cc[0:r, 3:4], cc[0:r, 2:3], cc[0:r, 2:3], ALU.mult),
        reads=[cck], writes=[cck], tag=f"{tag}_m2")
    p.add("vector", lambda v: v.tensor_scalar(
        cc[0:r, 4:5], cc[0:r, 1:2], 1.0 / DM, EPS, ALU.mult, ALU.add),
        reads=[cck], writes=[cck], tag=f"{tag}_ex2")
    p.add("vector", lambda v: v.tensor_tensor(
        cc[0:r, 5:6], cc[0:r, 4:5], cc[0:r, 3:4], ALU.subtract),
        reads=[cck], writes=[cck], tag=f"{tag}_var")
    p.add("scalar", lambda sc: sc.activation(
        cc[0:r, 6:7], cc[0:r, 5:6], AF.Sqrt),
        reads=[cck], writes=[cck], tag=f"{tag}_sqrt")
    p.add("vector", lambda v: v.reciprocal(
        cc[0:r, 7:8], cc[0:r, 6:7]),
        reads=[cck], writes=[cck], tag=f"{tag}_rstd")
    p.add("vector", lambda v: v.tensor_tensor(
        cc[0:r, 2:3], cc[0:r, 2:3], cc[0:r, 7:8], ALU.mult),
        reads=[cck], writes=[cck], tag=f"{tag}_mr")
    p.add("vector", lambda v: v.tensor_scalar(
        cc[0:r, 2:3], cc[0:r, 2:3], -1.0, 0.0, ALU.mult, ALU.add),
        reads=[cck], writes=[cck], tag=f"{tag}_neg")
    p.add("vector", lambda v: v.tensor_scalar(
        out_ap[0:r, :], xap[0:r, :], cc[0:r, 7:8], cc[0:r, 2:3],
        ALU.mult, ALU.add),
        reads=[xkey, cck], writes=[out_key], tag=f"{tag}_xn")




def build_L1():
    nc = bass.Bass("TRN2", target_bir_lowering=False, debug=False)
    p = Prog()

    x0 = nc.dram_tensor("x0", [T_OWN, DM], F32, kind="ExternalInput").ap()
    wqkv = nc.dram_tensor("wqkv", [DM, 3 * DH], F32, kind="ExternalInput").ap()
    bqkv = nc.dram_tensor("bqkv", [1, 3 * DH], F32, kind="ExternalInput").ap()
    q3 = nc.dram_tensor("q3", [512 * 128, 12], U8, kind="ExternalInput").ap()
    wnorm = nc.dram_tensor("wnorm", [512 * 128, 1], F32, kind="ExternalInput").ap()
    inproj_f32 = nc.dram_tensor("inproj_f32", [PROJ // 8, DM], F32, kind="ExternalInput").ap()
    outproj_f32 = nc.dram_tensor("outproj_f32", [DM // 8, DIN], F32, kind="ExternalInput").ap()

    q_out = nc.dram_tensor("q_out", [DH, T_OWN], BF16, kind="ExternalOutput").ap()
    k_out = nc.dram_tensor("k_out", [DH, T_OWN], BF16, kind="ExternalOutput").ap()
    v_out = nc.dram_tensor("v_out", [T_OWN, DH], BF16, kind="ExternalOutput").ap()
    wffnt_out = nc.dram_tensor("wffnt_out", [DM, 512], BF16, kind="ExternalOutput").ap()
    ipt_out = nc.dram_tensor("ipt_out", [DM, PROJ // 8], BF16, kind="ExternalOutput").ap()
    opt_out = nc.dram_tensor("opt_out", [DIN, DM // 8], BF16, kind="ExternalOutput").ap()
    id_bf = nc.dram_tensor("id_bf", [128, 128], BF16, kind="ExternalInput").ap()

    xn1 = nc.dram_tensor("xn1", [T_OWN, DM], BF16).ap()
    wqkv_bf = nc.dram_tensor("wqkv_bf", [DM, 3 * DH], BF16).ap()

    with contextlib.ExitStack() as stack:
        ctx = Ctx(nc, p, stack)

        def cast_chunks(dst, src, rows, tag, nchunks=4):
            step = (rows + nchunks - 1) // nchunks
            for i in range(0, rows, step):
                r = min(step, rows - i)
                p.add("gpsimd", lambda g, i=i, r=r: g.dma_start(
                    out=dst[i:i + r, :], in_=src[i:i + r, :]),
                    reads=[], writes=[(tag, i)], dma=True, tag=f"{tag}{i}")

        cast_chunks(wqkv_bf, wqkv, DM, "cast_qkv", 4)
        wqkv_cast_keys = [("cast_qkv", i) for i in range(0, DM, DM // 4)]

        with ctx.scope():
            emit_ln_stream(ctx, x0, xn1, "xn1", T_OWN, "ln1")
        xn1_keys = [("xn1", i) for i in range(8)]

        xn1fm = ctx.sbuf("xn1fm", [128, 32 * T_OWN], BF16)
        for cc_i in range(32):
            p.add("sync", lambda sc, cc_i=cc_i: sc.dma_start(
                out=xn1fm[:, cc_i * T_OWN:(cc_i + 1) * T_OWN],
                in_=xn1[:, cc_i * 128:(cc_i + 1) * 128], transpose=True),
                reads=xn1_keys, writes=[("xn1fm", cc_i)], dma=True,
                tag=f"trl_xn1_{cc_i}")
        xn1fm_keys = [("xn1fm", i) for i in range(32)]

        bqb = ctx.sbuf("bqb", [1, 3 * DH], BF16)
        ones = ctx.sbuf("ones", [1, 512], BF16)
        p.add("gpsimd", lambda g: g.dma_start(out=bqb[:, :], in_=bqkv[:, :]),
              reads=[], writes=["bqb"], dma=True, tag="ld_bqb")
        p.add("vector", lambda v: v.memset(ones[:, :], 1.0),
              reads=[], writes=["ones"], tag="ones")

        wtile = [ctx.sbuf(f"wt{j}", [128, 768], BF16) for j in range(4)]
        psb = [ctx.psum(f"ps{j}", [128, 512], F32) for j in range(7)]
        bkt = ctx.psum("bankT", [128, 1024], BF16)
        evb = [ctx.sbuf(f"ev{j}", [128, 512], BF16) for j in range(2)]
        identb = ctx.sbuf("identb", [128, 128], BF16)
        p.add("sync", lambda sy: sy.dma_start(out=identb[:, :], in_=id_bf[:, :]),
              writes=["identb"], dma=True, tag="ld_idb")
        trev = [ctx.sbuf(f"trev{j}", [128, 128], BF16) for j in range(4)]
        f_groups = [list(range(0, 6)), list(range(6, 12)), list(range(12, 16))]
        for th in range(2):
            tsl = slice(th * 512, (th + 1) * 512)
            for gi, fg in enumerate(f_groups):
                f0 = fg[0] * 128
                fw = len(fg) * 128
                for cc_i in range(32):
                    j = cc_i % 4
                    wk = (f"wt", j)
                    p.add("sync", lambda sy, cc_i=cc_i, j=j, f0=f0, fw=fw: sy.dma_start(
                        out=wtile[j][:, 0:fw],
                        in_=wqkv_bf[cc_i * 128:(cc_i + 1) * 128, f0:f0 + fw]),
                        reads=wqkv_cast_keys, writes=[wk], dma=True,
                        tag=f"qk_w{th}_{gi}_{cc_i}")
                    for fi, f in enumerate(fg):
                        bank = fi
                        p.add("tensor", lambda te, cc_i=cc_i, j=j, fi=fi, bank=bank, tsl=tsl: te.matmul(
                            psb[bank][:, :],
                            wtile[j][:, fi * 128:(fi + 1) * 128],
                            xn1fm[:, cc_i * T_OWN:cc_i * T_OWN + T_OWN][:, tsl],
                            start=(cc_i == 0), stop=False),
                            reads=[wk, ("xn1fm", cc_i)], writes=[("bank", bank)],
                            tag=f"qk_mm{th}_{gi}_{cc_i}_{fi}")
                for fi, f in enumerate(fg):
                    bank = fi
                    p.add("tensor", lambda te, f=f, bank=bank: te.matmul(
                        psb[bank][:, :], bqb[:, f * 128:(f + 1) * 128],
                        ones[:, :], start=False, stop=True),
                        reads=["bqb", "ones"], writes=[("bank", bank)],
                        tag=f"qk_bias{th}_{gi}_{fi}")
                for fi, f in enumerate(fg):
                    bank = fi
                    j = fi % 2
                    p.add("scalar", lambda sc, bank=bank, j=j: sc.activation(
                        evb[j][:, :], psb[bank][:, :], AF.Copy),
                        reads=[("bank", bank)], writes=[("ev", j)],
                        tag=f"qk_ev{th}_{gi}_{fi}")
                    dst = q_out if f < 8 else k_out
                    fr = f * 128 if f < 8 else (f - 8) * 128
                    p.add("gpsimd", lambda g, dst=dst, fr=fr, j=j, th=th: g.dma_start(
                        out=dst[fr:fr + 128, th * 512:(th + 1) * 512],
                        in_=evb[j][:, :]),
                        reads=[("ev", j)], writes=[("qkout", f, th)], dma=True,
                        tag=f"qk_st{th}_{gi}_{fi}")

        wvt = [ctx.sbuf(f"wvt{j}", [128, 1024], BF16) for j in range(4)]
        for tp in range(4):
            for cc_i in range(32):
                j = cc_i % 4
                wk = (f"wvt", j)
                p.add("sync", lambda sy, cc_i=cc_i, j=j: sy.dma_start(
                    out=wvt[j][:, :],
                    in_=wqkv_bf[cc_i * 128:(cc_i + 1) * 128, 2048:3072]),
                    reads=wqkv_cast_keys, writes=[wk], dma=True,
                    tag=f"v_w{tp}_{cc_i}")
                for ti in range(2):
                    t = tp * 2 + ti
                    for fh in range(2):
                        bank = ti * 2 + fh
                        p.add("tensor", lambda te, cc_i=cc_i, j=j, t=t, fh=fh, bank=bank: te.matmul(
                            psb[bank][:, :],
                            xn1fm[:, cc_i * T_OWN + t * 128: cc_i * T_OWN + (t + 1) * 128],
                            wvt[j][:, fh * 512:(fh + 1) * 512],
                            start=(cc_i == 0), stop=False),
                            reads=[wk, ("xn1fm", cc_i)], writes=[("bank", bank)],
                            tag=f"v_mm{t}_{fh}_{cc_i}")
            for ti in range(2):
                t = tp * 2 + ti
                for fh in range(2):
                    bank = ti * 2 + fh
                    p.add("tensor", lambda te, fh=fh, bank=bank, t=t: te.matmul(
                        psb[bank][:, :], ones[:, 0:128],
                        bqb[:, 2048 + fh * 512:2048 + (fh + 1) * 512],
                        start=False, stop=True),
                        reads=["bqb", "ones"], writes=[("bank", bank)],
                        tag=f"v_bias{t}_{fh}")
                    j = fh
                    p.add("scalar", lambda sc, bank=bank, j=j: sc.activation(
                        evb[j][:, :], psb[bank][:, :], AF.Copy),
                        reads=[("bank", bank)], writes=[("ev", j)],
                        tag=f"v_ev{t}_{fh}")
                    p.add("gpsimd", lambda g, t=t, fh=fh, j=j: g.dma_start(
                        out=v_out[t * 128:(t + 1) * 128, fh * 512:(fh + 1) * 512],
                        in_=evb[j][:, :]),
                        reads=[("ev", j)], writes=[("vout", t, fh)], dma=True,
                        tag=f"v_st{t}_{fh}")

        with ctx.scope():
            q3v = q3.rearrange("(o g) b -> o (g b)", g=128)
            wnv = wnorm.rearrange("(o g) c -> o (g c)", g=128)
            dqb = ctx.sbuf("dqb", [128, 1536], U8)
            di = [ctx.sbuf(f"di{j}", [128, 512], I32) for j in range(3)]
            dn = ctx.sbuf("dn", [128, 512], I32)
            dq = ctx.sbuf("dq", [128, DM], I32)
            df = ctx.sbuf("df", [128, DM], F32)
            dnrm = ctx.sbuf("dnrm", [128, 128], F32)
            dout = ctx.sbuf("dout", [128, DM], BF16)
            for blk in range(4):
                p.add("sync", lambda sy, blk=blk: sy.dma_start(
                    out=dqb[:, :], in_=q3v[blk * 128:(blk + 1) * 128, :]),
                    reads=[("dq_done", blk - 1)] if blk else [],
                    writes=["dqb"], dma=True, tag=f"dq_ld{blk}")
                p.add("sync", lambda sy, blk=blk: sy.dma_start(
                    out=dnrm[:, :], in_=wnv[blk * 128:(blk + 1) * 128, :]),
                    reads=[], writes=["dnrm"], dma=True, tag=f"dq_ldn{blk}")
                bv = dqb[:, :].rearrange("p (t k) -> p t k", k=3)
                for j in range(3):
                    p.add("vector", lambda v, j=j, bv=bv: v.tensor_copy(
                        di[j][:, :], bv[:, :, j]),
                        reads=["dqb"], writes=[("di", j)], tag=f"dq_b{blk}_{j}")
                p.add("vector", lambda v: v.tensor_scalar(
                    dn[:, :], di[1][:, :], 8, 0, ALU.logical_shift_left, ALU.bitwise_or),
                    reads=[("di", 1)], writes=["dn"], tag=f"dq_s1_{blk}")
                p.add("vector", lambda v: v.tensor_tensor(
                    dn[:, :], dn[:, :], di[0][:, :], ALU.bitwise_or),
                    reads=["dn", ("di", 0)], writes=["dn"], tag=f"dq_o1_{blk}")
                p.add("vector", lambda v: v.tensor_scalar(
                    di[1][:, :], di[2][:, :], 16, 0, ALU.logical_shift_left, ALU.bitwise_or),
                    reads=[("di", 2)], writes=[("di", 1)], tag=f"dq_s2_{blk}")
                p.add("vector", lambda v: v.tensor_tensor(
                    dn[:, :], dn[:, :], di[1][:, :], ALU.bitwise_or),
                    reads=["dn", ("di", 1)], writes=["dn"], tag=f"dq_o2_{blk}")
                dqv = dq[:, :].rearrange("p (t k) -> p t k", k=8)
                for pos in range(8):
                    p.add("vector", lambda v, pos=pos, dqv=dqv: v.tensor_scalar(
                        dqv[:, :, pos], dn[:, :], 3 * pos, 7,
                        ALU.logical_shift_right, ALU.bitwise_and),
                        reads=["dn"], writes=["dq"], tag=f"dq_d{blk}_{pos}")
                p.add("vector", lambda v: v.tensor_scalar(
                    dq[:, :], dq[:, :], 2, -7, ALU.mult, ALU.add),
                    reads=["dq"], writes=["dq"], tag=f"dq_aff{blk}")
                p.add("vector", lambda v: v.tensor_copy(df[:, :], dq[:, :]),
                    reads=["dq"], writes=["df"], tag=f"dq_f{blk}")
                nview = dnrm[:, :].unsqueeze(-1).broadcast_to([128, 128, 32])
                dfv = df[:, :].rearrange("p (a b) -> p a b", a=128)
                p.add("vector", lambda v, nview=nview, dfv=dfv: v.scalar_tensor_tensor(
                    dout[:, :].rearrange("p (a b) -> p a b", a=128),
                    dfv, 1.0 / 7.0, nview, ALU.mult, ALU.mult),
                    reads=["df", "dnrm"], writes=["dout"], tag=f"dq_w{blk}")
                for cb in range(32):
                    tj = cb % 4
                    p.add("tensor", lambda te, cb=cb: te.transpose(
                        bkt[:, 0:128], dout[:, cb * 128:(cb + 1) * 128], identb[:, :]),
                        reads=["dout", "identb"], writes=["bkt"], tag=f"dq_t{blk}_{cb}")
                    p.add("vector", lambda v, tj=tj: v.tensor_copy(
                        trev[tj][:, :], bkt[:, 0:128]),
                        reads=["bkt"], writes=[("trev", tj)], tag=f"dq_tc{blk}_{cb}")
                    p.add("gpsimd", lambda g, blk=blk, cb=cb, tj=tj: g.dma_start(
                        out=wffnt_out[cb * 128:(cb + 1) * 128, blk * 128:(blk + 1) * 128],
                        in_=trev[tj][:, :]),
                        reads=[("trev", tj)], writes=[("dq_done", blk)], dma=True,
                        tag=f"dq_st{blk}_{cb}")

            wtc = [ctx.sbuf(f"wtc{j}", [128, DIN], BF16) for j in range(2)]
            NIP = PROJ // 8
            for ft in range((NIP + 127) // 128):
                r = min(128, NIP - ft * 128)
                j = ft % 2
                p.add("gpsimd", lambda g, ft=ft, r=r, j=j: g.dma_start(
                    out=wtc[j][0:r, 0:DM], in_=inproj_f32[ft * 128:ft * 128 + r, :]),
                    writes=[("wtc", j)], dma=True, tag=f"ipl{ft}")
                for cb in range(32):
                    tj = cb % 4
                    p.add("tensor", lambda te, r=r, j=j, cb=cb: te.transpose(
                        bkt[:, 0:r], wtc[j][0:r, cb * 128:(cb + 1) * 128],
                        identb[0:r, 0:r]),
                        reads=[("wtc", j), "identb"], writes=["bkt"], tag=f"ipt{ft}_{cb}")
                    p.add("vector", lambda v, tj=tj, r=r: v.tensor_copy(
                        trev[tj][:, 0:r], bkt[:, 0:r]),
                        reads=["bkt"], writes=[("trev", tj)], tag=f"iptc{ft}_{cb}")
                    p.add("scalar", lambda v, ft=ft, cb=cb, tj=tj, r=r: v.dma_start(
                        out=ipt_out[cb * 128:(cb + 1) * 128, ft * 128:ft * 128 + r],
                        in_=trev[tj][:, 0:r]),
                        reads=[("trev", tj)], writes=[("ipt_o", ft, cb)], dma=True,
                        tag=f"ips{ft}_{cb}")
            for ft in range(4):
                j = ft % 2
                p.add("gpsimd", lambda g, ft=ft, j=j: g.dma_start(
                    out=wtc[j][:, :], in_=outproj_f32[ft * 128:(ft + 1) * 128, :]),
                    writes=[("wtc", j)], dma=True, tag=f"opl{ft}")
                for cb in range(64):
                    tj = cb % 4
                    p.add("tensor", lambda te, j=j, cb=cb: te.transpose(
                        bkt[:, 0:128], wtc[j][:, cb * 128:(cb + 1) * 128], identb[:, :]),
                        reads=[("wtc", j), "identb"], writes=["bkt"], tag=f"opt{ft}_{cb}")
                    p.add("vector", lambda v, tj=tj: v.tensor_copy(
                        trev[tj][:, :], bkt[:, 0:128]),
                        reads=["bkt"], writes=[("trev", tj)], tag=f"optc{ft}_{cb}")
                    p.add("scalar", lambda v, ft=ft, cb=cb, tj=tj: v.dma_start(
                        out=opt_out[cb * 128:(cb + 1) * 128, ft * 128:(ft + 1) * 128],
                        in_=trev[tj][:, :]),
                        reads=[("trev", tj)], writes=[("opt_o", ft, cb)], dma=True,
                        tag=f"ops{ft}_{cb}")

        p.emit(nc)
    return nc



T_PAD = 1152
NQT = 9
QT_ROWS = [128] * 8 + [3]


def build_L2():
    nc = bass.Bass("TRN2", target_bir_lowering=False, debug=False)
    p = Prog()

    x0e = nc.dram_tensor("x0e", [T_EXT, DM], F32, kind="ExternalInput").ap()
    q_ext = nc.dram_tensor("q_ext", [DH, T_EXT], BF16, kind="ExternalInput").ap()
    k_fm = nc.dram_tensor("k_fm", [DH, 8192], BF16, kind="ExternalInput").ap()
    v_tm = nc.dram_tensor("v_tm", [8192, DH], BF16, kind="ExternalInput").ap()
    wo = nc.dram_tensor("wo", [DH, DM], F32, kind="ExternalInput").ap()
    bo = nc.dram_tensor("bo", [1, DM], F32, kind="ExternalInput").ap()
    wffn = nc.dram_tensor("wffn", [DM, DM], BF16, kind="ExternalInput").ap()
    fbias = nc.dram_tensor("fbias", [1, DM], F32, kind="ExternalInput").ap()
    ipbf = nc.dram_tensor("ipbf", [DM, PROJ], BF16, kind="ExternalInput").ap()
    convw_r = nc.dram_tensor("convw_r", [128, 320], F32, kind="ExternalInput").ap()
    convb_r = nc.dram_tensor("convb_r", [128, 80], F32, kind="ExternalInput").ap()
    dtb_col = nc.dram_tensor("dtb_col", [128, 1], F32, kind="ExternalInput").ap()
    alog_col = nc.dram_tensor("alog_col", [128, 1], F32, kind="ExternalInput").ap()
    d_exp = nc.dram_tensor("d_exp", [128, DIN], F32, kind="ExternalInput").ap()
    halo_m = nc.dram_tensor("halo_m", [128, 4], F32, kind="ExternalInput").ap()
    tri0 = nc.dram_tensor("tri0", [128, 256], F32, kind="ExternalInput").ap()
    tri1 = nc.dram_tensor("tri1", [128, 256], F32, kind="ExternalInput").ap()
    id_bf = nc.dram_tensor("id_bf", [128, 128], BF16, kind="ExternalInput").ap()
    id_f = nc.dram_tensor("id_f", [128, 128], F32, kind="ExternalInput").ap()

    x2_out = nc.dram_tensor("x2_out", [T_EXT, DM], F32, kind="ExternalOutput").ap()
    y_part = nc.dram_tensor("y_part", [T_OWN, DIN], F32, kind="ExternalOutput").ap()
    gate_out = nc.dram_tensor("gate_out", [T_OWN, DIN], F32, kind="ExternalOutput").ap()
    c_out = nc.dram_tensor("c_out", [G * NST, T_OWN], BF16, kind="ExternalOutput").ap()
    acs_out = nc.dram_tensor("acs_out", [T_OWN, NH], F32, kind="ExternalOutput").ap()
    t_out = nc.dram_tensor("t_out", [NH, NCH], F32, kind="ExternalOutput").ap()
    s_out = nc.dram_tensor("s_out", [NST, NH * HD], F32, kind="ExternalOutput").ap()
    sloc_out = nc.dram_tensor("sloc_out", [(NCH - 1) * NST, NH * HD], F32, kind="ExternalOutput").ap()

    x1d = nc.dram_tensor("x1d", [T_EXT, DM], F32, kind="ExternalOutput").ap()
    xn2d = nc.dram_tensor("xn2d", [T_PAD, DM], BF16).ap()
    xn3d = nc.dram_tensor("xn3d", [T_PAD, DM], BF16).ap()
    xsil_d = nc.dram_tensor("xsil_d", [DIN, T_OWN], BF16).ap()
    acsht_d = nc.dram_tensor("acsht_d", [NH, T_OWN], F32).ap()

    with contextlib.ExitStack() as stack:
        ctx = Ctx(nc, p, stack)
        bank = [ctx.psum(f"bank{i}", [128, 512], F32) for i in range(7)]
        bkt = ctx.psum("bankT", [128, 1024], BF16)
        BK = [("bank", i) for i in range(7)]
        BKT = ("bankT",)

        ones_b = ctx.sbuf("ones_b", [1, 512], BF16)
        ones_f = ctx.sbuf("ones_f", [1, 128], F32)
        identb = ctx.sbuf("identb", [128, 128], BF16)
        identf = ctx.sbuf("identf", [128, 128], F32)
        ev_f = [ctx.sbuf(f"ev_f{j}", [128, 512], F32) for j in range(4)]
        p.add("vector", lambda v: v.memset(ones_b[:, :], 1.0), writes=["ones_b"], tag="onesb")
        p.add("vector", lambda v: v.memset(ones_f[:, :], 1.0), writes=["ones_f"], tag="onesf")
        p.add("sync", lambda sy: sy.dma_start(out=identb[:, :], in_=id_bf[:, :]),
              writes=["identb"], dma=True, tag="ld_idb")
        p.add("sync", lambda sy: sy.dma_start(out=identf[:, :], in_=id_f[:, :]),
              writes=["identf"], dma=True, tag="ld_idf")

        with ctx.scope():
            zb = ctx.sbuf("zb", [128, DM], BF16)
            p.add("vector", lambda v: v.memset(zb[:, :], 0.0), writes=["zb"], tag="zb")
            for nm, d in (("xn2d", xn2d), ("xn3d", xn3d)):
                p.add("gpsimd", lambda g, d=d: g.dma_start(
                    out=d[1024:1152, :], in_=zb[:, :]),
                    reads=["zb"], writes=[(nm, 8)], dma=True, tag=f"zpad_{nm}")

        ao_keys = [("ao_fm", dc, qt) for dc in range(8) for qt in range(NQT)]
        with ctx.scope():
            ao_fm = ctx.sbuf("ao_fm", [128, 8 * T_EXT], BF16)
            with ctx.scope():
                q_sb = ctx.sbuf("q_sb", [128, 8 * T_EXT], BF16)
                for dc in range(8):
                    p.add("sync", lambda sy, dc=dc: sy.dma_start(
                        out=q_sb[:, dc * T_EXT:(dc + 1) * T_EXT],
                        in_=q_ext[dc * 128:(dc + 1) * 128, :]),
                        writes=[("q_sb", dc)], dma=True, tag=f"ldq{dc}")
                KQ = 1024
                k_sb = [ctx.sbuf(f"k_sb{j}", [128, 8 * KQ], BF16) for j in range(2)]
                v_sb = [ctx.sbuf(f"v_sb{j}", [128, (KQ // 128) * DH], BF16) for j in range(2)]
                S_f = [ctx.sbuf(f"S_f{q}", [128, KQ], F32) for q in range(3)]
                P_b = [ctx.sbuf(f"P_b{q}", [128, KQ], BF16) for q in range(3)]
                PT_b = [ctx.sbuf(f"PT_b{q}", [128, 16 * 128], BF16) for q in range(3)]
                oacc = [ctx.sbuf(f"oacc{t}", [128, DH], F32) for t in range(NQT)]
                mrun = ctx.sbuf("mrun", [128, NQT], F32)
                srun = ctx.sbuf("srun", [128, NQT], F32)
                tcolr = [ctx.sbuf(f"tcol{q}", [128, 6], F32) for q in range(3)]
                ao_tm = [ctx.sbuf(f"ao_tm{q}", [128, DH], BF16) for q in range(3)]
                SCALE = 1.0 / 32.0
                NKQ = 8192 // KQ
                NKC = KQ // 128

                def attnA(kq, qt, kp):
                    r = QT_ROWS[qt]
                    q0 = qt * 128
                    qp = qt % 3
                    tcol = tcolr[qp]
                    for sb4 in range(KQ // 512):
                        bk = sb4 % 4
                        for dc in range(8):
                            p.add("tensor", lambda te, dc=dc, sb4=sb4, bk=bk, q0=q0, r=r, kp=kp: te.matmul(
                                bank[bk][0:r, :],
                                q_sb[:, dc * T_EXT + q0: dc * T_EXT + q0 + r],
                                k_sb[kp][:, dc * KQ + sb4 * 512: dc * KQ + (sb4 + 1) * 512],
                                start=(dc == 0), stop=(dc == 7)),
                                reads=[("q_sb", dc), ("k_sb", kp, dc)], writes=[("bank", bk)],
                                tag=f"s_mm{kq}_{qt}_{sb4}_{dc}")
                        p.add("scalar", lambda sc, sb4=sb4, bk=bk, r=r, qp=qp: sc.activation(
                            S_f[qp][0:r, sb4 * 512:(sb4 + 1) * 512], bank[bk][0:r, :], AF.Copy),
                            reads=[("bank", bk)], writes=[("S_f", qp, sb4)],
                            tag=f"s_ev{kq}_{qt}_{sb4}")
                    SFK = [("S_f", qp, i) for i in range(KQ // 512)]
                    p.add("vector", lambda v, r=r, tcol=tcol, qp=qp: v.tensor_reduce(
                        tcol[0:r, 0:1], S_f[qp][0:r, :], AX.X, ALU.max, negate=True),
                        reads=SFK, writes=[("tcol", qp, 0)], tag=f"qmax{kq}_{qt}")
                    if kq == 0:
                        p.add("vector", lambda v, r=r, qt=qt, tcol=tcol, qp=qp: v.tensor_scalar(
                            mrun[0:r, qt:qt + 1], tcol[0:r, 0:1], -1.0, 0.0,
                            ALU.mult, ALU.add),
                            reads=[("tcol", qp, 0)], writes=[("mrun", qt)],
                            tag=f"mr0_{kq}_{qt}")
                    else:
                        p.add("vector", lambda v, r=r, tcol=tcol: v.tensor_scalar(
                            tcol[0:r, 1:2], tcol[0:r, 0:1], -1.0, 0.0, ALU.mult, ALU.add),
                            reads=[("tcol", qp, 0)], writes=[("tcol", qp, 1)], tag=f"qm_{kq}_{qt}")
                        p.add("vector", lambda v, r=r, qt=qt, tcol=tcol: v.tensor_tensor(
                            tcol[0:r, 2:3], mrun[0:r, qt:qt + 1], tcol[0:r, 1:2], ALU.max),
                            reads=[("tcol", qp, 1), ("mrun", qt)], writes=[("tcol", qp, 2)],
                            tag=f"mn_{kq}_{qt}")
                        p.add("vector", lambda v, r=r, qt=qt, tcol=tcol: v.tensor_tensor(
                            tcol[0:r, 3:4], mrun[0:r, qt:qt + 1], tcol[0:r, 2:3], ALU.subtract),
                            reads=[("tcol", qp, 2), ("mrun", qt)], writes=[("tcol", qp, 3)],
                            tag=f"md_{kq}_{qt}")
                        p.add("scalar", lambda sc, r=r, tcol=tcol: sc.activation(
                            tcol[0:r, 4:5], tcol[0:r, 3:4], AF.Exp, scale=SCALE),
                            reads=[("tcol", qp, 3)], writes=[("tcol", qp, 4)], tag=f"al_{kq}_{qt}")
                        p.add("vector", lambda v, r=r, qt=qt, tcol=tcol: v.tensor_copy(
                            mrun[0:r, qt:qt + 1], tcol[0:r, 2:3]),
                            reads=[("tcol", qp, 2)], writes=[("mrun", qt)], tag=f"mu_{kq}_{qt}")
                    p.add("vector", lambda v, r=r, qt=qt, tcol=tcol: v.tensor_scalar(
                        tcol[0:r, 5:6], mrun[0:r, qt:qt + 1], -SCALE, 0.0, ALU.mult, ALU.add),
                        reads=[("mrun", qt)], writes=[("tcol", qp, 5)], tag=f"pb_{kq}_{qt}")
                    p.add("scalar", lambda sc, r=r, tcol=tcol, qp=qp: sc.activation(
                        P_b[qp][0:r, :], S_f[qp][0:r, :], AF.Exp, scale=SCALE,
                        bias=tcol[0:r, 5:6], accum_out=tcol[0:r, 0:1]),
                        reads=SFK + [("tcol", qp, 5)], writes=[("P_b", qp), ("tcol", qp, 0)],
                        tag=f"pe_{kq}_{qt}")

                def attnB(kq, qt, kp):
                    r = QT_ROWS[qt]
                    qp = qt % 3
                    tcol = tcolr[qp]
                    if kq == 0:
                        p.add("vector", lambda v, r=r, qt=qt, tcol=tcol: v.tensor_copy(
                            srun[0:r, qt:qt + 1], tcol[0:r, 0:1]),
                            reads=[("tcol", qp, 0)], writes=[("srun", qt)], tag=f"s0_{kq}_{qt}")
                    else:
                        p.add("vector", lambda v, r=r, qt=qt, tcol=tcol: v.scalar_tensor_tensor(
                            srun[0:r, qt:qt + 1], srun[0:r, qt:qt + 1], tcol[0:r, 4:5],
                            tcol[0:r, 0:1], ALU.mult, ALU.add),
                            reads=[("tcol", qp, 0), ("tcol", qp, 4), ("srun", qt)],
                            writes=[("srun", qt)], tag=f"sr_{kq}_{qt}")
                        p.add("vector", lambda v, r=r, qt=qt, tcol=tcol: v.tensor_scalar(
                            oacc[qt][0:r, :], oacc[qt][0:r, :], tcol[0:r, 4:5], 0.0,
                            ALU.mult, ALU.add),
                            reads=[("oacc", qt), ("tcol", qp, 4)], writes=[("oacc", qt)],
                            tag=f"oa_{kq}_{qt}")
                    for kc in range(NKC):
                        p.add("tensor", lambda te, kc=kc, r=r, qp=qp: te.transpose(
                            bkt[:, 0:r], P_b[qp][0:r, kc * 128:(kc + 1) * 128], identb[0:r, 0:r]),
                            reads=[("P_b", qp), "identb"], writes=[BKT], tag=f"pt_{kq}_{qt}_{kc}")
                        p.add("vector", lambda v, kc=kc, r=r, qp=qp: v.tensor_copy(
                            PT_b[qp][:, kc * 128:kc * 128 + r], bkt[:, 0:r]),
                            reads=[BKT], writes=[("PT_b", qp, kc)], tag=f"pc_{kq}_{qt}_{kc}")
                    for fh in range(2):
                        bk = 4 + fh
                        for kc in range(NKC):
                            p.add("tensor", lambda te, kc=kc, fh=fh, bk=bk, r=r, qp=qp, kp=kp: te.matmul(
                                bank[bk][0:r, :],
                                PT_b[qp][:, kc * 128:kc * 128 + r],
                                v_sb[kp][:, kc * DH + fh * 512: kc * DH + (fh + 1) * 512],
                                start=(kc == 0), stop=(kc == NKC - 1)),
                                reads=[("PT_b", qp, kc), ("v_sb", kp, kc)], writes=[("bank", bk)],
                                tag=f"pv_{kq}_{qt}_{fh}_{kc}")
                        if kq == 0:
                            p.add("scalar", lambda sc, fh=fh, bk=bk, r=r, qt=qt: sc.activation(
                                oacc[qt][0:r, fh * 512:(fh + 1) * 512],
                                bank[bk][0:r, :], AF.Copy),
                                reads=[("bank", bk)], writes=[("oacc", qt)],
                                tag=f"pz_{kq}_{qt}_{fh}")
                        else:
                            p.add("vector", lambda v, fh=fh, bk=bk, r=r, qt=qt: v.tensor_tensor(
                                oacc[qt][0:r, fh * 512:(fh + 1) * 512],
                                oacc[qt][0:r, fh * 512:(fh + 1) * 512],
                                bank[bk][0:r, :], ALU.add),
                                reads=[("bank", bk), ("oacc", qt)], writes=[("oacc", qt)],
                                tag=f"pa_{kq}_{qt}_{fh}")

                LA = 2
                for kq in range(NKQ):
                    kp = kq % 2
                    for dc in range(8):
                        p.add("sync", lambda sy, dc=dc, kq=kq, kp=kp: sy.dma_start(
                            out=k_sb[kp][:, dc * KQ:(dc + 1) * KQ],
                            in_=k_fm[dc * 128:(dc + 1) * 128, kq * KQ:(kq + 1) * KQ]),
                            writes=[("k_sb", kp, dc)], dma=True, tag=f"ldk{kq}_{dc}")
                    for kc in range(NKC):
                        p.add("scalar", lambda sc, kc=kc, kq=kq, kp=kp: sc.dma_start(
                            out=v_sb[kp][:, kc * DH:(kc + 1) * DH],
                            in_=v_tm[kq * KQ + kc * 128: kq * KQ + (kc + 1) * 128, :]),
                            writes=[("v_sb", kp, kc)], dma=True, tag=f"ldv{kq}_{kc}")
                    for qt in range(NQT + LA):
                        if qt < NQT:
                            attnA(kq, qt, kp)
                        if qt >= LA:
                            attnB(kq, qt - LA, kp)
                for qt in range(NQT):
                    r = QT_ROWS[qt]
                    qp = qt % 3
                    tcol = tcolr[qp]
                    p.add("vector", lambda v, r=r, qt=qt, tcol=tcol: v.reciprocal(
                        tcol[0:r, 0:1], srun[0:r, qt:qt + 1]),
                        reads=[("srun", qt)], writes=[("tcol", qp, 0)], tag=f"fr{qt}")
                    p.add("vector", lambda v, r=r, qt=qt, tcol=tcol, qp=qp: v.tensor_scalar(
                        ao_tm[qp][0:r, :], oacc[qt][0:r, :], tcol[0:r, 0:1], 0.0,
                        ALU.mult, ALU.add),
                        reads=[("oacc", qt), ("tcol", qp, 0)], writes=[("ao_tm", qp)], tag=f"fs{qt}")
                    for dc in range(8):
                        p.add("tensor", lambda te, dc=dc, r=r, qp=qp: te.transpose(
                            bkt[:, 0:r], ao_tm[qp][0:r, dc * 128:(dc + 1) * 128], identb[0:r, 0:r]),
                            reads=[("ao_tm", qp), "identb"], writes=[BKT], tag=f"at{qt}_{dc}")
                        p.add("vector", lambda v, dc=dc, r=r, qt=qt: v.tensor_copy(
                            ao_fm[:, dc * T_EXT + qt * 128: dc * T_EXT + qt * 128 + r],
                            bkt[:, 0:r]),
                            reads=[BKT], writes=[("ao_fm", dc, qt)], tag=f"af{qt}_{dc}")
            with ctx.scope():
                wo_t = [ctx.sbuf(f"wo_t{j}", [128, 512], BF16) for j in range(4)]
                bo_b = ctx.sbuf("bo_b", [1, DM], BF16)
                x0_t = [ctx.sbuf(f"x0_t{j}", [128, 512], F32) for j in range(2)]
                p.add("gpsimd", lambda g: g.dma_start(out=bo_b[:, :], in_=bo[:, :]),
                      writes=["bo_b"], dma=True, tag="ld_bo")
                for qtg in (range(0, 6), range(6, 9)):
                    for f5 in range(8):
                        for dc in range(8):
                            j = dc % 4
                            p.add("gpsimd", lambda g, f5=f5, dc=dc, j=j: g.dma_start(
                                out=wo_t[j][:, :],
                                in_=wo[dc * 128:(dc + 1) * 128, f5 * 512:(f5 + 1) * 512]),
                                writes=[("wo_t", j)], dma=True, tag=f"lw{f5}_{dc}")
                            for bi, qt in enumerate(qtg):
                                r = QT_ROWS[qt]
                                p.add("tensor", lambda te, dc=dc, j=j, bi=bi, qt=qt, r=r: te.matmul(
                                    bank[bi][0:r, :],
                                    ao_fm[:, dc * T_EXT + qt * 128: dc * T_EXT + qt * 128 + r],
                                    wo_t[j][:, :], start=(dc == 0), stop=False),
                                    reads=[("wo_t", j), ("ao_fm", dc, qt)], writes=[BK[bi]],
                                    tag=f"wm{f5}_{qt}_{dc}")
                        for bi, qt in enumerate(qtg):
                            r = QT_ROWS[qt]
                            p.add("tensor", lambda te, bi=bi, r=r, f5=f5: te.matmul(
                                bank[bi][0:r, :], ones_b[:, 0:r],
                                bo_b[:, f5 * 512:(f5 + 1) * 512], start=False, stop=True),
                                reads=["ones_b", "bo_b"], writes=[BK[bi]], tag=f"wb{f5}_{qt}")
                            j = bi % 2
                            p.add("sync", lambda sy, qt=qt, f5=f5, j=j, r=r: sy.dma_start(
                                out=x0_t[j][0:r, :],
                                in_=x0e[qt * 128:qt * 128 + r, f5 * 512:(f5 + 1) * 512]),
                                writes=[("x0_t", j)], dma=True, tag=f"lx{f5}_{qt}")
                            p.add("vector", lambda v, bi=bi, j=j, r=r: v.tensor_tensor(
                                ev_f[j][0:r, :], bank[bi][0:r, :], x0_t[j][0:r, :], ALU.add),
                                reads=[BK[bi], ("x0_t", j)], writes=[("ev_f", j)],
                                tag=f"wr{f5}_{qt}")
                            p.add("gpsimd", lambda g, qt=qt, f5=f5, j=j, r=r: g.dma_start(
                                out=x1d[qt * 128:qt * 128 + r, f5 * 512:(f5 + 1) * 512],
                                in_=ev_f[j][0:r, :]),
                                reads=[("ev_f", j)], writes=[("x1d", qt, f5)], dma=True,
                                tag=f"sx{f5}_{qt}")
                emit_ln_stream(ctx, x1d, xn2d, "xn2d", T_EXT, "ln2",
                               src_keys=[("x1d", qt, f5) for qt in range(NQT) for f5 in range(8)])
        X1K = [("x1d", qt, f5) for qt in range(NQT) for f5 in range(8)]
        XN2K = [("xn2d", i) for i in range(9)]
        with ctx.scope():
            xn2fm = ctx.sbuf("xn2fm", [128, 32 * T_PAD], BF16)
            for cc_i in range(32):
                p.add("sync", lambda sc, cc_i=cc_i: sc.dma_start(
                    out=xn2fm[:, cc_i * T_PAD:(cc_i + 1) * T_PAD],
                    in_=xn2d[:, cc_i * 128:(cc_i + 1) * 128], transpose=True),
                    reads=XN2K, writes=[("xn2fm", cc_i)], dma=True, tag=f"t2_{cc_i}")
            fb_b = ctx.sbuf("fb_b", [1, DM], BF16)
            p.add("gpsimd", lambda g: g.dma_start(out=fb_b[:, :], in_=fbias[:, :]),
                  writes=["fb_b"], dma=True, tag="ld_fb")
            wf_t = [ctx.sbuf(f"wf_t{j}", [128, 512], BF16) for j in range(8)]
            xr_t = [ctx.sbuf(f"xr_t{j}", [128, 512], F32) for j in range(4)]

            def ffn_pass(tgs, fm_sb, fm_key, src_dram, src_keys, dst_dram,
                         dst_key, rows, tagp):
                for tg in tgs:
                    for og in range(8):
                        for cc_i in range(32):
                            j = cc_i % 8
                            p.add("sync", lambda sy, og=og, cc_i=cc_i, j=j: sy.dma_start(
                                out=wf_t[j][:, :],
                                in_=wffn[cc_i * 128:(cc_i + 1) * 128, og * 512:(og + 1) * 512]),
                                writes=[(f"{tagp}wf", j)], dma=True,
                                tag=f"{tagp}w{tg[0]}_{og}_{cc_i}")
                            for bi, qt in enumerate(tg):
                                r = rows[qt]
                                bb = (og % 2) * 3 + bi
                                p.add("tensor", lambda te, cc_i=cc_i, j=j, bb=bb, qt=qt, r=r: te.matmul(
                                    bank[bb][0:r, :],
                                    fm_sb[:, cc_i * T_PAD + qt * 128: cc_i * T_PAD + qt * 128 + r],
                                    wf_t[j][:, :], start=(cc_i == 0), stop=False),
                                    reads=[(f"{tagp}wf", j), (fm_key, cc_i)],
                                    writes=[BK[bb]], tag=f"{tagp}m{og}_{qt}_{cc_i}")
                        for bi, qt in enumerate(tg):
                            r = rows[qt]
                            bb = (og % 2) * 3 + bi
                            p.add("tensor", lambda te, bb=bb, r=r, og=og: te.matmul(
                                bank[bb][0:r, :], ones_b[:, 0:r],
                                fb_b[:, og * 512:(og + 1) * 512], start=False, stop=True),
                                reads=["ones_b", "fb_b"], writes=[BK[bb]],
                                tag=f"{tagp}b{og}_{qt}")
                            j = bi % 2
                            p.add("scalar", lambda sc, qt=qt, og=og, j=j, r=r: sc.dma_start(
                                out=xr_t[j][0:r, :],
                                in_=src_dram[qt * 128:qt * 128 + r, og * 512:(og + 1) * 512]),
                                reads=src_keys, writes=[(f"{tagp}x", j)], dma=True,
                                tag=f"{tagp}l{og}_{qt}")
                            p.add("vector", lambda v, bb=bb, j=j, r=r: v.tensor_tensor(
                                ev_f[j][0:r, :], bank[bb][0:r, :], xr_t[j][0:r, :], ALU.add),
                                reads=[BK[bb], (f"{tagp}x", j)], writes=[("ev_f", j)],
                                tag=f"{tagp}r{og}_{qt}")
                            p.add("gpsimd", lambda g, qt=qt, og=og, j=j, r=r: g.dma_start(
                                out=dst_dram[qt * 128:qt * 128 + r, og * 512:(og + 1) * 512],
                                in_=ev_f[j][0:r, :]),
                                reads=[("ev_f", j)], writes=[(dst_key, qt, og)], dma=True,
                                tag=f"{tagp}s{og}_{qt}")

            ffn_pass([range(0, 3), range(3, 6), range(6, 9)], xn2fm, "xn2fm",
                     x1d, X1K, x2_out, "x2o", QT_ROWS, "f1")
            emit_ln_stream(ctx, x2_out, xn3d, "xn3d", T_EXT, "ln3",
                           src_keys=[("x2o", qt, f5) for qt in range(NQT) for f5 in range(8)])
        X2K = [("x2o", qt, f5) for qt in range(NQT) for f5 in range(8)]

        XN3K = [("xn3d", i) for i in range(9)]

        B_fm = ctx.sbuf("B_fm", [128, 8 * T_OWN], BF16)
        C_fm = ctx.sbuf("C_fm", [128, 8 * T_OWN], BF16)
        dt_raw = ctx.sbuf("dt_raw", [128, T_PAD], F32)
        with ctx.scope():
            xn3fm = ctx.sbuf("xn3fm", [128, 32 * T_PAD], BF16)
            for cc_i in range(32):
                p.add("sync", lambda sc, cc_i=cc_i: sc.dma_start(
                    out=xn3fm[:, cc_i * T_PAD:(cc_i + 1) * T_PAD],
                    in_=xn3d[:, cc_i * 128:(cc_i + 1) * 128], transpose=True),
                    reads=XN3K, writes=[("xn3fm", cc_i)], dma=True, tag=f"t3_{cc_i}")
            ip_t = [ctx.sbuf(f"ip_t{j}", [128, 512], BF16) for j in range(8)]
            gate_ev = [ctx.sbuf(f"gate_ev{j}", [128, 512], F32) for j in range(2)]
            for tg in (range(0, 4), range(4, 8)):
                for f5 in range(16):
                    for cc_i in range(32):
                        j = cc_i % 8
                        p.add("sync", lambda sy, f5=f5, cc_i=cc_i, j=j: sy.dma_start(
                            out=ip_t[j][:, :],
                            in_=ipbf[cc_i * 128:(cc_i + 1) * 128, f5 * 512:(f5 + 1) * 512]),
                            writes=[("ip_t", j)], dma=True, tag=f"gw{tg[0]}_{f5}_{cc_i}")
                        for bi, t in enumerate(tg):
                            p.add("tensor", lambda te, cc_i=cc_i, j=j, bi=bi, t=t: te.matmul(
                                bank[bi][:, :],
                                xn3fm[:, cc_i * T_PAD + 3 + t * 128: cc_i * T_PAD + 3 + (t + 1) * 128],
                                ip_t[j][:, :], start=(cc_i == 0), stop=(cc_i == 31)),
                                reads=[("ip_t", j), ("xn3fm", cc_i)], writes=[BK[bi]],
                                tag=f"gm{f5}_{t}_{cc_i}")
                    for bi, t in enumerate(tg):
                        j = bi % 2
                        p.add("scalar", lambda sc, bi=bi, j=j, f5=f5, t=t: sc.activation(
                            gate_ev[j][:, :], bank[bi][:, :], AF.Silu),
                            reads=[BK[bi]], writes=[("gate_ev", j)], tag=f"ge{f5}_{t}")
                        p.add("gpsimd", lambda g, t=t, f5=f5, j=j: g.dma_start(
                            out=gate_out[t * 128:(t + 1) * 128, f5 * 512:(f5 + 1) * 512],
                            in_=gate_ev[j][:, :]),
                            reads=[("gate_ev", j)], writes=[("gate_o", t, f5)], dma=True,
                            tag=f"gs{f5}_{t}")
            stash = [ctx.sbuf(f"stash{c}", [128, 512], BF16) for c in range(32)]
            xbcr = ctx.sbuf("xbcr", [128, T_PAD], F32)
            conv_a = ctx.sbuf("conv_a", [128, T_OWN], F32)
            conv_b2 = ctx.sbuf("conv_b2", [128, T_OWN], F32)
            cwr = ctx.sbuf("cwr", [128, 320], F32)
            cbr = ctx.sbuf("cbr", [128, 80], F32)
            hm = ctx.sbuf("hm", [128, 4], F32)
            xsil_b = ctx.sbuf("xsil_b", [128, T_OWN], BF16)
            p.add("sync", lambda sy: sy.dma_start(out=cwr[:, :], in_=convw_r[:, :]),
                  writes=["cwr"], dma=True, tag="ld_cwr")
            p.add("sync", lambda sy: sy.dma_start(out=cbr[:, :], in_=convb_r[:, :]),
                  writes=["cbr"], dma=True, tag="ld_cbr")
            p.add("sync", lambda sy: sy.dma_start(out=hm[:, :], in_=halo_m[:, :]),
                  writes=["hm"], dma=True, tag="ld_hm")

            def emit_conv_silu(ft, tagc):
                p.add("vector", lambda v: v.tensor_tensor(
                    xbcr[:, 0:3], xbcr[:, 0:3], hm[:, 0:3], ALU.mult),
                    reads=["xbcr", "hm"], writes=["xbcr"], tag=f"{tagc}hm")
                p.add("vector", lambda v, ft=ft: v.tensor_scalar(
                    conv_a[:, :], xbcr[:, 0:T_OWN], cwr[:, ft * 4:ft * 4 + 1],
                    cbr[:, ft:ft + 1], ALU.mult, ALU.add),
                    reads=["xbcr", "cwr", "cbr"], writes=["conv_a"], tag=f"{tagc}c0")
                p.add("vector", lambda v, ft=ft: v.scalar_tensor_tensor(
                    conv_b2[:, :], xbcr[:, 1:1 + T_OWN], cwr[:, ft * 4 + 1:ft * 4 + 2],
                    conv_a[:, :], ALU.mult, ALU.add),
                    reads=["xbcr", "cwr", "conv_a"], writes=["conv_b2"], tag=f"{tagc}c1")
                p.add("vector", lambda v, ft=ft: v.scalar_tensor_tensor(
                    conv_a[:, :], xbcr[:, 2:2 + T_OWN], cwr[:, ft * 4 + 2:ft * 4 + 3],
                    conv_b2[:, :], ALU.mult, ALU.add),
                    reads=["xbcr", "cwr", "conv_b2"], writes=["conv_a"], tag=f"{tagc}c2")
                p.add("vector", lambda v, ft=ft: v.scalar_tensor_tensor(
                    conv_b2[:, :], xbcr[:, 3:3 + T_OWN], cwr[:, ft * 4 + 3:ft * 4 + 4],
                    conv_a[:, :], ALU.mult, ALU.add),
                    reads=["xbcr", "cwr", "conv_a"], writes=["conv_b2"], tag=f"{tagc}c3")
                if ft < 64:
                    p.add("scalar", lambda sc: sc.activation(
                        xsil_b[:, :], conv_b2[:, :], AF.Silu),
                        reads=["conv_b2"], writes=["xsil_b"], tag=f"{tagc}si")
                    p.add("gpsimd", lambda g, ft=ft: g.dma_start(
                        out=xsil_d[ft * 128:(ft + 1) * 128, :], in_=xsil_b[:, :]),
                        reads=["xsil_b"], writes=[("xsil", ft)], dma=True, tag=f"{tagc}sx")
                elif ft < 72:
                    gi = ft - 64
                    p.add("scalar", lambda sc, gi=gi: sc.activation(
                        B_fm[:, gi * T_OWN:(gi + 1) * T_OWN], conv_b2[:, :], AF.Silu),
                        reads=["conv_b2"], writes=[("B_fm", gi)], tag=f"{tagc}si")
                else:
                    gi = ft - 72
                    p.add("scalar", lambda sc, gi=gi: sc.activation(
                        C_fm[:, gi * T_OWN:(gi + 1) * T_OWN], conv_b2[:, :], AF.Silu),
                        reads=["conv_b2"], writes=[("C_fm", gi)], tag=f"{tagc}si")
                    p.add("gpsimd", lambda g, gi=gi: g.dma_start(
                        out=c_out[gi * 128:(gi + 1) * 128, :],
                        in_=C_fm[:, gi * T_OWN:(gi + 1) * T_OWN]),
                        reads=[("C_fm", gi)], writes=[("c_o", gi)], dma=True,
                        tag=f"{tagc}sc")

            TSL = [(0, 512), (512, 512), (1024, 128)]

            def emit_fsub(ft, sub, tagx):
                fofs = (ft % 4) * 128
                for cc_i in range(32):
                    for si, (t0, tw) in enumerate(TSL):
                        bi = sub * 3 + si
                        p.add("tensor", lambda te, cc_i=cc_i, fofs=fofs, t0=t0, tw=tw, bi=bi: te.matmul(
                            bank[bi][:, 0:tw],
                            stash[cc_i][:, fofs:fofs + 128],
                            xn3fm[:, cc_i * T_PAD + t0: cc_i * T_PAD + t0 + tw],
                            start=(cc_i == 0), stop=(cc_i == 31)),
                            reads=[("stash", cc_i), ("xn3fm", cc_i)], writes=[BK[bi]],
                            tag=f"{tagx}m{si}_{cc_i}")
                if ft < 80:
                    for si, (t0, tw) in enumerate(TSL):
                        bi = sub * 3 + si
                        p.add("scalar", lambda sc, bi=bi, t0=t0, tw=tw: sc.activation(
                            xbcr[:, t0:t0 + tw], bank[bi][:, 0:tw], AF.Copy),
                            reads=[BK[bi]], writes=["xbcr"], tag=f"{tagx}e{si}")
                    emit_conv_silu(ft, tagx)
                else:
                    for si, (t0, tw) in enumerate(TSL):
                        bi = sub * 3 + si
                        p.add("scalar", lambda sc, bi=bi, t0=t0, tw=tw: sc.activation(
                            dt_raw[:, t0:t0 + tw], bank[bi][:, 0:tw], AF.Copy),
                            reads=[BK[bi]], writes=["dt_raw"], tag=f"{tagx}d{si}")

            for fg in range(20):
                frow0 = DIN + fg * 512
                for cc_i in range(32):
                    p.add("sync", lambda sy, frow0=frow0, cc_i=cc_i: sy.dma_start(
                        out=stash[cc_i][:, :],
                        in_=ipbf[cc_i * 128:(cc_i + 1) * 128, frow0:frow0 + 512]),
                        writes=[("stash", cc_i)], dma=True, tag=f"xw{fg}_{cc_i}")
                emit_fsub(fg * 4 + 0, 0, f"xa{fg}_0_")
                emit_fsub(fg * 4 + 1, 1, f"xa{fg}_1_")
                emit_fsub(fg * 4 + 2, 0, f"xb{fg}_2_")
                emit_fsub(fg * 4 + 3, 1, f"xb{fg}_3_")
            for cc_i in range(32):
                p.add("sync", lambda sy, cc_i=cc_i: sy.dma_start(
                    out=stash[cc_i][:, 0:128],
                    in_=ipbf[cc_i * 128:(cc_i + 1) * 128, DIN + CONV_DIM:PROJ]),
                    writes=[("stash", cc_i)], dma=True, tag=f"dw_{cc_i}")
            emit_fsub(80, 0, "dt_")

        acs_ti = ctx.sbuf("acs_ti", [128, 8 * 128], F32)
        dt_ti = ctx.sbuf("dt_ti", [128, 8 * 128], F32)
        dtds_ti = ctx.sbuf("dtds_ti", [128, 8 * 128], F32)
        ttile = ctx.sbuf("ttile", [128, NCH], F32)
        expt_bc = ctx.sbuf("expt_bc", [128, NCH * 128], F32)
        with ctx.scope():
            dtb_s = ctx.sbuf("dtb_s", [128, 1], F32)
            al_s = ctx.sbuf("al_s", [128, 1], F32)
            p.add("sync", lambda sy: sy.dma_start(out=dtb_s[:, :], in_=dtb_col[:, :]),
                  writes=["dtb_s"], dma=True, tag="ld_dtb")
            p.add("sync", lambda sy: sy.dma_start(out=al_s[:, :], in_=alog_col[:, :]),
                  writes=["al_s"], dma=True, tag="ld_al")
            dt_ht = ctx.sbuf("dt_ht", [128, T_OWN], F32)
            sp_a = ctx.sbuf("sp_a", [128, T_OWN], F32)
            sp_b = ctx.sbuf("sp_b", [128, T_OWN], F32)
            nega = ctx.sbuf("nega", [128, 2], F32)
            adt_ht = ctx.sbuf("adt_ht", [128, T_OWN], F32)
            acs_ht = ctx.sbuf("acs_ht", [128, T_OWN], F32)
            ds_ht = ctx.sbuf("ds_ht", [128, T_OWN], F32)
            dtds_ht = ctx.sbuf("dtds_ht", [128, T_OWN], F32)
            zeros_t = ctx.sbuf("zeros_t", [128, CS], F32)
            p.add("vector", lambda v: v.memset(zeros_t[:, :], 0.0),
                  writes=["zeros_t"], tag="zt")
            p.add("vector", lambda v: v.tensor_scalar(
                dt_ht[:, :], dt_raw[:, 3:3 + T_OWN], dtb_s[:, 0:1], 0.0, ALU.add, ALU.add),
                reads=["dt_raw", "dtb_s"], writes=["dt_ht"], tag="sp_z")
            p.add("scalar", lambda sc: sc.activation(sp_a[:, :], dt_ht[:, :], AF.Abs),
                  reads=["dt_ht"], writes=["sp_a"], tag="sp_abs")
            p.add("scalar", lambda sc: sc.activation(sp_b[:, :], sp_a[:, :], AF.Exp, scale=-1.0),
                  reads=["sp_a"], writes=["sp_b"], tag="sp_exp")
            p.add("scalar", lambda sc: sc.activation(sp_a[:, :], sp_b[:, :], AF.Ln, bias=1.0),
                  reads=["sp_b"], writes=["sp_a"], tag="sp_ln")
            p.add("vector", lambda v: v.tensor_scalar(
                sp_b[:, :], dt_ht[:, :], 0.0, 0.0, ALU.max, ALU.add),
                reads=["dt_ht"], writes=["sp_b"], tag="sp_relu")
            p.add("vector", lambda v: v.tensor_tensor(
                dt_ht[:, :], sp_a[:, :], sp_b[:, :], ALU.add),
                reads=["sp_a", "sp_b"], writes=["dt_ht"], tag="sp_add")
            p.add("scalar", lambda sc: sc.activation(nega[:, 0:1], al_s[:, 0:1], AF.Exp),
                  reads=["al_s"], writes=["nega"], tag="na_e")
            p.add("vector", lambda v: v.tensor_scalar(
                nega[:, 1:2], nega[:, 0:1], -1.0, 0.0, ALU.mult, ALU.add),
                reads=["nega"], writes=["nega"], tag="na_n")
            p.add("vector", lambda v: v.tensor_scalar(
                adt_ht[:, :], dt_ht[:, :], nega[:, 1:2], 0.0, ALU.mult, ALU.add),
                reads=["dt_ht", "nega"], writes=["adt_ht"], tag="adt")
            for c in range(NCH):
                p.add("vector", lambda v, c=c: v.tensor_tensor_scan(
                    acs_ht[:, c * CS:(c + 1) * CS], adt_ht[:, c * CS:(c + 1) * CS],
                    zeros_t[:, :], 0.0, ALU.add, ALU.add),
                    reads=["adt_ht", "zeros_t"], writes=["acs_ht"], tag=f"scan{c}")
            p.add("gpsimd", lambda g: g.dma_start(out=acsht_d[:, :], in_=acs_ht[:, :]),
                  reads=["acs_ht"], writes=["acsht_d"], dma=True, tag="st_acsht")
            for c in range(NCH):
                p.add("vector", lambda v, c=c: v.tensor_copy(
                    ttile[:, c:c + 1], acs_ht[:, (c + 1) * CS - 1:(c + 1) * CS]),
                    reads=["acs_ht"], writes=["ttile"], tag=f"tc{c}")
            p.add("gpsimd", lambda g: g.dma_start(out=t_out[:, :], in_=ttile[:, :]),
                  reads=["ttile"], writes=["t_o"], dma=True, tag="st_t")
            for c in range(NCH):
                p.add("scalar", lambda sc, c=c: sc.activation(
                    ds_ht[:, c * CS:(c + 1) * CS], acs_ht[:, c * CS:(c + 1) * CS],
                    AF.Exp, scale=-1.0, bias=ttile[:, c:c + 1]),
                    reads=["acs_ht", "ttile"], writes=["ds_ht"], tag=f"ds{c}")
            p.add("vector", lambda v: v.tensor_tensor(
                dtds_ht[:, :], dt_ht[:, :], ds_ht[:, :], ALU.mult),
                reads=["dt_ht", "ds_ht"], writes=["dtds_ht"], tag="dtds")
            for srcap, srck, dst, kk in ((acs_ht, "acs_ht", acs_ti, "acs_ti"),
                                         (dt_ht, "dt_ht", dt_ti, "dt_ti"),
                                         (dtds_ht, "dtds_ht", dtds_ti, "dtds_ti")):
                for tt in range(8):
                    p.add("tensor", lambda te, srcap=srcap, tt=tt: te.transpose(
                        bank[6][:, 0:128], srcap[:, tt * 128:(tt + 1) * 128], identf[:, :]),
                        reads=[srck, "identf"], writes=[BK[6]], tag=f"{kk}t{tt}")
                    p.add("vector", lambda v, dst=dst, tt=tt, kk=kk: v.tensor_copy(
                        dst[:, tt * 128:(tt + 1) * 128], bank[6][:, 0:128]),
                        reads=[BK[6]], writes=[(kk, tt)], tag=f"{kk}c{tt}")
            for tt in range(8):
                p.add("gpsimd", lambda g, tt=tt: g.dma_start(
                    out=acs_out[tt * 128:(tt + 1) * 128, :],
                    in_=acs_ti[:, tt * 128:(tt + 1) * 128]),
                    reads=[("acs_ti", tt)], writes=[("acs_o", tt)], dma=True,
                    tag=f"sa{tt}")
            expt_r = ctx.sbuf("expt_r", [1, 128], F32)
            for c in range(NCH):
                p.add("gpsimd", lambda g, c=c: g.dma_start(
                    out=expt_r[0:1, 0:128],
                    in_=acs_out[(c + 1) * CS - 1:(c + 1) * CS, :]),
                    reads=[("acs_o", 2 * c + 1)], writes=["expt_r"], dma=True,
                    tag=f"er{c}")
                p.add("scalar", lambda sc: sc.activation(
                    expt_r[0:1, 0:128], expt_r[0:1, 0:128], AF.Exp),
                    reads=["expt_r"], writes=["expt_r"], tag=f"ee{c}")
                p.add("tensor", lambda te, c=c: te.matmul(
                    bank[6][:, 0:128], ones_f[:, :], expt_r[0:1, 0:128],
                    start=True, stop=True),
                    reads=["expt_r", "ones_f"], writes=[BK[6]], tag=f"em{c}")
                p.add("vector", lambda v, c=c: v.tensor_copy(
                    expt_bc[:, c * 128:(c + 1) * 128], bank[6][:, 0:128]),
                    reads=[BK[6]], writes=[("expt_bc", c)], tag=f"ec{c}")

        tri_s = ctx.sbuf("tri_s", [128, 512], F32)
        p.add("sync", lambda sy: sy.dma_start(out=tri_s[:, 0:256], in_=tri0[:, :]),
              writes=["tri_s"], dma=True, tag="ld_tri0")
        p.add("sync", lambda sy: sy.dma_start(out=tri_s[:, 256:512], in_=tri1[:, :]),
              writes=["tri_s"], dma=True, tag="ld_tri1")
        xstage = [ctx.sbuf(f"xstage{h2}", [128, 1024], BF16) for h2 in range(4)]
        btm = [ctx.sbuf(f"btm{j}", [128, 256], BF16) for j in range(2)]
        gmt = [ctx.sbuf(f"gmt{j}", [128, 512], F32) for j in range(2)]
        ltg = [ctx.sbuf(f"ltg{j}", [128, 512], BF16) for j in range(3)]
        lt1 = [ctx.sbuf(f"lt1_{j}", [128, 512], F32) for j in range(3)]
        arow = ctx.sbuf("arow", [1, 4096], F32)
        xs1 = [ctx.sbuf(f"xs1_{j}", [128, 128], BF16) for j in range(3)]
        xs2 = [ctx.sbuf(f"xs2_{j}", [128, 128], BF16) for j in range(3)]
        piece = ctx.sbuf("piece", [128, 1024], F32)
        dxl = ctx.sbuf("dxl", [128, 1024], F32)
        dex_t = [ctx.sbuf(f"dex_t{j}", [128, 1024], F32) for j in range(2)]
        slocal = ctx.sbuf("slocal", [128, DIN], F32)
        srun_s = ctx.sbuf("srun_s", [128, DIN], F32)
        p.add("vector", lambda v: v.memset(srun_s[:, :], 0.0),
              writes=["srun_s"], tag="sr0")
        for c in range(NCH):
            if c > 0:
                p.add("gpsimd", lambda g, c=c: g.dma_start(
                    out=sloc_out[(c - 1) * NST:c * NST, :], in_=srun_s[:, :]),
                    reads=["srun_s"], writes=[("sloc_o", c)], dma=True,
                    tag=f"ssl{c}")
            for g_i in range(G):
                g2 = g_i % 2
                for h2 in range(2):
                    p.add("sync", lambda sc, c=c, g_i=g_i, h2=h2, g2=g2: sc.dma_start(
                        out=xstage[g2 * 2 + h2][:, :],
                        in_=xsil_d[g_i * 1024:(g_i + 1) * 1024,
                                   c * CS + h2 * 128: c * CS + (h2 + 1) * 128],
                        transpose=True),
                        reads=[("xsil", g_i * 8 + k) for k in range(8)],
                        writes=[("xstage", g2, h2)], dma=True, tag=f"xs{c}_{g_i}_{h2}")
                p.add("sync", lambda sy, g_i=g_i, g2=g2: sy.dma_start(
                    out=dex_t[g2][:, :], in_=d_exp[:, g_i * 1024:(g_i + 1) * 1024]),
                    writes=[("dex_t", g2)], dma=True, tag=f"dx{c}_{g_i}")
                for h2 in range(2):
                    p.add("tensor", lambda te, c=c, g_i=g_i, h2=h2: te.transpose(
                        bkt[:, 0:128],
                        B_fm[:, g_i * T_OWN + c * CS + h2 * 128: g_i * T_OWN + c * CS + (h2 + 1) * 128],
                        identb[:, :]),
                        reads=[("B_fm", g_i), "identb"], writes=[BKT],
                        tag=f"bt{c}_{g_i}_{h2}")
                    p.add("vector", lambda v, h2=h2, g2=g2: v.tensor_copy(
                        btm[g2][:, h2 * 128:(h2 + 1) * 128], bkt[:, 0:128]),
                        reads=[BKT], writes=[("btm", g2, h2)], tag=f"bc{c}_{g_i}_{h2}")
                for jt in range(2):
                    p.add("tensor", lambda te, c=c, g_i=g_i, jt=jt: te.matmul(
                        bank[6][:, 0:256],
                        B_fm[:, g_i * T_OWN + c * CS + jt * 128: g_i * T_OWN + c * CS + (jt + 1) * 128],
                        C_fm[:, g_i * T_OWN + c * CS: g_i * T_OWN + (c + 1) * CS],
                        start=True, stop=True),
                        reads=[("B_fm", g_i), ("C_fm", g_i)], writes=[BK[6]],
                        tag=f"gt{c}_{g_i}_{jt}")
                    p.add("vector", lambda v, jt=jt, g2=g2: v.tensor_tensor(
                        gmt[g2][:, jt * 256:(jt + 1) * 256], bank[6][:, 0:256],
                        tri_s[:, jt * 256:(jt + 1) * 256], ALU.mult),
                        reads=[BK[6], "tri_s"], writes=[("gmt", g2, jt)],
                        tag=f"ge{c}_{g_i}_{jt}")
                p.add("gpsimd", lambda g, c=c, g_i=g_i: g.dma_start(
                    out=arow[0:1, :].rearrange("p (h t) -> p h t", h=16),
                    in_=acsht_d[g_i * 16:(g_i + 1) * 16,
                                c * CS:(c + 1) * CS].unsqueeze(0)),
                    reads=["acsht_d"], writes=["arow"], dma=True,
                    tag=f"ar{c}_{g_i}")

                def ssdA(hh, c=c, g_i=g_i, g2=g2):
                    h = g_i * 16 + hh
                    rbk = 3 + hh % 4
                    lj = hh % 3
                    p.add("tensor", lambda te, hh=hh, rbk=rbk: te.matmul(
                        bank[rbk][:, 0:256], ones_f[:, :],
                        arow[0:1, hh * 256:(hh + 1) * 256],
                        start=True, stop=True),
                        reads=["arow", "ones_f"], writes=[("bank", rbk)], tag=f"rm{c}_{h}")
                    for jt in range(2):
                        p.add("scalar", lambda sc, c=c, h=h, jt=jt, rbk=rbk, lj=lj: sc.activation(
                            lt1[lj][:, jt * 256:jt * 256 + 256], bank[rbk][:, 0:256],
                            AF.Relu, scale=-1.0,
                            bias=acs_ti[:, (2 * c + jt) * 128 + h:(2 * c + jt) * 128 + h + 1]),
                            reads=[("bank", rbk), ("acs_ti", 2 * c + jt)], writes=[("lt1", lj)],
                            tag=f"lr{c}_{h}_{jt}")
                        p.add("scalar", lambda sc, jt=jt, lj=lj: sc.activation(
                            lt1[lj][:, jt * 256:jt * 256 + 256],
                            lt1[lj][:, jt * 256:jt * 256 + 256], AF.Exp, scale=-1.0),
                            reads=[("lt1", lj)], writes=[("lt1", lj)], tag=f"le{c}_{h}_{jt}")
                        p.add("vector", lambda v, jt=jt, lj=lj, g2=g2: v.tensor_tensor(
                            ltg[lj][:, jt * 256:(jt + 1) * 256],
                            lt1[lj][:, jt * 256:jt * 256 + 256],
                            gmt[g2][:, jt * 256:(jt + 1) * 256], ALU.mult),
                            reads=[("lt1", lj), ("gmt", g2, jt)], writes=[("ltg", lj)],
                            tag=f"lg{c}_{h}_{jt}")
                    for h2 in range(2):
                        p.add("vector", lambda v, h2=h2, hh=hh, c=c, h=h, lj=lj, g2=g2: v.tensor_scalar(
                            xs1[lj][:, h2 * 64:(h2 + 1) * 64],
                            xstage[g2 * 2 + h2][:, hh * 64:(hh + 1) * 64],
                            dt_ti[:, (2 * c + h2) * 128 + h:(2 * c + h2) * 128 + h + 1],
                            0.0, ALU.mult, ALU.add),
                            reads=[("xstage", g2, h2), ("dt_ti", 2 * c + h2)],
                            writes=[("xs1", lj)], tag=f"x1_{c}_{h}_{h2}")
                        p.add("vector", lambda v, h2=h2, hh=hh, c=c, h=h, lj=lj, g2=g2: v.tensor_scalar(
                            xs2[lj][:, h2 * 64:(h2 + 1) * 64],
                            xstage[g2 * 2 + h2][:, hh * 64:(hh + 1) * 64],
                            dtds_ti[:, (2 * c + h2) * 128 + h:(2 * c + h2) * 128 + h + 1],
                            0.0, ALU.mult, ALU.add),
                            reads=[("xstage", g2, h2), ("dtds_ti", 2 * c + h2)],
                            writes=[("xs2", lj)], tag=f"x2_{c}_{h}_{h2}")

                def ssdB(hh, c=c, g_i=g_i, g2=g2):
                    h = g_i * 16 + hh
                    slot = hh % 8
                    lj = hh % 3
                    for it in range(2):
                        bi = it
                        for jt in range(2):
                            p.add("tensor", lambda te, it=it, jt=jt, bi=bi, slot=slot, lj=lj: te.matmul(
                                bank[bi][:, slot * 64:(slot + 1) * 64],
                                ltg[lj][:, jt * 256 + it * 128: jt * 256 + (it + 1) * 128],
                                xs1[lj][:, jt * 64:(jt + 1) * 64],
                                start=(slot == 0 and jt == 0),
                                stop=(slot == 7 and jt == 1)),
                                reads=[("ltg", lj), ("xs1", lj)], writes=[("bank", bi)],
                                tag=f"yd{c}_{h}_{it}_{jt}")
                    for h2 in range(2):
                        p.add("tensor", lambda te, h2=h2, slot=slot, lj=lj, g2=g2: te.matmul(
                            bank[2][:, slot * 64:(slot + 1) * 64],
                            btm[g2][:, h2 * 128:(h2 + 1) * 128],
                            xs2[lj][:, h2 * 64:(h2 + 1) * 64],
                            start=(slot == 0 and h2 == 0),
                            stop=(slot == 7 and h2 == 1)),
                            reads=[("btm", g2, h2), ("xs2", lj)], writes=[("bank", 2)],
                            tag=f"sm{c}_{h}_{h2}")
                    if slot == 7:
                        hb = hh // 8
                        h0 = g_i * 16 + hb * 8
                        p.add("vector", lambda v, h0=h0: v.tensor_copy(
                            slocal[:, h0 * 64:(h0 + 8) * 64], bank[2][:, :]),
                            reads=[("bank", 2)], writes=[("slocal", h0)],
                            tag=f"se{c}_{h0}")
                        for it in range(2):
                            bi2 = it
                            p.add("vector", lambda v, it=it, hb=hb, g2=g2: v.tensor_tensor(
                                dxl[:, 0:512],
                                xstage[g2 * 2 + it][:, hb * 512:(hb + 1) * 512],
                                dex_t[g2][:, hb * 512:(hb + 1) * 512], ALU.mult),
                                reads=[("xstage", g2, it), ("dex_t", g2)], writes=["dxl"],
                                tag=f"dx{c}_{h0}_{it}")
                            p.add("vector", lambda v, bi2=bi2: v.tensor_tensor(
                                piece[:, 0:512], bank[bi2][:, :], dxl[:, 0:512],
                                ALU.add),
                                reads=[("bank", bi2), "dxl"], writes=["piece"],
                                tag=f"yp{c}_{h0}_{it}")
                            p.add("gpsimd", lambda g, c=c, it=it, h0=h0: g.dma_start(
                                out=y_part[c * CS + it * 128: c * CS + (it + 1) * 128,
                                           h0 * 64:(h0 + 8) * 64],
                                in_=piece[:, 0:512]),
                                reads=["piece"], writes=[("yp_o", c, it, h0)],
                                dma=True, tag=f"ys{c}_{h0}_{it}")

                for hh in range(16 + 2):
                    if hh < 16:
                        ssdA(hh)
                    if hh >= 2:
                        ssdB(hh - 2)
            ex_view = expt_bc[:, c * 128:(c + 1) * 128].unsqueeze(-1).broadcast_to([128, 128, 64])
            sr_view = srun_s[:, :].rearrange("p (h d) -> p h d", d=64)
            p.add("vector", lambda v, ex_view=ex_view, sr_view=sr_view: v.tensor_tensor(
                sr_view, sr_view, ex_view, ALU.mult),
                reads=["srun_s", ("expt_bc", c)], writes=["srun_s"], tag=f"su{c}")
            p.add("vector", lambda v: v.tensor_tensor(
                srun_s[:, :], srun_s[:, :], slocal[:, :], ALU.add),
                reads=["srun_s", "slocal"] + [("slocal", g2 * 16) for g2 in range(G)] + [("slocal", g2 * 16 + 8) for g2 in range(G)],
                writes=["srun_s"], tag=f"sa{c}")
        p.add("gpsimd", lambda g: g.dma_start(out=s_out[:, :], in_=srun_s[:, :]),
              reads=["srun_s"], writes=["s_o"], dma=True, tag="st_send")

        p.emit(nc)
    return nc




def build_L3():
    nc = bass.Bass("TRN2", target_bir_lowering=False, debug=False)
    p = Prog()

    x2_in = nc.dram_tensor("x2_in", [T_OWN, DM], F32, kind="ExternalInput").ap()
    ypart = nc.dram_tensor("ypart", [T_OWN, DIN], F32, kind="ExternalInput").ap()
    gate = nc.dram_tensor("gate", [T_OWN, DIN], F32, kind="ExternalInput").ap()
    cfm_in = nc.dram_tensor("cfm_in", [G * NST, T_OWN], BF16, kind="ExternalInput").ap()
    acs_in = nc.dram_tensor("acs_in", [T_OWN, NH], F32, kind="ExternalInput").ap()
    sall = nc.dram_tensor("sall", [NH, 8 * NST * HD], F32, kind="ExternalInput").ap()
    sloc_h = nc.dram_tensor("sloc_h", [NH, 3 * NST * HD], F32, kind="ExternalInput").ap()
    townT = nc.dram_tensor("townT", [NH, NCH], F32, kind="ExternalInput").ap()
    tallT = nc.dram_tensor("tallT", [NH, 8 * NCH], F32, kind="ExternalInput").ap()
    m_lt = nc.dram_tensor("m_lt", [128, 8], F32, kind="ExternalInput").ap()
    mexp = nc.dram_tensor("mexp", [128, 64], F32, kind="ExternalInput").ap()
    wffn = nc.dram_tensor("wffn", [DM, DM], BF16, kind="ExternalInput").ap()
    fbias = nc.dram_tensor("fbias", [1, DM], F32, kind="ExternalInput").ap()
    wop = nc.dram_tensor("wop", [DIN, DM], BF16, kind="ExternalInput").ap()
    norw_b = nc.dram_tensor("norw_b", [128, DIN], F32, kind="ExternalInput").ap()

    out = nc.dram_tensor("out", [T_OWN, DM], F32, kind="ExternalOutput").ap()

    sinh_d = nc.dram_tensor("sinh_d", [NCH * NH, NST * HD], F32).ap()
    ynorm_d = nc.dram_tensor("ynorm_d", [T_OWN, DIN], BF16).ap()
    x3_d = nc.dram_tensor("x3_d", [T_OWN, DM], F32).ap()
    xn4_d = nc.dram_tensor("xn4_d", [T_OWN, DM], BF16).ap()

    with contextlib.ExitStack() as stack:
        ctx = Ctx(nc, p, stack)
        bank = [ctx.psum(f"bank{i}", [128, 512], F32) for i in range(7)]
        bkt = ctx.psum("bankT", [128, 1024], BF16)
        BK = [("bank", i) for i in range(7)]
        ones_b = ctx.sbuf("ones_b", [1, 512], BF16)
        p.add("vector", lambda v: v.memset(ones_b[:, :], 1.0), writes=["ones_b"], tag="onesb")
        ev_f = [ctx.sbuf(f"ev_f{j}", [128, 512], F32) for j in range(4)]

        with ctx.scope():
            tt_s = ctx.sbuf("tt_s", [128, 8 * NCH], F32)
            to_s = ctx.sbuf("to_s", [128, NCH], F32)
            mlt_s = ctx.sbuf("mlt_s", [128, 8], F32)
            mex_s = ctx.sbuf("mex_s", [128, 64], F32)
            ttot = ctx.sbuf("ttot", [128, 8], F32)
            tmp8 = ctx.sbuf("tmp8", [128, 8], F32)
            wl = ctx.sbuf("wl", [128, 8], F32)
            wfac = ctx.sbuf("wfac", [128, 8], F32)
            pic = ctx.sbuf("pic", [128, NCH], F32)
            cum = ctx.sbuf("cum", [128, NCH], F32)
            zz = ctx.sbuf("zz", [128, NCH], F32)
            sacc = [ctx.sbuf(f"sacc{j}", [128, DIN], F32) for j in range(2)]
            sld = [ctx.sbuf(f"sld{j}", [128, DIN], F32) for j in range(2)]
            p.add("sync", lambda sy: sy.dma_start(out=tt_s[:, :], in_=tallT[:, :]),
                  writes=["tt_s"], dma=True, tag="ld_tt")
            p.add("sync", lambda sy: sy.dma_start(out=to_s[:, :], in_=townT[:, :]),
                  writes=["to_s"], dma=True, tag="ld_to")
            p.add("sync", lambda sy: sy.dma_start(out=mlt_s[:, :], in_=m_lt[:, :]),
                  writes=["mlt_s"], dma=True, tag="ld_mlt")
            p.add("sync", lambda sy: sy.dma_start(out=mex_s[:, :], in_=mexp[:, :]),
                  writes=["mex_s"], dma=True, tag="ld_mex")
            ttv = tt_s[:, :].rearrange("p (j c) -> p j c", c=NCH)
            p.add("vector", lambda v: v.tensor_tensor(
                ttot[:, :], ttv[:, :, 0], ttv[:, :, 1], ALU.add),
                reads=["tt_s"], writes=["ttot"], tag="tt0")
            p.add("vector", lambda v: v.tensor_tensor(
                ttot[:, :], ttot[:, :], ttv[:, :, 2], ALU.add),
                reads=["tt_s", "ttot"], writes=["ttot"], tag="tt1")
            p.add("vector", lambda v: v.tensor_tensor(
                ttot[:, :], ttot[:, :], ttv[:, :, 3], ALU.add),
                reads=["tt_s", "ttot"], writes=["ttot"], tag="tt2")
            for j in range(8):
                p.add("vector", lambda v, j=j: v.tensor_tensor(
                    tmp8[:, :], ttot[:, :], mex_s[:, j * 8:(j + 1) * 8], ALU.mult),
                    reads=["ttot", "mex_s"], writes=["tmp8"], tag=f"wm{j}")
                p.add("vector", lambda v, j=j: v.tensor_reduce(
                    wl[:, j:j + 1], tmp8[:, :], AX.X, ALU.add),
                    reads=["tmp8"], writes=["wl"], tag=f"ws{j}")
            p.add("scalar", lambda sc: sc.activation(wfac[:, :], wl[:, :], AF.Exp),
                  reads=["wl"], writes=["wfac"], tag="we")
            p.add("vector", lambda v: v.tensor_tensor(
                wfac[:, :], wfac[:, :], mlt_s[:, :], ALU.mult),
                reads=["wfac", "mlt_s"], writes=["wfac"], tag="wmk")
            for j in range(8):
                sj = j % 2
                p.add("sync", lambda sy, j=j, sj=sj: sy.dma_start(
                    out=sld[sj][:, :], in_=sall[:, j * DIN:(j + 1) * DIN]),
                    writes=[("sld", sj)], dma=True, tag=f"lds{j}")
                if j == 0:
                    p.add("vector", lambda v, sj=sj: v.tensor_scalar(
                        sacc[0][:, :], sld[sj][:, :], wfac[:, 0:1], 0.0, ALU.mult, ALU.add),
                        reads=[("sld", sj), "wfac"], writes=[("sacc", 0)], tag="sg0")
                else:
                    p.add("vector", lambda v, j=j, sj=sj: v.scalar_tensor_tensor(
                        sacc[j % 2][:, :], sld[sj][:, :], wfac[:, j:j + 1],
                        sacc[(j - 1) % 2][:, :], ALU.mult, ALU.add),
                        reads=[("sld", sj), "wfac", ("sacc", (j - 1) % 2)],
                        writes=[("sacc", j % 2)], tag=f"sg{j}")
            p.add("vector", lambda v: v.memset(zz[:, :], 0.0), writes=["zz"], tag="zz")
            p.add("vector", lambda v: v.tensor_tensor_scan(
                cum[:, :], to_s[:, :], zz[:, :], 0.0, ALU.add, ALU.add),
                reads=["to_s", "zz"], writes=["cum"], tag="cumt")
            p.add("vector", lambda v: v.memset(pic[:, 0:1], 1.0), writes=["pic"], tag="pic0")
            p.add("scalar", lambda sc: sc.activation(
                pic[:, 1:4], cum[:, 0:3], AF.Exp),
                reads=["cum"], writes=["pic"], tag="pice")
            for c in range(NCH):
                sj = c % 2
                p.add("vector", lambda v, c=c, sj=sj: v.tensor_scalar(
                    sld[sj][:, :], sacc[1][:, :], pic[:, c:c + 1], 0.0, ALU.mult, ALU.add),
                    reads=[("sacc", 1), "pic"], writes=[("sld", sj)], tag=f"si{c}")
                if c > 0:
                    p.add("vector", lambda v, c=c: v.tensor_tensor(
                        sld[:, :], sld[:, :],
                        sloc_h[0:0, 0:0] if False else sld[:, :], ALU.add),
                        reads=["sld"], writes=["sld"], tag=f"dummy{c}") if False else None
                    p.add("sync", lambda sy, c=c: sy.dma_start(
                        out=sacc[0][:, :], in_=sloc_h[:, (c - 1) * DIN:c * DIN]),
                        writes=[("sacc", 0)], dma=True, tag=f"ldsl{c}")
                    p.add("vector", lambda v, sj=sj: v.tensor_tensor(
                        sld[sj][:, :], sld[sj][:, :], sacc[0][:, :], ALU.add),
                        reads=[("sld", sj), ("sacc", 0)], writes=[("sld", sj)], tag=f"sia{c}")
                p.add("gpsimd", lambda g, c=c, sj=sj: g.dma_start(
                    out=sinh_d[c * NH:(c + 1) * NH, :], in_=sld[sj][:, :]),
                    reads=[("sld", sj)], writes=[("sinh", c)], dma=True, tag=f"ssi{c}")

        with ctx.scope():
            cfm = ctx.sbuf("cfm", [128, 8 * T_OWN], BF16)
            for gi in range(8):
                p.add("sync", lambda sy, gi=gi: sy.dma_start(
                    out=cfm[:, gi * T_OWN:(gi + 1) * T_OWN],
                    in_=cfm_in[gi * 128:(gi + 1) * 128, :]),
                    writes=[("cfm", gi)], dma=True, tag=f"ldc{gi}")
            eacs = ctx.sbuf("eacs", [128, 8 * 128], F32)
            acst = ctx.sbuf("acst", [128, 8 * 128], F32)
            for tt in range(8):
                p.add("sync", lambda sy, tt=tt: sy.dma_start(
                    out=acst[:, tt * 128:(tt + 1) * 128],
                    in_=acs_in[tt * 128:(tt + 1) * 128, :]),
                    writes=[("acst", tt)], dma=True, tag=f"lda{tt}")
            p.add("scalar", lambda sc: sc.activation(eacs[:, :], acst[:, :], AF.Exp),
                  reads=[("acst", tt) for tt in range(8)], writes=["eacs"], tag="eacs")
            norw_s = ctx.sbuf("norw_s", [128, DIN], F32)
            p.add("sync", lambda sy: sy.dma_start(out=norw_s[:, :], in_=norw_b[:, :]),
                  writes=["norw_s"], dma=True, tag="ld_nw")
            sraw = [ctx.sbuf(f"sraw{j}", [128, 512], F32) for j in range(2)]
            sbf = [ctx.sbuf(f"sbf{j}", [128, 512], BF16) for j in range(2)]
            ybuf = [ctx.sbuf(f"ybuf{it}", [128, DIN], F32) for it in range(2)]
            gbuf = ctx.sbuf("gbuf", [128, DIN], F32)
            yrow = [ctx.sbuf(f"yrow{j}", [128, 512], F32) for j in range(4)]
            t1 = [ctx.sbuf(f"t1_{j}", [128, 512], F32) for j in range(4)]
            vcol = ctx.sbuf("vcol", [128, 8], F32)
            sqs = ctx.sbuf("sqs", [128, GSZ], F32)
            ynb = ctx.sbuf("ynb", [128, DIN], BF16)
            for c in range(NCH):
                def yoffA(hs, c=c):
                    j = hs % 2
                    p.add("gpsimd", lambda g, c=c, hs=hs, j=j: g.dma_start(
                        out=sraw[j][:, :].rearrange("p (h d) -> p h d", h=8),
                        in_=sinh_d[c * NH + hs * 8:c * NH + (hs + 1) * 8, :].rearrange(
                            "h (n d) -> n h d", n=NST)),
                        reads=[("sinh", c)], writes=[("sraw", j)], dma=True,
                        tag=f"lsi{c}_{hs}")
                    p.add("vector", lambda v, j=j: v.tensor_copy(sbf[j][:, :], sraw[j][:, :]),
                          reads=[("sraw", j)], writes=[("sbf", j)], tag=f"cbf{c}_{hs}")
                    for it in range(2):
                        bi = (hs % 2) * 2 + it
                        for slot in range(8):
                            h = hs * 8 + slot
                            gi = h // 16
                            p.add("tensor", lambda te, c=c, it=it, gi=gi, slot=slot, bi=bi, j=j: te.matmul(
                                bank[bi][:, slot * 64:(slot + 1) * 64],
                                cfm[:, gi * T_OWN + c * CS + it * 128: gi * T_OWN + c * CS + (it + 1) * 128],
                                sbf[j][:, slot * 64:(slot + 1) * 64],
                                start=(slot == 0), stop=(slot == 7)),
                                reads=[("cfm", gi), ("sbf", j)], writes=[BK[bi]],
                                tag=f"yo{c}_{it}_{hs}_{slot}")
                        jr = (hs % 2) * 2 + it
                        p.add("scalar", lambda sc, c=c, it=it, hs=hs, jr=jr: sc.dma_start(
                            out=yrow[jr][:, :],
                            in_=ypart[c * CS + it * 128: c * CS + (it + 1) * 128,
                                      hs * 512:(hs + 1) * 512]),
                            writes=[("yrow", jr)], dma=True, tag=f"lyp{c}_{it}_{hs}")

                def yoffB(hs, c=c):
                    for it in range(2):
                        bi = (hs % 2) * 2 + it
                        jr = (hs % 2) * 2 + it
                        exv = eacs[:, (2 * c + it) * 128 + hs * 8:(2 * c + it) * 128 + (hs + 1) * 8] \
                            .unsqueeze(-1).broadcast_to([128, 8, 64])
                        p.add("vector", lambda v, bi=bi, exv=exv, jr=jr: v.tensor_tensor(
                            t1[jr][:, :].rearrange("p (a b) -> p a b", a=8),
                            bank[bi][:, :].rearrange("p (a b) -> p a b", a=8),
                            exv, ALU.mult),
                            reads=[BK[bi], "eacs"], writes=[("t1", jr)], tag=f"ysc{c}_{it}_{hs}")
                        p.add("vector", lambda v, hs=hs, it=it, jr=jr: v.tensor_tensor(
                            ybuf[it][:, hs * 512:(hs + 1) * 512], t1[jr][:, :],
                            yrow[jr][:, :], ALU.add),
                            reads=[("t1", jr), ("yrow", jr)],
                            writes=[("ybuf", it, hs), ("ybufE", it)],
                            tag=f"yad{c}_{it}_{hs}")

                for hs in range(16 + 1):
                    if hs < 16:
                        yoffA(hs)
                    if hs >= 1:
                        yoffB(hs - 1)
                for it in range(2):
                    YB = [("ybuf", it, hs) for hs in range(16)] + [("ybufE", it)]
                    p.add("gpsimd", lambda g, c=c, it=it: g.dma_start(
                        out=gbuf[:, :],
                        in_=gate[c * CS + it * 128: c * CS + (it + 1) * 128, :]),
                        writes=["gbuf"], dma=True, tag=f"lg{c}_{it}")
                    p.add("vector", lambda v, it=it: v.tensor_tensor(
                        ybuf[it][:, :], ybuf[it][:, :], gbuf[:, :], ALU.mult),
                        reads=YB + ["gbuf"], writes=YB, tag=f"ygm{c}_{it}")
                    for gi in range(8):
                        p.add("scalar", lambda sc, gi=gi, it=it: sc.activation(
                            sqs[:, :], ybuf[it][:, gi * GSZ:(gi + 1) * GSZ], AF.Square,
                            accum_out=vcol[:, gi:gi + 1]),
                            reads=YB, writes=["sqs", "vcol"], tag=f"ysq{c}_{it}_{gi}")
                    p.add("vector", lambda v: v.tensor_scalar(
                        vcol[:, :], vcol[:, :], 1.0 / GSZ, EPS, ALU.mult, ALU.add),
                        reads=["vcol"], writes=["vcol"], tag=f"yv{c}_{it}")
                    p.add("scalar", lambda sc: sc.activation(vcol[:, :], vcol[:, :], AF.Sqrt),
                          reads=["vcol"], writes=["vcol"], tag=f"yvs{c}_{it}")
                    p.add("vector", lambda v: v.reciprocal(vcol[:, :], vcol[:, :]),
                          reads=["vcol"], writes=["vcol"], tag=f"yvr{c}_{it}")
                    for gi in range(8):
                        p.add("vector", lambda v, gi=gi, it=it: v.tensor_scalar(
                            ybuf[it][:, gi * GSZ:(gi + 1) * GSZ],
                            ybuf[it][:, gi * GSZ:(gi + 1) * GSZ],
                            vcol[:, gi:gi + 1], 0.0, ALU.mult, ALU.add),
                            reads=YB + ["vcol"], writes=YB, tag=f"yn{c}_{it}_{gi}")
                    p.add("vector", lambda v, it=it: v.tensor_tensor(
                        ynb[:, :], ybuf[it][:, :], norw_s[:, :], ALU.mult),
                        reads=YB + ["norw_s"], writes=["ynb"], tag=f"ynw{c}_{it}")
                    p.add("gpsimd", lambda g, c=c, it=it: g.dma_start(
                        out=ynorm_d[c * CS + it * 128: c * CS + (it + 1) * 128, :],
                        in_=ynb[:, :]),
                        reads=["ynb"], writes=[("ynd", 2 * c + it)], dma=True,
                        tag=f"syn{c}_{it}")
        YNK = [("ynd", i) for i in range(8)]

        with ctx.scope():
            yfm = ctx.sbuf("yfm", [128, 64 * 512], BF16)
            wcol = [ctx.sbuf(f"wcol{j}", [128, DM], BF16) for j in range(4)]
            opacc = [ctx.sbuf(f"opacc{t}", [128, DM], F32) for t in range(4)]
            x2_t = [ctx.sbuf(f"x2_t{j}", [128, DM], F32) for j in range(1)]
            xo_t = [ctx.sbuf(f"xo_t{j}", [128, DM], F32) for j in range(1)]
            for tg0 in (0, 4):
                for ch in range(64):
                    p.add("sync", lambda sc, ch=ch, tg0=tg0: sc.dma_start(
                        out=yfm[:, ch * 512:(ch + 1) * 512],
                        in_=ynorm_d[tg0 * 128:(tg0 + 4) * 128, ch * 128:(ch + 1) * 128],
                        transpose=True),
                        reads=YNK, writes=[("yfm", ch)], dma=True, tag=f"tyf{tg0}_{ch}")
                for cg in range(16):
                    for cc in range(4):
                        ch = cg * 4 + cc
                        p.add("sync", lambda sy, ch=ch, cc=cc: sy.dma_start(
                            out=wcol[cc][:, :],
                            in_=wop[ch * 128:(ch + 1) * 128, :]),
                            writes=[("wcol", cc)], dma=True, tag=f"two{tg0}_{cg}_{cc}")
                    for bi in range(4):
                        for f5 in range(8):
                            bb = (cg * 32 + bi * 8 + f5) % 7
                            for cc in range(4):
                                ch = cg * 4 + cc
                                p.add("tensor", lambda te, ch=ch, cc=cc, bi=bi, f5=f5, bb=bb: te.matmul(
                                    bank[bb][:, :],
                                    yfm[:, ch * 512 + bi * 128: ch * 512 + (bi + 1) * 128],
                                    wcol[cc][:, f5 * 512:(f5 + 1) * 512],
                                    start=(cc == 0), stop=(cc == 3)),
                                    reads=[("wcol", cc), ("yfm", ch)], writes=[BK[bb]],
                                    tag=f"om{tg0}_{cg}_{bi}_{f5}_{cc}")
                            if cg == 0:
                                p.add("scalar", lambda sc, bi=bi, f5=f5, bb=bb: sc.activation(
                                    opacc[bi][:, f5 * 512:(f5 + 1) * 512],
                                    bank[bb][:, :], AF.Copy),
                                    reads=[BK[bb]], writes=[("opacc", bi)],
                                    tag=f"oe{tg0}_{cg}_{bi}_{f5}")
                            else:
                                p.add("vector", lambda v, bi=bi, f5=f5, bb=bb: v.tensor_tensor(
                                    opacc[bi][:, f5 * 512:(f5 + 1) * 512],
                                    bank[bb][:, :],
                                    opacc[bi][:, f5 * 512:(f5 + 1) * 512], ALU.add),
                                    reads=[BK[bb], ("opacc", bi)], writes=[("opacc", bi)],
                                    tag=f"oa{tg0}_{cg}_{bi}_{f5}")
                for bi in range(4):
                    t = tg0 + bi
                    j = bi % 2
                    p.add("scalar", lambda sc, t=t: sc.dma_start(
                        out=x2_t[0][:, :], in_=x2_in[t * 128:(t + 1) * 128, :]),
                        writes=[("x2_t", 0)], dma=True, tag=f"lx2{tg0}_{t}")
                    p.add("vector", lambda v, bi=bi: v.tensor_tensor(
                        xo_t[0][:, :], opacc[bi][:, :], x2_t[0][:, :], ALU.add),
                        reads=[("opacc", bi), ("x2_t", 0)], writes=[("xo_t", 0)],
                        tag=f"or{tg0}_{bi}")
                    p.add("gpsimd", lambda g, t=t: g.dma_start(
                        out=x3_d[t * 128:(t + 1) * 128, :], in_=xo_t[0][:, :]),
                        reads=[("xo_t", 0)], writes=[("x3d", t)], dma=True,
                        tag=f"sx3{tg0}_{t}")
        X3K = [("x3d", t) for t in range(8)]

        XN4K = [("xn4d", i) for i in range(8)]
        with ctx.scope():
            emit_ln_stream(ctx, x3_d, xn4_d, "xn4d", T_OWN, "ln4", src_keys=X3K)
            xn4fm = ctx.sbuf("xn4fm", [128, 32 * T_OWN], BF16)
            for cc_i in range(32):
                p.add("sync", lambda sc, cc_i=cc_i: sc.dma_start(
                    out=xn4fm[:, cc_i * T_OWN:(cc_i + 1) * T_OWN],
                    in_=xn4_d[:, cc_i * 128:(cc_i + 1) * 128], transpose=True),
                    reads=XN4K, writes=[("xn4fm", cc_i)], dma=True, tag=f"t4_{cc_i}")
            fb_b = ctx.sbuf("fb_b", [1, DM], BF16)
            p.add("gpsimd", lambda g: g.dma_start(out=fb_b[:, :], in_=fbias[:, :]),
                  writes=["fb_b"], dma=True, tag="ld_fb")
            wf_t = [ctx.sbuf(f"wf_t{j}", [128, 512], BF16) for j in range(8)]
            xr_t = [ctx.sbuf(f"xr_t{j}", [128, 512], F32) for j in range(4)]
            for tg in (range(0, 3), range(3, 6), range(6, 8)):
                for og in range(8):
                    for cc_i in range(32):
                        j = cc_i % 8
                        p.add("sync", lambda sy, og=og, cc_i=cc_i, j=j: sy.dma_start(
                            out=wf_t[j][:, :],
                            in_=wffn[cc_i * 128:(cc_i + 1) * 128, og * 512:(og + 1) * 512]),
                            writes=[("f2wf", j)], dma=True, tag=f"f2w{tg[0]}_{og}_{cc_i}")
                        for bi, qt in enumerate(tg):
                            bb = (og % 2) * 3 + bi
                            p.add("tensor", lambda te, cc_i=cc_i, j=j, bb=bb, qt=qt: te.matmul(
                                bank[bb][:, :],
                                xn4fm[:, cc_i * T_OWN + qt * 128: cc_i * T_OWN + (qt + 1) * 128],
                                wf_t[j][:, :], start=(cc_i == 0), stop=False),
                                reads=[("f2wf", j), ("xn4fm", cc_i)], writes=[BK[bb]],
                                tag=f"f2m{og}_{qt}_{cc_i}")
                    for bi, qt in enumerate(tg):
                        bb = (og % 2) * 3 + bi
                        p.add("tensor", lambda te, bb=bb, og=og: te.matmul(
                            bank[bb][:, :], ones_b[:, 0:128],
                            fb_b[:, og * 512:(og + 1) * 512], start=False, stop=True),
                            reads=["ones_b", "fb_b"], writes=[BK[bb]], tag=f"f2b{og}_{qt}")
                        j = bi % 2
                        p.add("scalar", lambda sc, qt=qt, og=og, j=j: sc.dma_start(
                            out=xr_t[j][:, :],
                            in_=x3_d[qt * 128:(qt + 1) * 128, og * 512:(og + 1) * 512]),
                            reads=X3K, writes=[("f2x", j)], dma=True, tag=f"f2l{og}_{qt}")
                        p.add("vector", lambda v, bb=bb, j=j: v.tensor_tensor(
                            ev_f[j][:, :], bank[bb][:, :], xr_t[j][:, :], ALU.add),
                            reads=[BK[bb], ("f2x", j)], writes=[("ev_f", j)],
                            tag=f"f2r{og}_{qt}")
                        p.add("gpsimd", lambda g, qt=qt, og=og, j=j: g.dma_start(
                            out=out[qt * 128:(qt + 1) * 128, og * 512:(og + 1) * 512],
                            in_=ev_f[j][:, :]),
                            reads=[("ev_f", j)], writes=[("out", qt, og)], dma=True,
                            tag=f"f2s{og}_{qt}")
        p.emit(nc)
    return nc




def kernel(**inputs):
    x = np.asarray(inputs["L0_attn_a_x"])
    Wqkv = np.asarray(inputs["L0_attn_a_Wqkv"])
    bqkv = np.asarray(inputs["L0_attn_a_bqkv"])
    Wo = np.asarray(inputs["L0_attn_a_Wo"])
    bo = np.asarray(inputs["L0_attn_a_bo"])
    inw = np.asarray(inputs["L0_attn_b_in_proj_w"])
    convw = np.asarray(inputs["L0_attn_b_conv_w"])
    convb = np.asarray(inputs["L0_attn_b_conv_b"])
    dtb = np.asarray(inputs["L0_attn_b_dt_bias"])
    alog = np.asarray(inputs["L0_attn_b_A_log"])
    Dvec = np.asarray(inputs["L0_attn_b_D"])
    normw = np.asarray(inputs["L0_attn_b_norm_w"])
    outw = np.asarray(inputs["L0_attn_b_out_proj_w"])
    q3 = np.asarray(inputs["L0_ffn_weight_q3"])
    wnorm = np.asarray(inputs["L0_ffn_weight_norm"])
    fbias = np.asarray(inputs["L0_ffn_bias"])

    x2d = np.ascontiguousarray(x[0])

    nc1 = build_L1()
    in_maps = []
    for i in range(NCORES):
        sl = slice(i * T_OWN, (i + 1) * T_OWN)
        in_maps.append({
            "x0": np.ascontiguousarray(x2d[sl]),
            "wqkv": Wqkv,
            "bqkv": np.ascontiguousarray(bqkv.reshape(1, -1)),
            "q3": np.ascontiguousarray(
                q3.view(np.uint8).reshape(512 * 128 * NCORES, 12)[
                    i * 512 * 128:(i + 1) * 512 * 128]),
            "wnorm": np.ascontiguousarray(
                wnorm.reshape(-1, 1)[i * 512 * 128:(i + 1) * 512 * 128]),
            "inproj_f32": np.ascontiguousarray(
                inw[i * (PROJ // 8):(i + 1) * (PROJ // 8)]),
            "outproj_f32": np.ascontiguousarray(
                outw[i * (DM // 8):(i + 1) * (DM // 8)]),
            "id_bf": np.eye(128).astype(BF),
        })
    res1 = run_bass_kernel_spmd(nc1, in_maps, core_ids=list(range(NCORES)))
    r1 = res1.results
    Q_fm = np.concatenate([r1[i]["q_out"] for i in range(NCORES)], axis=1)
    K_fm = np.concatenate([r1[i]["k_out"] for i in range(NCORES)], axis=1)
    V_tm = np.concatenate([r1[i]["v_out"] for i in range(NCORES)], axis=0)
    w_ffn = np.concatenate([r1[i]["wffnt_out"] for i in range(NCORES)], axis=1)
    inproj_b = np.concatenate([r1[i]["ipt_out"] for i in range(NCORES)], axis=1)
    outproj_b = np.concatenate([r1[i]["opt_out"] for i in range(NCORES)], axis=1)

    nc2 = build_L2()
    trim0 = np.triu(np.ones((128, 256), np.float32))
    trim1 = np.triu(np.ones((128, 256), np.float32), k=128)
    identb_np = np.eye(128).astype(BF)
    identf_np = np.eye(128).astype(np.float32)
    convw_rr = np.ascontiguousarray(
        convw[:, 0, :].reshape(80, 128, 4).transpose(1, 0, 2).reshape(128, 320))
    convb_rr = np.ascontiguousarray(convb.reshape(80, 128).T)
    dexp_np = np.ascontiguousarray(
        np.broadcast_to(np.repeat(Dvec, HD)[None, :], (128, DIN))).astype(np.float32)
    in_maps2 = []
    for i in range(NCORES):
        s = i * T_OWN
        x0ext = np.zeros((T_EXT, DM), np.float32)
        if i > 0:
            x0ext[0:HALO] = x2d[s - HALO:s]
        x0ext[HALO:] = x2d[s:s + T_OWN]
        qx = np.zeros((DH, T_EXT), BF)
        if i > 0:
            qx[:, 0:HALO] = Q_fm[:, s - HALO:s]
        qx[:, HALO:] = Q_fm[:, s:s + T_OWN]
        hmask = np.ones((128, 4), np.float32)
        if i == 0:
            hmask[:, 0:3] = 0.0
        in_maps2.append({
            "x0e": x0ext,
            "q_ext": np.ascontiguousarray(qx),
            "k_fm": K_fm, "v_tm": V_tm,
            "wo": Wo, "bo": np.ascontiguousarray(bo.reshape(1, -1)),
            "wffn": w_ffn, "fbias": np.ascontiguousarray(fbias.reshape(1, -1)),
            "ipbf": inproj_b,
            "convw_r": convw_rr, "convb_r": convb_rr,
            "dtb_col": np.ascontiguousarray(dtb.reshape(128, 1)),
            "alog_col": np.ascontiguousarray(alog.reshape(128, 1)),
            "d_exp": dexp_np,
            "halo_m": hmask,
            "tri0": trim0, "tri1": trim1,
            "id_bf": identb_np, "id_f": identf_np,
        })
    res2 = run_bass_kernel_spmd(nc2, in_maps2, core_ids=list(range(NCORES)))
    r2 = res2.results

    nc3 = build_L3()
    s_all_h = np.stack([r2[j]["s_out"].reshape(NST, NH, HD).transpose(1, 0, 2).reshape(NH, NST * HD)
                        for j in range(NCORES)], axis=1).reshape(NH, NCORES * NST * HD)
    t_all = np.concatenate([r2[j]["t_out"] for j in range(NCORES)], axis=1)
    norw_np = np.ascontiguousarray(
        np.broadcast_to(normw.reshape(1, DIN), (128, DIN))).astype(np.float32)
    in_maps3 = []
    for i in range(NCORES):
        slh = r2[i]["sloc_out"].reshape(3, NST, NH, HD).transpose(2, 0, 1, 3).reshape(NH, 3 * NST * HD)
        mlt = np.zeros((128, 8), np.float32)
        mlt[:, :i] = 1.0
        mex = np.zeros((8, 8), np.float32)
        for j in range(i):
            for k2 in range(j + 1, i):
                mex[j, k2] = 1.0
        in_maps3.append({
            "x2_in": np.ascontiguousarray(r2[i]["x2_out"][HALO:HALO + T_OWN]),
            "ypart": r2[i]["y_part"],
            "gate": r2[i]["gate_out"],
            "cfm_in": r2[i]["c_out"],
            "acs_in": r2[i]["acs_out"],
            "sall": np.ascontiguousarray(s_all_h),
            "sloc_h": np.ascontiguousarray(slh),
            "townT": np.ascontiguousarray(r2[i]["t_out"]),
            "tallT": np.ascontiguousarray(t_all),
            "m_lt": mlt,
            "mexp": np.ascontiguousarray(
                np.broadcast_to(mex.reshape(1, 64), (128, 64))).astype(np.float32),
            "wffn": w_ffn,
            "fbias": np.ascontiguousarray(fbias.reshape(1, -1)),
            "wop": outproj_b,
            "norw_b": norw_np,
        })
    res3 = run_bass_kernel_spmd(nc3, in_maps3, core_ids=list(range(NCORES)))
    r3 = res3.results
    out = np.concatenate([r3[i]["out"] for i in range(NCORES)], axis=0)
    return out.reshape(1, 8192, 4096)


_build_L1 = build_L1
